# version 21
# baseline (speedup 1.0000x reference)
import numpy as np
from contextlib import ExitStack

import concourse.bass as bass
import concourse.mybir as mybir
from concourse.bass_utils import run_bass_kernel_spmd
from concourse.alu_op_type import AluOpType as OP
from concourse.bass import IndirectOffsetOnAxis

F32 = mybir.dt.float32
BF16 = mybir.dt.bfloat16
U32 = mybir.dt.uint32
AF = mybir.ActivationFunctionType
AX = mybir.AxisListType

P = 128
D = 2048
SEQ = 2048
TB = 16
TBM = 8
DC = 16
H = 1024
UB = 8
NE = 8
NEXP = 9
EPS = 1e-5
INV_D = 1.0 / D

CAP = 384
NCH = CAP // P
W1WIN = 4
W2WIN = 4


class Sem:
    def __init__(self, h):
        self.h = h
        self.n = 0

    def inc(self, instr, k=1):
        instr.then_inc(self.h, k)
        self.n += k
        return self.n


def flat2(ap):
    return ap.rearrange("p a b -> p (a b)")


def build(want_taps=(), stage="full"):
    nc = bass.Bass("TRN2", target_bir_lowering=False, debug=False)

    x = nc.dram_tensor("x", [SEQ, D], F32, kind="ExternalInput").ap()
    gwT = nc.dram_tensor("gwT", [D, NE], F32, kind="ExternalInput").ap()
    w1t = nc.dram_tensor("w1t", [NE, D, H], F32, kind="ExternalInput").ap()
    w2t = nc.dram_tensor("w2t", [NE, H, D], F32, kind="ExternalInput").ap()
    w3t = nc.dram_tensor("w3t", [NE, D, H], F32, kind="ExternalInput").ap()
    ws1t = nc.dram_tensor("ws1t", [D, H], F32, kind="ExternalInput").ap()
    ws2t = nc.dram_tensor("ws2t", [H, D], F32, kind="ExternalInput").ap()
    ws3t = nc.dram_tensor("ws3t", [D, H], F32, kind="ExternalInput").ap()
    out = nc.dram_tensor("out", [TBM * P + CAP + P, D], F32, kind="ExternalOutput").ap()
    plist = nc.dram_tensor("plist", [NE, CAP + TBM * P, 2], F32, kind="Internal").ap()
    xfd = nc.dram_tensor("xfd", [TBM * P, D], BF16, kind="Internal").ap()

    x_t = x.rearrange("(tb p) d -> p tb d", p=P)
    out_t = out.rearrange("(tb p) d -> p tb d", p=P)

    with ExitStack() as top:
        e = top.enter_context
        acc = e(nc.sbuf_tensor("acc", [P, TBM, D], F32))
        identt = e(nc.sbuf_tensor("identt", [P, P], F32))
        iota8 = e(nc.sbuf_tensor("iota8", [P, NE], F32))
        gT = e(nc.sbuf_tensor("gT", [P, DC, NE], BF16))
        gws = e(nc.sbuf_tensor("gws", [P, DC, NE], F32))
        st_sum = e(nc.sbuf_tensor("st_sum", [P, TB], F32))
        st_ssq = e(nc.sbuf_tensor("st_ssq", [P, TB], F32))
        st_var = e(nc.sbuf_tensor("st_var", [P, TB], F32))
        st_rstd = e(nc.sbuf_tensor("st_rstd", [P, TB], F32))
        st_nmr = e(nc.sbuf_tensor("st_nmr", [P, TB], F32))
        st_tmp = e(nc.sbuf_tensor("st_tmp", [P, TB], F32))
        sqscr = e(nc.sbuf_tensor("sqscr", [P, D], F32))
        sm_mxn = e(nc.sbuf_tensor("sm_mxn", [P, 2], F32))
        sm_rs = e(nc.sbuf_tensor("sm_rs", [P, 2], F32))
        sm_lnr = e(nc.sbuf_tensor("sm_lnr", [P, 2], F32))
        sm_b2 = e(nc.sbuf_tensor("sm_b2", [P, 2], F32))

        pA = e(nc.psum_tensor("pA", [P, 512 * 4], F32))
        pB = e(nc.psum_tensor("pB", [P, 512 * 4], F32))
        pAR = [pA, pB]

        LDa = Sem(e(nc.semaphore("LDa")))
        LDx = [Sem(e(nc.semaphore(f"LDx{i}"))) for i in range(2)]
        LDg = Sem(e(nc.semaphore("LDg")))
        LDw1 = [Sem(e(nc.semaphore(f"LDw1_{i}"))) for i in range(2)]
        LDw3 = [Sem(e(nc.semaphore(f"LDw3_{i}"))) for i in range(2)]
        LDw2 = [Sem(e(nc.semaphore(f"LDw2_{i}"))) for i in range(2)]
        LDo = Sem(e(nc.semaphore("LDo")))
        LDf = [Sem(e(nc.semaphore(f"LDf{i}"))) for i in range(2)]
        GDc = [Sem(e(nc.semaphore(f"GDc{i}"))) for i in range(2)]
        LDl = [Sem(e(nc.semaphore(f"LDl{i}"))) for i in range(4)]
        LDxg = [Sem(e(nc.semaphore(f"LDxg{i}"))) for i in range(NCH)]
        GDr = Sem(e(nc.semaphore("GDr")))
        LDz = Sem(e(nc.semaphore("LDz")))
        GP = Sem(e(nc.semaphore("GP")))
        TP = Sem(e(nc.semaphore("TP")))
        PE = Sem(e(nc.semaphore("PE")))
        VE = Sem(e(nc.semaphore("VE")))
        SE = Sem(e(nc.semaphore("SE")))

        block = e(nc.Block())

        def tap(name, ap, waits=()):
            if name not in want_taps:
                return
            t = nc.dram_tensor("tap_" + name, list(ap.shape), ap.dtype,
                               kind="ExternalOutput").ap()

            @block.gpsimd
            def _(g, ap=ap, t=t, waits=tuple(waits)):
                for s, v in waits:
                    g.wait_ge(s.h, v)
                LDo.inc(g.dma_start(out=t, in_=ap), 16)

        @block.gpsimd
        def _(g):
            GP.inc(g.iota(iota8[:], pattern=[[1, NE]], base=0, channel_multiplier=0,
                          allow_small_or_imprecise_dtypes=True), 1)
            GP.inc(g.iota(identt[:], pattern=[[0, P]], base=0, channel_multiplier=1,
                          allow_small_or_imprecise_dtypes=True), 1)
            GP.inc(g.iota(sqscr[:, 0:P], pattern=[[1, P]], base=0, channel_multiplier=0,
                          allow_small_or_imprecise_dtypes=True), 1)
            LDg.inc(g.dma_start(out=gws[:], in_=gwT.rearrange("(m p) e -> p m e", p=P)), 16)
            LDa.inc(g.dma_start(out=acc[:], in_=x_t[:, 0:TBM, :]), 16)
        gp_consts = GP.n
        ld_gws = (LDg, LDg.n)
        ld_acc = (LDa, LDa.n)

        @block.vector
        def _(v):
            v.wait_ge(GP.h, gp_consts)
            v.wait_ge(ld_gws[0].h, ld_gws[1])
            VE.inc(v.tensor_tensor(identt[:], identt[:], sqscr[:, 0:P], OP.is_equal))
            VE.inc(v.tensor_copy(gT[:], gws[:]))
        ve_consts = VE.n

        st_m2 = e(nc.sbuf_tensor("st_m2", [P, TB], F32))

        def emit_ln_group(items, ld_waits=None, scalar_pre=None, vector_pre=None):
            cols = [c for c, _, _ in items]
            lo, hi = min(cols), max(cols) + 1
            csl = slice(lo, hi)

            @block.vector
            def _(v):
                if vector_pre is not None:
                    v.wait_ge(vector_pre[0].h, vector_pre[1])
                for i, (c, src_ap, _) in enumerate(items):
                    if ld_waits is not None and ld_waits[i] is not None:
                        v.wait_ge(ld_waits[i][0].h, ld_waits[i][1])
                    VE.inc(v.tensor_reduce(st_sum[:, c:c + 1], src_ap, AX.X, OP.add,
                                           negate=True))
            v_red = VE.n

            @block.scalar
            def _(s):
                if scalar_pre is not None:
                    s.wait_ge(scalar_pre[0].h, scalar_pre[1])
                for i, (c, src_ap, _) in enumerate(items):
                    if ld_waits is not None and ld_waits[i] is not None:
                        s.wait_ge(ld_waits[i][0].h, ld_waits[i][1])
                    if i > 0:
                        s.wait_ge(SE.h, SE.n)
                    SE.inc(s.activation(out=sqscr[:], in_=src_ap, func=AF.Square,
                                        accum_out=st_ssq[:, c:c + 1]))
            s_sq = SE.n

            @block.vector
            def _(v):
                v.wait_ge(VE.h, v_red)
                v.wait_ge(SE.h, s_sq)
                VE.inc(v.tensor_scalar_mul(st_sum[:, csl], st_sum[:, csl], INV_D))
                VE.inc(v.tensor_scalar_mul(st_tmp[:, csl], st_ssq[:, csl], INV_D))
            v_d1 = VE.n

            @block.vector
            def _(v):
                v.wait_ge(VE.h, v_d1)
                VE.inc(v.tensor_tensor(st_m2[:, csl], st_sum[:, csl], st_sum[:, csl],
                                       OP.mult))
            v_d2 = VE.n

            @block.vector
            def _(v):
                v.wait_ge(VE.h, v_d2)
                VE.inc(v.tensor_tensor(st_var[:, csl], st_tmp[:, csl], st_m2[:, csl],
                                       OP.subtract))
            v_d3 = VE.n

            @block.vector
            def _(v):
                v.wait_ge(VE.h, v_d3)
                VE.inc(v.tensor_scalar_add(st_var[:, csl], st_var[:, csl], EPS))
            v_d4 = VE.n

            @block.scalar
            def _(s):
                s.wait_ge(VE.h, v_d4)
                SE.inc(s.activation(out=st_tmp[:, csl], in_=st_var[:, csl], func=AF.Sqrt))
            s_std = SE.n

            @block.vector
            def _(v):
                v.wait_ge(SE.h, s_std)
                VE.inc(v.reciprocal(st_rstd[:, csl], st_tmp[:, csl]))
            v_rec = VE.n

            @block.vector
            def _(v):
                v.wait_ge(VE.h, v_rec)
                VE.inc(v.tensor_tensor(st_nmr[:, csl], st_sum[:, csl], st_rstd[:, csl],
                                       OP.mult))
            v_nmr = VE.n

            outs = []
            for c, src_ap, dst_ap in items:
                @block.scalar
                def _(s, c=c, src_ap=src_ap, dst_ap=dst_ap):
                    s.wait_ge(VE.h, v_nmr)
                    SE.inc(s.activation(out=dst_ap, in_=src_ap, func=AF.Identity,
                                        bias=st_nmr[:, c:c + 1],
                                        scale=st_rstd[:, c:c + 1]))
                outs.append(SE.n)
            return outs

        def emit_ln(src_ap, dst_ap, col, ld_wait=None, scalar_pre=None, vector_pre=None):
            return emit_ln_group([(col, src_ap, dst_ap)], ld_waits=[ld_wait],
                                 scalar_pre=scalar_pre, vector_pre=vector_pre)[0]

        arena_free = [(None, 0), (None, 0)]
        with ExitStack() as attn_scope:
            ea = attn_scope.enter_context
            Xn = ea(nc.sbuf_tensor("Xn", [P, TB, D], BF16))
            XnT = ea(nc.sbuf_tensor("XnT", [P, DC, TBM, P], BF16))
            xring = [ea(nc.sbuf_tensor(f"xring{i}", [P, D], F32)) for i in range(2)]
            Asb = [ea(nc.sbuf_tensor("Asb0", [P, D], BF16))]
            ATsl = [ea(nc.sbuf_tensor("ATsl0", [P, DC, 4, P], BF16))]

            s_norm = emit_ln_group(
                [(tb, acc[:, tb, :], Xn[:, tb, :]) for tb in range(TBM)],
                ld_waits=[ld_acc] * TBM,
                scalar_pre=(VE, ve_consts), vector_pre=None)
            ring_free = [(None, 0), (None, 0)]
            for tb in range(TBM, TB):
                slot = tb % 2
                @block.gpsimd
                def _(g, tb=tb, slot=slot, gate=tuple(ring_free[slot])):
                    sm, val = gate
                    if sm is not None:
                        g.wait_ge(sm.h, val)
                    LDx[slot].inc(g.dma_start(out=xring[slot][:], in_=x_t[:, tb, :]), 16)
                sn = emit_ln(xring[slot][:], Xn[:, tb, :], tb,
                             ld_wait=(LDx[slot], LDx[slot].n))
                s_norm.append(sn)
                ring_free[slot] = (SE, sn)
            s_ln1_all = SE.n
            tap("ln1", Xn[:], waits=[(SE, s_ln1_all)])

            if stage == "ln1":
                @block.gpsimd
                def _(g):
                    g.wait_ge(LDo.h, LDo.n)
                return nc

            for tb in range(TBM):
                @block.sync
                def _(sy, tb=tb):
                    sy.wait_ge(SE.h, s_norm[tb])
                    TP.inc(sy.dma_start(out=XnT[:, :, tb, :], in_=Xn[:, tb, :],
                                        transpose=True), 16)
                    sy.wait_ge(TP.h, TP.n)
            tp_xnt = TP.n
            tap("xnt", XnT[:], waits=[(TP, tp_xnt)])

            if stage == "xnt":
                @block.gpsimd
                def _(g):
                    g.wait_ge(LDo.h, LDo.n)
                return nc

            accw = {}
            asb_free = [(None, 0)]
            atsl_free = [(None, 0)]
            tp_at = {}
            for g_r in range(4):
                for ibl in range(4):
                    gi = 4 * g_r + ibl
                    par = gi % 2
                    ps = pAR[par]

                    @block.tensor
                    def _(t, gi=gi, ps=ps, gate=tuple(arena_free[par])):
                        if gi == 2:
                            t.wait_ge(SE.h, s_ln1_all)
                        sm, val = gate
                        if sm is not None:
                            t.wait_ge(sm.h, val)
                        for ct in range(TB):
                            if gi < 2:
                                t.wait_ge(SE.h, s_norm[ct])
                            for jr in range(4):
                                ins = t.matmul(ps[:, jr * 512:(jr + 1) * 512],
                                               Xn[:, ct, gi * P:(gi + 1) * P],
                                               Xn[:, ct, jr * 512:(jr + 1) * 512],
                                               start=(ct == 0), stop=(ct == TB - 1),
                                               skip_group_check=True)
                        PE.inc(ins)
                    pe_gram = PE.n

                    if stage == "gram0" and gi == 0:
                        @block.vector
                        def _(v, ps=ps, pe_gram=pe_gram):
                            v.wait_ge(PE.h, pe_gram)
                            VE.inc(v.tensor_copy(sqscr[:], ps[:]))
                        tap("G0", sqscr[:], waits=[(VE, VE.n)])

                        @block.gpsimd
                        def _(g):
                            g.wait_ge(LDo.h, LDo.n)
                        return nc

                    mxn = sm_mxn[:, par:par + 1]
                    rs = sm_rs[:, par:par + 1]
                    lnr = sm_lnr[:, par:par + 1]
                    b2 = sm_b2[:, par:par + 1]

                    @block.vector
                    def _(v, ps=ps, pe_gram=pe_gram, mxn=mxn):
                        v.wait_ge(PE.h, pe_gram)
                        VE.inc(v.tensor_reduce(mxn, ps[:], AX.X, OP.max, negate=True))
                    v_mxn = VE.n

                    @block.scalar
                    def _(s, gi=gi, ps=ps, v_mxn=v_mxn, mxn=mxn, rs=rs,
                          gate=tuple(asb_free[0])):
                        s.wait_ge(VE.h, v_mxn)
                        sm, val = gate
                        if sm is not None:
                            s.wait_ge(sm.h, val)
                        SE.inc(s.activation(out=Asb[0][:], in_=ps[:], func=AF.Exp,
                                            bias=mxn, accum_out=rs))
                    s_e1 = SE.n

                    @block.scalar
                    def _(s, s_e1=s_e1, rs=rs, lnr=lnr):
                        s.wait_ge(SE.h, s_e1)
                        SE.inc(s.activation(out=lnr, in_=rs, func=AF.Ln))
                    s_ln = SE.n

                    @block.vector
                    def _(v, s_ln=s_ln, mxn=mxn, lnr=lnr, b2=b2):
                        v.wait_ge(SE.h, s_ln)
                        VE.inc(v.tensor_tensor(b2, mxn, lnr, OP.subtract))
                    v_b2 = VE.n

                    @block.scalar
                    def _(s, gi=gi, ps=ps, v_b2=v_b2, b2=b2):
                        s.wait_ge(VE.h, v_b2)
                        SE.inc(s.activation(out=Asb[0][:], in_=ps[:], func=AF.Exp,
                                            bias=b2))
                    s_e2 = (SE, SE.n)
                    arena_free[par] = s_e2

                    if stage == "gram1" and gi == 0:
                        tap("A0", Asb[0][:], waits=[(SE, s_e2)])

                        @block.gpsimd
                        def _(g):
                            g.wait_ge(LDo.h, LDo.n)
                        return nc

                    @block.sync
                    def _(sy, g_r=g_r, ibl=ibl, gi=gi, s_e2=s_e2,
                          gate=tuple(atsl_free[0])):
                        sy.wait_ge(s_e2[0].h, s_e2[1])
                        sm, val = gate
                        if sm is not None and ibl == 0:
                            sy.wait_ge(sm.h, val)
                        TP.inc(sy.dma_start(out=ATsl[0][:, :, ibl, :],
                                            in_=Asb[0][:], transpose=True), 16)
                        sy.wait_ge(TP.h, TP.n)
                    asb_free[0] = (TP, TP.n)

                    if stage == "at1" and gi == 0:
                        tap("AT0", ATsl[0][:], waits=[(TP, TP.n)])

                        @block.gpsimd
                        def _(g):
                            g.wait_ge(LDo.h, LDo.n)
                        return nc
                tp_at[g_r] = TP.n

                for tb in range(TBM):
                    par = tb % 2
                    slot = pAR[par][:, 0:512]

                    @block.tensor
                    def _(t, g_r=g_r, tb=tb, slot=slot, gate=tuple(arena_free[par])):
                        t.wait_ge(TP.h, tp_at[g_r])
                        if g_r == 0 and tb == 0:
                            t.wait_ge(TP.h, tp_xnt)
                        sm, val = gate
                        if sm is not None:
                            t.wait_ge(sm.h, val)
                        for m in range(DC):
                            ins = t.matmul(slot, XnT[:, m, tb, :],
                                           flat2(ATsl[0][:, m, :, :]),
                                           start=(m == 0), stop=(m == DC - 1))
                        PE.inc(ins)
                    pe_out = PE.n

                    @block.vector
                    def _(v, g_r=g_r, tb=tb, slot=slot, pe_out=pe_out):
                        v.wait_ge(PE.h, pe_out)
                        a_sl = acc[:, tb, g_r * 512:(g_r + 1) * 512]
                        VE.inc(v.tensor_tensor(a_sl, a_sl, slot, OP.add))
                    arena_free[par] = (VE, VE.n)
                    accw[(g_r, tb)] = VE.n
                atsl_free[0] = (PE, PE.n)
            ve_x1 = VE.n
            tap("x1", acc[:], waits=[(VE, ve_x1)])


        if stage == "attn":
            @block.gpsimd
            def _(g):
                g.wait_ge(LDo.h, LDo.n)
            return nc
        with ExitStack() as ffn_scope:
            ef = ffn_scope.enter_context
            scores = ef(nc.sbuf_tensor("scores", [P, TBM, NE], F32))
            topv = ef(nc.sbuf_tensor("topv", [P, TBM, NE], F32))
            topi = ef(nc.sbuf_tensor("topi", [P, TBM, NE], U32))
            topf = ef(nc.sbuf_tensor("topf", [P, TBM, NE], F32))
            Ssc = ef(nc.sbuf_tensor("Ssc", [P, TBM, NE], F32))
            STb = ef(nc.sbuf_tensor("STb", [NE, TBM * P], BF16))
            W1T = ef(nc.sbuf_tensor("W1T", [P, DC, 2, 256], BF16))
            W3T = ef(nc.sbuf_tensor("W3T", [P, DC, 2, 256], BF16))
            W2T = ef(nc.sbuf_tensor("W2T", [P, UB, 2, 512], BF16))
            tmp1 = ef(nc.sbuf_tensor("tmp1", [P, 512], F32))
            tmp3 = ef(nc.sbuf_tensor("tmp3", [P, 512], F32))
            silb = ef(nc.sbuf_tensor("silb", [P, 512], F32))
            posu = ef(nc.sbuf_tensor("posu", [P, TBM, NE], U32))
            scpm = ef(nc.sbuf_tensor("scpm", [P, TBM, NE], F32))
            vals = ef(nc.sbuf_tensor("vals", [P, TBM], F32))
            pairb = ef(nc.sbuf_tensor("pairb", [P, 2, TBM, 2], F32))
            pref = ef(nc.sbuf_tensor("pref", [P, NCH, 2], F32))
            ones1 = ef(nc.sbuf_tensor("ones1", [1, P], F32))
            idxf = ef(nc.sbuf_tensor("idxf", [P, 4, NCH, 2], F32))
            idxu = ef(nc.sbuf_tensor("idxu", [P, 4, NCH], U32))
            gmin = ef(nc.sbuf_tensor("gmin", [P, 4, NCH], U32))
            e3t = ef(nc.sbuf_tensor("e3t", [P, TBM, NE], F32))
            sc_done = {}
            lists_ready = {}



            from contextlib import ExitStack as _ES
            xf_scope = _ES()
            cw_mask = xf_scope.enter_context(nc.sbuf_tensor("cw_mask", [NE, TBM * P], F32))
            cw_pos = xf_scope.enter_context(nc.sbuf_tensor("cw_pos", [NE, TBM * P], F32))
            cw_trash = xf_scope.enter_context(nc.sbuf_tensor("cw_trash", [NE, TBM * P], F32))
            cw_ones = xf_scope.enter_context(nc.sbuf_tensor("cw_ones", [NE, TBM * P], F32))
            hbuf = xf_scope.enter_context(nc.sbuf_tensor("hbuf", [P, UB, H], BF16))
            xfT = xf_scope.enter_context(nc.sbuf_tensor("xfT", [P, DC, TBM, P], BF16))
            xfring = xf_scope.enter_context(nc.sbuf_tensor("xfring", [P, 2, D], BF16))
            sp = {}



            def alloc_sparse():
                xf_scope.close()
                sp["sgrow"] = ffn_scope.enter_context(nc.sbuf_tensor("sgrow", [1, 4, CAP], F32))
                sp["Sg"] = ffn_scope.enter_context(nc.sbuf_tensor("Sg", [P, 2, CAP], F32))
                sp["xg"] = ffn_scope.enter_context(nc.sbuf_tensor("xg", [P, NCH, D], BF16))
                sp["xgT"] = ffn_scope.enter_context(nc.sbuf_tensor("xgT", [P, 2, NCH, DC, P], BF16))
                sp["hb"] = ffn_scope.enter_context(nc.sbuf_tensor("hb", [P, UB, CAP], BF16))
                sp["rgout"] = ffn_scope.enter_context(nc.sbuf_tensor("rgout", [P, NCH, D], BF16))

            tp_xf = []
            gate_prev = (None, 0)
            for grp in range(4):
                tbs = range(grp * 2, grp * 2 + 2)
                if grp > 0:
                    @block.scalar
                    def _(s, g0=LDf[0].n, g1=LDf[1].n):
                        s.wait_ge(LDf[0].h, g0)
                        s.wait_ge(LDf[1].h, g1)
                sns = emit_ln_group(
                    [(tb, acc[:, tb, :], xfring[:, tb % 2, :]) for tb in tbs],
                    scalar_pre=(VE, ve_x1) if grp == 0 else gate_prev,
                    vector_pre=(VE, ve_x1) if grp == 0 else None)
                for j, tb in enumerate(tbs):
                    @block.sync
                    def _(sy, tb=tb, sn=sns[j]):
                        sy.wait_ge(SE.h, sn)
                        TP.inc(sy.dma_start(out=xfT[:, :, tb, :],
                                            in_=xfring[:, tb % 2, :],
                                            transpose=True), 16)
                        sy.wait_ge(TP.h, TP.n)
                    tp_xf.append(TP.n)

                    @block.gpsimd
                    def _(g, tb=tb, sn=sns[j]):
                        g.wait_ge(SE.h, sn)
                        LDf[tb % 2].inc(g.dma_start(
                            out=xfd.rearrange("(tb p) d -> p tb d", p=P)[:, tb, :],
                            in_=xfring[:, tb % 2, :]), 16)
                gate_prev = (TP, TP.n)
            tp_xfT = TP.n
            ldf_all = (LDf[0].n, LDf[1].n)

            gate_arena0 = tuple(arena_free[0])
            gate_arena1 = tuple(arena_free[1])
            pe_gate = []
            s_gate_evac = []
            for tb in range(TBM):
                @block.tensor
                def _(t, tb=tb):
                    t.wait_ge(VE.h, ve_consts)
                    t.wait_ge(TP.h, tp_xf[tb])
                    if tb < 2:
                        sm, val = gate_arena1
                        if sm is not None:
                            t.wait_ge(sm.h, val)
                    if tb >= 2:
                        t.wait_ge(SE.h, s_gate_evac[tb - 2])
                    for m in range(DC):
                        ins = t.matmul(pAR[1][:, (tb % 2) * 512:(tb % 2) * 512 + NE],
                                       xfT[:, m, tb, :], gT[:, m, 0:NE],
                                       start=(m == 0), stop=(m == DC - 1))
                    PE.inc(ins)
                pe_gate.append(PE.n)

                @block.scalar
                def _(s, tb=tb):
                    s.wait_ge(PE.h, pe_gate[tb])
                    SE.inc(s.activation(out=scores[:, tb, :],
                                        in_=pAR[1][:, (tb % 2) * 512:(tb % 2) * 512 + NE],
                                        func=AF.Sigmoid))
                s_gate_evac.append(SE.n)
            s_scores = SE.n
            tap("scores", scores[:], waits=[(SE, s_scores)])

            if stage == "scores":
                @block.gpsimd
                def _(g):
                    g.wait_ge(LDo.h, LDo.n)
                return nc

            @block.vector
            def _(v):
                v.wait_ge(SE.h, s_scores)
                for tb in range(TBM):
                    VE.inc(v.max(topv[:, tb, :], scores[:, tb, :]))
            v_max = VE.n

            @block.vector
            def _(v):
                v.wait_ge(VE.h, v_max)
                for tb in range(TBM):
                    VE.inc(v.max_index(topi[:, tb, :], topv[:, tb, :], scores[:, tb, :]))
            v_mi = VE.n

            @block.vector
            def _(v):
                v.wait_ge(VE.h, v_mi)
                v.wait_ge(GP.h, 1)
                for tb in range(TBM):
                    VE.inc(v.tensor_copy(topf[:, tb, :], topi[:, tb, :]))
            v_tf = VE.n

            @block.vector
            def _(v):
                v.wait_ge(VE.h, v_tf)
                for tb in range(TBM):
                    VE.inc(v.tensor_scalar(Ssc[:, tb, :], iota8[:], topf[:, tb, 0:1],
                                           None, OP.is_equal))
                    VE.inc(v.tensor_scalar(e3t[:, tb, :], iota8[:], topf[:, tb, 1:2],
                                           None, OP.is_equal))
            v_eq = VE.n

            @block.vector
            def _(v):
                v.wait_ge(VE.h, v_eq)
                for tb in range(TBM):
                    VE.inc(v.tensor_scalar(Ssc[:, tb, :], Ssc[:, tb, :], topv[:, tb, 0:1],
                                           None, OP.mult))
            v_s1 = VE.n

            @block.vector
            def _(v):
                v.wait_ge(VE.h, v_s1)
                for tb in range(TBM):
                    VE.inc(v.scalar_tensor_tensor(Ssc[:, tb, :], e3t[:, tb, :],
                                                  topv[:, tb, 1:2], Ssc[:, tb, :],
                                                  OP.mult, OP.add))
            v_S = VE.n
            tap("S", Ssc[:], waits=[(VE, v_S)])

            st_state = {}

            def emit_ST():
                ve_st = []
                for tb in range(TBM):
                    @block.tensor
                    def _(t, tb=tb):
                        t.wait_ge(VE.h, v_S)
                        t.wait_ge(VE.h, ve_consts)
                        if tb == 0:
                            sm, val = gate_arena1
                            if sm is not None:
                                t.wait_ge(sm.h, val)
                        else:
                            t.wait_ge(VE.h, ve_st[tb - 1])
                        PE.inc(t.transpose(pAR[1][0:NE, 1024:1024 + P], Ssc[:, tb, :],
                                           identt[:]))
                    pe_st = PE.n

                    @block.vector
                    def _(v, tb=tb, pe_st=pe_st):
                        v.wait_ge(PE.h, pe_st)
                        VE.inc(v.tensor_copy(STb[0:NE, tb * P:(tb + 1) * P],
                                             pAR[1][0:NE, 1024:1024 + P]))
                    ve_st.append(VE.n)
                st_state["v_stb"] = VE.n

            def emit_compact():
                v_stb = st_state["v_stb"]

                @block.vector
                def _(v):
                    v.wait_ge(VE.h, v_stb)
                    v.wait_ge(GP.h, gp_sparse)
                    VE.inc(v.memset(cw_ones[:], 1.0))
                    VE.inc(v.tensor_single_scalar(cw_mask[:], STb[:], 0.0, OP.is_gt))
                c_m = VE.n

                @block.vector
                def _(v):
                    v.wait_ge(VE.h, c_m)
                    VE.inc(v.tensor_tensor_scan(cw_pos[:], cw_ones[:], cw_mask[:],
                                                0.0, OP.mult, OP.add))
                c_p1 = VE.n

                @block.vector
                def _(v):
                    v.wait_ge(VE.h, c_p1)
                    VE.inc(v.tensor_single_scalar(cw_pos[:], cw_pos[:], -1.0, OP.add))
                    VE.inc(v.tensor_copy(cw_ones[:], STb[:]))
                c_p2 = VE.n

                @block.vector
                def _(v):
                    v.wait_ge(VE.h, c_p2)
                    VE.inc(v.tensor_tensor(cw_pos[:], cw_pos[:], cw_trash[:], OP.subtract))
                c_p3 = VE.n

                @block.vector
                def _(v):
                    v.wait_ge(VE.h, c_p3)
                    VE.inc(v.tensor_tensor(cw_pos[:], cw_mask[:], cw_pos[:], OP.mult))
                c_p4 = VE.n

                @block.vector
                def _(v):
                    v.wait_ge(VE.h, c_p4)
                    VE.inc(v.tensor_tensor(cw_pos[:], cw_pos[:], cw_trash[:], OP.add))
                c_pos = VE.n

                prev_ev = [0]
                for tb in range(TBM):
                    @block.tensor
                    def _(t, tb=tb):
                        t.wait_ge(VE.h, c_pos)
                        if tb > 0:
                            t.wait_ge(VE.h, prev_ev[0])
                        t.transpose(pAR[1][:, 1024:1024 + NE],
                                    cw_pos[0:NE, tb * P:(tb + 1) * P], identt[0:NE, 0:NE])
                        PE.inc(t.transpose(pAR[1][:, 1536:1536 + NE],
                                           cw_ones[0:NE, tb * P:(tb + 1) * P],
                                           identt[0:NE, 0:NE]))
                    pe_t = PE.n

                    @block.vector
                    def _(v, tb=tb, pe_t=pe_t):
                        v.wait_ge(PE.h, pe_t)
                        VE.inc(v.tensor_copy(posu[:, tb, :], pAR[1][:, 1024:1024 + NE]))
                        VE.inc(v.tensor_copy(scpm[:, tb, :], pAR[1][:, 1536:1536 + NE]))
                    prev_ev[0] = VE.n
                c_tp = VE.n
                st_state["c_tp"] = c_tp

                for ex in range(NE):
                    slot = ex % 2

                    @block.gpsimd
                    def _(g, ex=ex):
                        g.wait_ge(GP.h, gp_sparse)
                        LDz.inc(g.dma_start(
                            out=plist[ex, 0:CAP].rearrange("(c p) w -> p c w", p=P),
                            in_=pref[:]), 16)
                    ld_pf = LDz.n

                    @block.vector
                    def _(v, ex=ex, slot=slot, gate=GDc[slot].n):
                        v.wait_ge(VE.h, c_tp)
                        if gate:
                            v.wait_ge(GDc[slot].h, gate)
                        VE.inc(v.tensor_copy(
                            pairb[:, slot, :, 0:1],
                            vals[:].rearrange("p (c one) -> p c one", one=1)))
                        VE.inc(v.tensor_copy(pairb[:, slot, :, 1:2],
                                             scpm[:, :, ex:ex + 1]))
                    v_pair = VE.n

                    @block.gpsimd
                    def _(g, ex=ex, slot=slot, v_pair=v_pair, ld_pf=ld_pf):
                        g.wait_ge(VE.h, v_pair)
                        g.wait_ge(LDz.h, ld_pf)
                        for tb in range(TBM):
                            GDc[slot].inc(g.indirect_dma_start(
                                out=plist.rearrange("E c w -> (E c) w"),
                                out_offset=IndirectOffsetOnAxis(posu[:, tb, ex:ex + 1], 0),
                                in_=pairb[:, slot, tb, :],
                                in_offset=None,
                                element_offset=ex * (CAP + TBM * P) * 2), 16)
                    sc_done[ex] = (GDc[slot], GDc[slot].n)

            gather_done = set()
            gather_tp = {}

            def emit_gather_chain(ex):
                gather_done.add(ex)
                slot = ex % 4
                buf = ex % 2
                lr_ve, _sl = lists_ready[ex]
                for c in range(NCH):
                    @block.gpsimd
                    def _(g, ex=ex, c=c, slot=slot, lr_ve=lr_ve,
                          gate=tuple(xg_free[0]), mg=tuple(mem_gate)):
                        g.wait_ge(VE.h, lr_ve)
                        if c == 0:
                            g.wait_ge(LDf[0].h, ldf_all[0])
                            g.wait_ge(LDf[1].h, ldf_all[1])
                            sm, val = gate
                            if sm is not None:
                                g.wait_ge(sm.h, val)
                            sm2, val2 = mg
                            if sm2 is not None:
                                g.wait_ge(sm2.h, val2)
                        LDxg[c].inc(g.indirect_dma_start(
                            out=sp["xg"][:, c, :],
                            out_offset=None,
                            in_=xfd,
                            in_offset=IndirectOffsetOnAxis(gmin[:, slot, c:c + 1], 0)), 16)
                    ld_gs = [LDxg[i].n for i in range(NCH)]

                @block.sync
                def _(sy, buf=buf, ld_gs=tuple(ld_gs), xfree=tuple(xgt_free[buf]),
                      mg=tuple(mem_gate)):
                    for i in range(NCH):
                        sy.wait_ge(LDxg[i].h, ld_gs[i])
                    sm, val = xfree
                    if sm is not None:
                        sy.wait_ge(sm.h, val)
                    sm2, val2 = mg
                    if sm2 is not None:
                        sy.wait_ge(sm2.h, val2)
                    TP.inc(sy.dma_start(out=sp["xgT"][:, buf, :, :, :],
                                        in_=sp["xg"][:].rearrange("p c d -> p (c d)"),
                                        transpose=True), 16)
                    sy.wait_ge(TP.h, TP.n)
                gather_tp[ex] = TP.n
                xg_free[0] = (TP, TP.n)

            def emit_reload(ex, gate):
                slot = ex % 4

                @block.gpsimd
                def _(g, ex=ex, slot=slot, sc=sc_done[ex], gate=gate):
                    g.wait_ge(sc[0].h, sc[1])
                    if gate[0] is not None:
                        g.wait_ge(gate[0].h, gate[1])
                    LDl[slot].inc(g.dma_start(
                        out=idxf[:, slot, :, :],
                        in_=plist[ex, 0:CAP].rearrange("(c p) w -> p c w", p=P)), 16)
                    LDl[slot].inc(g.dma_start(
                        out=sp["sgrow"][0:1, slot, :],
                        in_=plist[ex, 0:CAP, 1:2].rearrange("c w -> (w c)")[None, :],
                        allow_slow_non_contiguous=True), 16)
                ld_l = LDl[slot].n

                @block.vector
                def _(v, slot=slot, ld_l=ld_l):
                    v.wait_ge(LDl[slot].h, ld_l)
                    VE.inc(v.tensor_copy(idxu[:, slot, :], idxf[:, slot, :, 0]))
                v_iu = VE.n

                @block.vector
                def _(v, slot=slot, v_iu=v_iu):
                    v.wait_ge(VE.h, v_iu)
                    VE.inc(v.tensor_single_scalar(gmin[:, slot, :], idxu[:, slot, :],
                                                  TBM * P - 1, OP.min))
                lists_ready[ex] = (VE.n, slot)

            w1t_r = w1t.rearrange("E (m p) u -> E p m u", p=P)
            w3t_r = w3t.rearrange("E (m p) u -> E p m u", p=P)
            w2t_r = w2t.rearrange("E (mu p) d -> E p mu d", p=P)
            ws1t_r = ws1t.rearrange("(m p) u -> p m u", p=P)
            ws3t_r = ws3t.rearrange("(m p) u -> p m u", p=P)
            ws2t_r = ws2t.rearrange("(mu p) d -> p mu d", p=P)

            w1_free = [(VE, ve_x1), (VE, ve_x1)]
            w3_free = [(VE, ve_x1), (VE, ve_x1)]
            w2_free = [(VE, ve_x1), (VE, ve_x1)]
            w_ready = {}
            w_emitted = set()

            def load_w(ex, kind, idx):
                if ex >= NEXP or (ex, kind, idx) in w_emitted:
                    return
                w_emitted.add((ex, kind, idx))
                slot = idx % 2
                if kind == 1:
                    srcs = w1t_r[ex] if ex < NE else ws1t_r
                    srcap = srcs[:, :, idx * 256:(idx + 1) * 256]
                    dst = W1T[:, :, slot, :]
                    sem, gate = LDw1[slot], tuple(w1_free[slot])
                elif kind == 3:
                    srcs = w3t_r[ex] if ex < NE else ws3t_r
                    srcap = srcs[:, :, idx * 256:(idx + 1) * 256]
                    dst = W3T[:, :, slot, :]
                    sem, gate = LDw3[slot], tuple(w3_free[slot])
                else:
                    srcs = w2t_r[ex] if ex < NE else ws2t_r
                    srcap = srcs[:, :, idx * 512:(idx + 1) * 512]
                    dst = W2T[:, :, slot, :]
                    sem, gate = LDw2[slot], tuple(w2_free[slot])

                @block.gpsimd
                def _(g, srcap=srcap, dst=dst, sem=sem, gate=gate):
                    sm, val = gate
                    if sm is not None:
                        g.wait_ge(sm.h, val)
                    sem.inc(g.dma_start(out=dst, in_=srcap), 16)
                w_ready[(ex, kind, idx)] = (sem, sem.n)

            arena_free_z = [tuple(arena_free[0]), (None, 0)]
            last_sil = [0]
            last_hmul = [0]
            slotfree_r = [(None, 0), (None, 0)]
            h_readers_done = [0]

            ORDER = [NE] + list(range(NE))
            rscat = {}
            sg_free = (None, 0)
            xgt_free = [(None, 0), (None, 0)]
            xg_free = [(None, 0)]
            rg_free = (None, 0)
            mem_gate = (None, 0)
            reload_done = set()
            pending_scatter = [None]
            ve_acc_done = 0

            for ei, ex in enumerate(ORDER):
                nxt = ORDER[ei + 1] if ei + 1 < len(ORDER) else NEXP
                is_shared = (ex == NE)
                for q in range(2):
                    load_w(ex, 1, q)
                    load_w(ex, 3, q)
                for kg in range(2):
                    load_w(ex, 2, kg)

                if ei == 0:
                    @block.gpsimd
                    def _(g):
                        g.wait_ge(VE.h, ve_x1)
                        GP.inc(g.iota(vals[:], pattern=[[P, TBM]], base=0, channel_multiplier=1,
                                      allow_small_or_imprecise_dtypes=True))
                        GP.inc(g.iota(pref[:, :, 0:1], pattern=[[P, NCH], [0, 1]], base=TBM * P,
                                      channel_multiplier=1, allow_small_or_imprecise_dtypes=True))
                        GP.inc(g.iota(cw_trash[:], pattern=[[1, TBM * P]], base=CAP,
                                      channel_multiplier=0, allow_small_or_imprecise_dtypes=True))
                    gp_sparse = GP.n

                    @block.vector
                    def _(v):
                        v.wait_ge(GP.h, gp_sparse)
                        v.wait_ge(VE.h, ve_x1)
                        VE.inc(v.memset(pref[:, :, 1:2], 0.0))
                        VE.inc(v.memset(ones1[:], 1.0))

                    ve_zero_src = VE.n

                    ldz_zero = LDz.n

                if is_shared:
                    pair_free = [tuple(arena_free_z[0]), tuple(arena_free_z[0])]
                    for step in range(2 * UB):
                        ub, th = step // 2, step % 2
                        q = ub // 2
                        uo = (ub % 2) * P
                        o = (step % 2) * 1024
                        z1 = pAR[0][:, o:o + 512]
                        z3 = pAR[0][:, o + 512:o + 1024]
                        h_sl = hbuf[:, ub, th * 512:(th + 1) * 512]

                        @block.tensor
                        def _(t, ex=ex, q=q, uo=uo, th=th, z1=z1, z3=z3,
                              gate=tuple(pair_free[step % 2])):
                            r1 = w_ready[(ex, 1, q)]
                            r3 = w_ready[(ex, 3, q)]
                            t.wait_ge(r1[0].h, r1[1])
                            t.wait_ge(r3[0].h, r3[1])
                            sm, val = gate
                            if sm is not None:
                                t.wait_ge(sm.h, val)
                            for m in range(DC):
                                l1 = W1T[:, m, q % 2, uo:uo + P]
                                l3 = W3T[:, m, q % 2, uo:uo + P]
                                rhs = flat2(xfT[:, m, th * 4:(th + 1) * 4, :])
                                t.matmul(z1, l1, rhs, start=(m == 0),
                                         stop=(m == DC - 1), skip_group_check=True)
                                ins = t.matmul(z3, l3, rhs, start=(m == 0),
                                               stop=(m == DC - 1), skip_group_check=True)
                            PE.inc(ins)
                        pe_z = PE.n
                        if step == 7:
                            emit_ST()
                            emit_compact()
                        if ub % 2 == 1 and th == 1:
                            w1_free[q % 2] = (PE, pe_z)
                            w3_free[q % 2] = (PE, pe_z)
                            if q + 2 <= 3:
                                load_w(ex, 1, q + 2)
                                load_w(ex, 3, q + 2)
                            else:
                                load_w(nxt, 1, q - 2)
                                load_w(nxt, 3, q - 2)

                        @block.scalar
                        def _(s, pe_z=pe_z, z1=z1, lh=last_hmul[0]):
                            s.wait_ge(PE.h, pe_z)
                            if lh:
                                s.wait_ge(VE.h, lh)
                            SE.inc(s.activation(out=silb[:], in_=z1, func=AF.Silu))
                        s_sil = SE.n
                        last_sil[0] = s_sil

                        @block.vector
                        def _(v, z3=z3, h_sl=h_sl, s_sil=s_sil):
                            v.wait_ge(SE.h, s_sil)
                            VE.inc(v.tensor_tensor(h_sl, silb[:], z3, OP.mult))
                        last_hmul[0] = VE.n
                        pair_free[step % 2] = (VE, VE.n)
                    arena_free_z[0] = (VE, VE.n)
                    arena_free_z[1] = (VE, st_state["c_tp"])
                    ve_h_all = VE.n
                    pe_shared_z = PE.n

                    for kg in range(4):
                        for tb in range(TBM):
                            par = tb % 2
                            slot = pAR[par][:, 0:512]

                            @block.tensor
                            def _(t, ex=ex, kg=kg, tb=tb, slot=slot,
                                  gate=tuple(slotfree_r[par]),
                                  zgate=tuple(arena_free_z[par]), ve_h_all=ve_h_all):
                                r2 = w_ready[(ex, 2, kg)]
                                t.wait_ge(r2[0].h, r2[1])
                                t.wait_ge(VE.h, ve_h_all)
                                sm, val = gate
                                if sm is not None:
                                    t.wait_ge(sm.h, val)
                                if kg == 0 and tb < 2:
                                    sm2, val2 = zgate
                                    if sm2 is not None:
                                        t.wait_ge(sm2.h, val2)
                                for uc in range(UB):
                                    ins = t.matmul(slot,
                                                   hbuf[:, uc, tb * P:(tb + 1) * P],
                                                   W2T[:, uc, kg % 2, :],
                                                   start=(uc == 0), stop=(uc == UB - 1))
                                PE.inc(ins)
                            pe_r = PE.n
                            if tb == TBM - 1:
                                w2_free[kg % 2] = (PE, pe_r)
                                if kg + 2 <= 3:
                                    load_w(ex, 2, kg + 2)
                                else:
                                    load_w(nxt, 2, kg - 2)

                            @block.vector
                            def _(v, kg=kg, tb=tb, slot=slot, pe_r=pe_r,
                                  aw=accw[(kg, tb)]):
                                v.wait_ge(PE.h, pe_r)
                                v.wait_ge(VE.h, aw)
                                a_sl = acc[:, tb, kg * 512:(kg + 1) * 512]
                                VE.inc(v.tensor_tensor(a_sl, a_sl, slot, OP.add))
                            slotfree_r[par] = (VE, VE.n)
                            accw[(kg, tb)] = VE.n
                    h_readers_done[0] = PE.n
                    arena_free_z = [(VE, VE.n), (VE, VE.n)]
                    ve_acc_done = VE.n

                    mem_gate = (PE, PE.n)
                    alloc_sparse()
                    for _re in range(2):
                        emit_reload(_re, mem_gate)
                        reload_done.add(_re)
                    emit_gather_chain(0)

                    @block.gpsimd
                    def _(g):
                        g.wait_ge(VE.h, ve_acc_done)
                        LDz.inc(g.dma_start(out=out_t[:, 0:TBM, :], in_=acc[:]), 16)
                    ldz_acc = LDz.n
                    rg_free = (LDz, ldz_acc)
                    continue

                slot = ex % 4
                sgslot = ex % 2
                if ex not in reload_done:
                    emit_reload(ex, (GDr, rscat.get(ex - 4, 0)))
                    reload_done.add(ex)
                if ex + 2 < NE and (ex + 2) not in reload_done:
                    emit_reload(ex + 2, mem_gate if ex + 2 < 4 else
                                (GDr, rscat.get(ex - 2, 0)))
                    reload_done.add(ex + 2)
                lr_ve, _sl = lists_ready[ex]
                buf = ex % 2

                if pending_scatter[0] is not None:
                    pending_scatter[0]()
                    pending_scatter[0] = None
                if ex not in gather_done:
                    emit_gather_chain(ex)
                tp_xgt_all = gather_tp[ex]

                ve_sg_box = [0]
                for ub in range(UB):
                    par = ub % 2
                    ps = pAR[par]
                    q = ub // 2
                    uo = (ub % 2) * P
                    z1 = ps[:, 0:CAP]
                    z3 = ps[:, 512:512 + CAP]
                    h_sl = sp["hb"][:, ub, :]

                    @block.tensor
                    def _(t, ex=ex, ub=ub, q=q, uo=uo, ps=ps, z1=z1, z3=z3, buf=buf,
                          gate=tuple(arena_free_z[par]), tp_xgt_all=tp_xgt_all):
                        r1 = w_ready[(ex, 1, q)]
                        r3 = w_ready[(ex, 3, q)]
                        t.wait_ge(r1[0].h, r1[1])
                        t.wait_ge(r3[0].h, r3[1])
                        t.wait_ge(TP.h, tp_xgt_all)
                        sm, val = gate
                        if sm is not None:
                            t.wait_ge(sm.h, val)
                        for m in range(DC):
                            l1 = W1T[:, m, q % 2, uo:uo + P]
                            l3 = W3T[:, m, q % 2, uo:uo + P]
                            rhs = sp["xgT"][:, buf, :, m, :]
                            t.matmul(z1, l1, rhs, start=(m == 0),
                                     stop=(m == DC - 1), skip_group_check=True)
                            ins = t.matmul(z3, l3, rhs, start=(m == 0),
                                           stop=(m == DC - 1), skip_group_check=True)
                        PE.inc(ins)
                    pe_z = PE.n
                    if ub == 0:
                        @block.tensor
                        def _(t, ex=ex, slot=slot, gate=tuple(sg_free)):
                            t.wait_ge(LDl[slot].h, LDl[slot].n)
                            sm, val = gate
                            if sm is not None:
                                t.wait_ge(sm.h, val)
                            PE.inc(t.matmul(pAR[0][:, 1536:1536 + CAP], ones1[0:1, :],
                                            sp["sgrow"][0:1, slot, :],
                                            start=True, stop=True))
                        pe_sg = PE.n

                        @block.vector
                        def _(v, sgslot=sgslot, pe_sg=pe_sg):
                            v.wait_ge(PE.h, pe_sg)
                            VE.inc(v.tensor_copy(sp["Sg"][:, sgslot, :],
                                                 pAR[0][:, 1536:1536 + CAP]))
                        sg_free = (VE, VE.n)
                        ve_sg_box[0] = VE.n
                    if ub % 2 == 1:
                        w1_free[q % 2] = (PE, pe_z)
                        w3_free[q % 2] = (PE, pe_z)
                        if q + 2 <= 3:
                            load_w(ex, 1, q + 2)
                            load_w(ex, 3, q + 2)
                        else:
                            load_w(nxt, 1, q - 2)
                            load_w(nxt, 3, q - 2)
                    if ub == UB - 1:
                        xgt_free[buf] = (PE, pe_z)

                    @block.vector
                    def _(v, pe_z=pe_z, z1=z1, sgslot=sgslot, ls=last_sil[0],
                          ve_sg=ve_sg_box[0]):
                        v.wait_ge(PE.h, pe_z)
                        v.wait_ge(VE.h, ve_sg)
                        if ls:
                            v.wait_ge(SE.h, ls)
                        VE.inc(v.tensor_tensor(tmp1[:, 0:CAP], z1, sp["Sg"][:, sgslot, :],
                                               OP.mult))
                    v_t1 = VE.n

                    @block.scalar
                    def _(s, v_t1=v_t1):
                        s.wait_ge(VE.h, v_t1)
                        SE.inc(s.activation(out=silb[:, 0:CAP], in_=tmp1[:, 0:CAP],
                                            func=AF.Silu))
                    s_sil = SE.n
                    last_sil[0] = s_sil

                    @block.vector
                    def _(v, z3=z3, h_sl=h_sl, sgslot=sgslot, s_sil=s_sil,
                          hrd=h_readers_done[0], lh=last_hmul[0]):
                        if lh:
                            v.wait_ge(VE.h, lh)
                        v_t3 = VE.inc(v.tensor_tensor(tmp3[:, 0:CAP], z3,
                                                      sp["Sg"][:, sgslot, :], OP.mult))
                        v.wait_ge(SE.h, s_sil)
                        v.wait_ge(PE.h, hrd)
                        v.wait_ge(VE.h, v_t3)
                        VE.inc(v.tensor_tensor(h_sl, silb[:, 0:CAP], tmp3[:, 0:CAP],
                                               OP.mult))
                    last_hmul[0] = VE.n
                    arena_free_z[par] = (VE, VE.n)
                ve_h_all = VE.n



                for kg in range(4):
                    for tg in range(NCH):
                        par = (kg * NCH + tg) % 2
                        rps = pAR[par][:, 1024:1536]

                        @block.tensor
                        def _(t, ex=ex, kg=kg, tg=tg, rps=rps,
                              gate=tuple(slotfree_r[par]), ve_h_all=ve_h_all):
                            r2 = w_ready[(ex, 2, kg)]
                            t.wait_ge(r2[0].h, r2[1])
                            t.wait_ge(VE.h, ve_h_all)
                            sm, val = gate
                            if sm is not None:
                                t.wait_ge(sm.h, val)
                            for uc in range(UB):
                                ins = t.matmul(rps, sp["hb"][:, uc, tg * P:(tg + 1) * P],
                                               W2T[:, uc, kg % 2, :],
                                               start=(uc == 0), stop=(uc == UB - 1))
                            PE.inc(ins)
                        pe_r = PE.n
                        if tg == NCH - 1:
                            w2_free[kg % 2] = (PE, pe_r)
                            if kg + 2 <= 3:
                                load_w(ex, 2, kg + 2)
                            else:
                                load_w(nxt, 2, kg - 2)

                        @block.vector
                        def _(v, kg=kg, tg=tg, rps=rps, pe_r=pe_r,
                              rgate=tuple(rg_free)):
                            v.wait_ge(PE.h, pe_r)
                            if kg == 0:
                                sm, val = rgate
                                if sm is not None:
                                    v.wait_ge(sm.h, val)
                            VE.inc(v.tensor_copy(sp["rgout"][:, tg, kg * 512:(kg + 1) * 512],
                                                 rps))
                        slotfree_r[par] = (VE, VE.n)
                ve_rg_all = VE.n

                def mk_scatter(ex=ex, slot=slot, ve_rg_all=ve_rg_all,
                               prev=rscat.get(ex - 1, 0), rgate=tuple(rg_free)):
                    def emit():
                        @block.gpsimd
                        def _(g):
                            g.wait_ge(VE.h, ve_rg_all)
                            sm, val = rgate
                            if sm is not None:
                                g.wait_ge(sm.h, val)
                            if prev:
                                g.wait_ge(GDr.h, prev)
                            for tg in range(NCH):
                                g.indirect_dma_start(
                                    out=out,
                                    out_offset=IndirectOffsetOnAxis(
                                        idxu[:, slot, tg:tg + 1], 0),
                                    in_=sp["rgout"][:, tg, :],
                                    in_offset=None,
                                    compute_op=OP.add).then_inc(GDr.h, 16)
                    return emit
                pending_scatter[0] = mk_scatter()
                GDr.n += NCH * 16
                rscat[ex] = GDr.n
                rg_free = (GDr, GDr.n)
                h_readers_done[0] = PE.n

            if pending_scatter[0] is not None:
                pending_scatter[0]()
                pending_scatter[0] = None
            ve_final = VE.n

            @block.gpsimd
            def _(g):
                g.wait_ge(GDr.h, GDr.n)
                g.wait_ge(LDz.h, LDz.n)
                g.wait_ge(LDo.h, LDo.n)

    return nc


def _host_inputs(L0_attn_x, L0_ffn_gate_w, L0_ffn_w1, L0_ffn_w2, L0_ffn_w3,
                 L0_ffn_ws1, L0_ffn_ws2, L0_ffn_ws3):
    xf = np.ascontiguousarray(np.asarray(L0_attn_x, dtype=np.float32))
    gw = np.ascontiguousarray(np.asarray(L0_ffn_gate_w, dtype=np.float32))
    w1 = np.ascontiguousarray(np.asarray(L0_ffn_w1, dtype=np.float32))
    w2 = np.ascontiguousarray(np.asarray(L0_ffn_w2, dtype=np.float32))
    w3 = np.ascontiguousarray(np.asarray(L0_ffn_w3, dtype=np.float32))
    ws1 = np.ascontiguousarray(np.asarray(L0_ffn_ws1, dtype=np.float32))
    ws2 = np.ascontiguousarray(np.asarray(L0_ffn_ws2, dtype=np.float32))
    ws3 = np.ascontiguousarray(np.asarray(L0_ffn_ws3, dtype=np.float32))
    gwT = np.ascontiguousarray(gw.T)
    w1t = np.ascontiguousarray(w1.transpose(0, 2, 1))
    w2t = np.ascontiguousarray(w2.transpose(0, 2, 1))
    w3t = np.ascontiguousarray(w3.transpose(0, 2, 1))
    ws1t = np.ascontiguousarray(ws1.T)
    ws2t = np.ascontiguousarray(ws2.T)
    ws3t = np.ascontiguousarray(ws3.T)
    in_maps = []
    for c in range(8):
        b, h = c // 2, c % 2
        mine = xf[b, h * 1024:(h + 1) * 1024]
        other = xf[b, (1 - h) * 1024:(2 - h) * 1024]
        xc = np.ascontiguousarray(np.concatenate([mine, other], axis=0))
        in_maps.append(dict(x=xc, gwT=gwT, w1t=w1t, w2t=w2t, w3t=w3t,
                            ws1t=ws1t, ws2t=ws2t, ws3t=ws3t))
    return in_maps


def kernel(L0_attn_x, L0_ffn_gate_w, L0_ffn_w1, L0_ffn_w2, L0_ffn_w3,
           L0_ffn_ws1, L0_ffn_ws2, L0_ffn_ws3):
    in_maps = _host_inputs(L0_attn_x, L0_ffn_gate_w, L0_ffn_w1, L0_ffn_w2,
                           L0_ffn_w3, L0_ffn_ws1, L0_ffn_ws2, L0_ffn_ws3)
    nc = build()
    res = run_bass_kernel_spmd(nc, in_maps, core_ids=list(range(8)))
    out = np.empty((4, 2048, 2048), dtype=np.float32)
    for c in range(8):
        b, h = c // 2, c % 2
        out[b, h * 1024:(h + 1) * 1024] = res.results[c]["out"][0:1024]
    return out


# version 22
# speedup vs baseline: 1.0104x; 1.0104x over previous
import numpy as np
from contextlib import ExitStack

import concourse.bass as bass
import concourse.mybir as mybir
from concourse.bass_utils import run_bass_kernel_spmd
from concourse.alu_op_type import AluOpType as OP
from concourse.bass import IndirectOffsetOnAxis

F32 = mybir.dt.float32
BF16 = mybir.dt.bfloat16
U32 = mybir.dt.uint32
AF = mybir.ActivationFunctionType
AX = mybir.AxisListType

P = 128
D = 2048
SEQ = 2048
TB = 16
TBM = 8
DC = 16
H = 1024
UB = 8
NE = 8
NEXP = 9
EPS = 1e-5
INV_D = 1.0 / D

CAP = 384
NCH = CAP // P
W1WIN = 4
W2WIN = 4


class Sem:
    def __init__(self, h):
        self.h = h
        self.n = 0

    def inc(self, instr, k=1):
        instr.then_inc(self.h, k)
        self.n += k
        return self.n


def flat2(ap):
    return ap.rearrange("p a b -> p (a b)")


def build(want_taps=(), stage="full"):
    nc = bass.Bass("TRN2", target_bir_lowering=False, debug=False)

    x = nc.dram_tensor("x", [SEQ, D], F32, kind="ExternalInput").ap()
    gwT = nc.dram_tensor("gwT", [D, NE], F32, kind="ExternalInput").ap()
    w1t = nc.dram_tensor("w1t", [NE, D, H], F32, kind="ExternalInput").ap()
    w2t = nc.dram_tensor("w2t", [NE, H, D], F32, kind="ExternalInput").ap()
    w3t = nc.dram_tensor("w3t", [NE, D, H], F32, kind="ExternalInput").ap()
    ws1t = nc.dram_tensor("ws1t", [D, H], F32, kind="ExternalInput").ap()
    ws2t = nc.dram_tensor("ws2t", [H, D], F32, kind="ExternalInput").ap()
    ws3t = nc.dram_tensor("ws3t", [D, H], F32, kind="ExternalInput").ap()
    out = nc.dram_tensor("out", [TBM * P + CAP + P, D], F32, kind="ExternalOutput").ap()
    plist = nc.dram_tensor("plist", [NE, CAP + TBM * P, 2], F32, kind="Internal").ap()
    xfd = nc.dram_tensor("xfd", [TBM * P, D], BF16, kind="Internal").ap()

    x_t = x.rearrange("(tb p) d -> p tb d", p=P)
    out_t = out.rearrange("(tb p) d -> p tb d", p=P)

    with ExitStack() as top:
        e = top.enter_context
        acc = e(nc.sbuf_tensor("acc", [P, TBM, D], F32))
        identt = e(nc.sbuf_tensor("identt", [P, P], F32))
        iota8 = e(nc.sbuf_tensor("iota8", [P, NE], F32))
        gT = e(nc.sbuf_tensor("gT", [P, DC, NE], BF16))
        gws = e(nc.sbuf_tensor("gws", [P, DC, NE], F32))
        st_sum = e(nc.sbuf_tensor("st_sum", [P, TB], F32))
        st_ssq = e(nc.sbuf_tensor("st_ssq", [P, TB], F32))
        st_var = e(nc.sbuf_tensor("st_var", [P, TB], F32))
        st_rstd = e(nc.sbuf_tensor("st_rstd", [P, TB], F32))
        st_nmr = e(nc.sbuf_tensor("st_nmr", [P, TB], F32))
        st_tmp = e(nc.sbuf_tensor("st_tmp", [P, TB], F32))
        sqscr = e(nc.sbuf_tensor("sqscr", [P, D], F32))
        sm_mxn = e(nc.sbuf_tensor("sm_mxn", [P, 2], F32))
        sm_rs = e(nc.sbuf_tensor("sm_rs", [P, 2], F32))
        sm_lnr = e(nc.sbuf_tensor("sm_lnr", [P, 2], F32))
        sm_b2 = e(nc.sbuf_tensor("sm_b2", [P, 2], F32))

        pA = e(nc.psum_tensor("pA", [P, 512 * 4], F32))
        pB = e(nc.psum_tensor("pB", [P, 512 * 4], F32))
        pAR = [pA, pB]

        LDa = Sem(e(nc.semaphore("LDa")))
        LDx = [Sem(e(nc.semaphore(f"LDx{i}"))) for i in range(2)]
        LDg = Sem(e(nc.semaphore("LDg")))
        LDw1 = [Sem(e(nc.semaphore(f"LDw1_{i}"))) for i in range(2)]
        LDw3 = [Sem(e(nc.semaphore(f"LDw3_{i}"))) for i in range(2)]
        LDw2 = [Sem(e(nc.semaphore(f"LDw2_{i}"))) for i in range(2)]
        LDo = Sem(e(nc.semaphore("LDo")))
        LDf = [Sem(e(nc.semaphore(f"LDf{i}"))) for i in range(2)]
        GDc = [Sem(e(nc.semaphore(f"GDc{i}"))) for i in range(2)]
        LDl = [Sem(e(nc.semaphore(f"LDl{i}"))) for i in range(4)]
        LDxg = [Sem(e(nc.semaphore(f"LDxg{i}"))) for i in range(NCH)]
        GDr = Sem(e(nc.semaphore("GDr")))
        LDz = Sem(e(nc.semaphore("LDz")))
        GP = Sem(e(nc.semaphore("GP")))
        TP = Sem(e(nc.semaphore("TP")))
        PE = Sem(e(nc.semaphore("PE")))
        VE = Sem(e(nc.semaphore("VE")))
        SE = Sem(e(nc.semaphore("SE")))

        block = e(nc.Block())

        def tap(name, ap, waits=()):
            if name not in want_taps:
                return
            t = nc.dram_tensor("tap_" + name, list(ap.shape), ap.dtype,
                               kind="ExternalOutput").ap()

            @block.gpsimd
            def _(g, ap=ap, t=t, waits=tuple(waits)):
                for s, v in waits:
                    g.wait_ge(s.h, v)
                LDo.inc(g.dma_start(out=t, in_=ap), 16)

        @block.gpsimd
        def _(g):
            GP.inc(g.iota(iota8[:], pattern=[[1, NE]], base=0, channel_multiplier=0,
                          allow_small_or_imprecise_dtypes=True), 1)
            GP.inc(g.iota(identt[:], pattern=[[0, P]], base=0, channel_multiplier=1,
                          allow_small_or_imprecise_dtypes=True), 1)
            GP.inc(g.iota(sqscr[:, 0:P], pattern=[[1, P]], base=0, channel_multiplier=0,
                          allow_small_or_imprecise_dtypes=True), 1)
            LDg.inc(g.dma_start(out=gws[:], in_=gwT.rearrange("(m p) e -> p m e", p=P)), 16)
            LDa.inc(g.dma_start(out=acc[:], in_=x_t[:, 0:TBM, :]), 16)
        gp_consts = GP.n
        ld_gws = (LDg, LDg.n)
        ld_acc = (LDa, LDa.n)

        @block.vector
        def _(v):
            v.wait_ge(GP.h, gp_consts)
            v.wait_ge(ld_gws[0].h, ld_gws[1])
            VE.inc(v.tensor_tensor(identt[:], identt[:], sqscr[:, 0:P], OP.is_equal))
            VE.inc(v.tensor_copy(gT[:], gws[:]))
        ve_consts = VE.n

        st_m2 = e(nc.sbuf_tensor("st_m2", [P, TB], F32))

        def emit_ln_group(items, ld_waits=None, scalar_pre=None, vector_pre=None):
            cols = [c for c, _, _ in items]
            lo, hi = min(cols), max(cols) + 1
            csl = slice(lo, hi)

            @block.vector
            def _(v):
                if vector_pre is not None:
                    v.wait_ge(vector_pre[0].h, vector_pre[1])
                for i, (c, src_ap, _) in enumerate(items):
                    if ld_waits is not None and ld_waits[i] is not None:
                        v.wait_ge(ld_waits[i][0].h, ld_waits[i][1])
                    VE.inc(v.tensor_reduce(st_sum[:, c:c + 1], src_ap, AX.X, OP.add,
                                           negate=True))
            v_red = VE.n

            @block.scalar
            def _(s):
                if scalar_pre is not None:
                    s.wait_ge(scalar_pre[0].h, scalar_pre[1])
                for i, (c, src_ap, _) in enumerate(items):
                    if ld_waits is not None and ld_waits[i] is not None:
                        s.wait_ge(ld_waits[i][0].h, ld_waits[i][1])
                    if i > 0:
                        s.wait_ge(SE.h, SE.n)
                    SE.inc(s.activation(out=sqscr[:], in_=src_ap, func=AF.Square,
                                        accum_out=st_ssq[:, c:c + 1]))
            s_sq = SE.n

            @block.vector
            def _(v):
                v.wait_ge(VE.h, v_red)
                v.wait_ge(SE.h, s_sq)
                VE.inc(v.tensor_scalar_mul(st_sum[:, csl], st_sum[:, csl], INV_D))
                VE.inc(v.tensor_scalar_mul(st_tmp[:, csl], st_ssq[:, csl], INV_D))
            v_d1 = VE.n

            @block.vector
            def _(v):
                v.wait_ge(VE.h, v_d1)
                VE.inc(v.tensor_tensor(st_m2[:, csl], st_sum[:, csl], st_sum[:, csl],
                                       OP.mult))
            v_d2 = VE.n

            @block.vector
            def _(v):
                v.wait_ge(VE.h, v_d2)
                VE.inc(v.tensor_tensor(st_var[:, csl], st_tmp[:, csl], st_m2[:, csl],
                                       OP.subtract))
            v_d3 = VE.n

            @block.vector
            def _(v):
                v.wait_ge(VE.h, v_d3)
                VE.inc(v.tensor_scalar_add(st_var[:, csl], st_var[:, csl], EPS))
            v_d4 = VE.n

            @block.scalar
            def _(s):
                s.wait_ge(VE.h, v_d4)
                SE.inc(s.activation(out=st_tmp[:, csl], in_=st_var[:, csl], func=AF.Sqrt))
            s_std = SE.n

            @block.vector
            def _(v):
                v.wait_ge(SE.h, s_std)
                VE.inc(v.reciprocal(st_rstd[:, csl], st_tmp[:, csl]))
            v_rec = VE.n

            @block.vector
            def _(v):
                v.wait_ge(VE.h, v_rec)
                VE.inc(v.tensor_tensor(st_nmr[:, csl], st_sum[:, csl], st_rstd[:, csl],
                                       OP.mult))
            v_nmr = VE.n

            outs = []
            for c, src_ap, dst_ap in items:
                @block.scalar
                def _(s, c=c, src_ap=src_ap, dst_ap=dst_ap):
                    s.wait_ge(VE.h, v_nmr)
                    SE.inc(s.activation(out=dst_ap, in_=src_ap, func=AF.Identity,
                                        bias=st_nmr[:, c:c + 1],
                                        scale=st_rstd[:, c:c + 1]))
                outs.append(SE.n)
            return outs

        def emit_ln(src_ap, dst_ap, col, ld_wait=None, scalar_pre=None, vector_pre=None):
            return emit_ln_group([(col, src_ap, dst_ap)], ld_waits=[ld_wait],
                                 scalar_pre=scalar_pre, vector_pre=vector_pre)[0]

        arena_free = [(None, 0), (None, 0)]
        with ExitStack() as attn_scope:
            ea = attn_scope.enter_context
            Xn = ea(nc.sbuf_tensor("Xn", [P, TB, D], BF16))
            XnT = ea(nc.sbuf_tensor("XnT", [P, DC, TBM, P], BF16))
            xring = [ea(nc.sbuf_tensor(f"xring{i}", [P, D], F32)) for i in range(2)]
            Asb = [ea(nc.sbuf_tensor("Asb0", [P, D], BF16))]
            ATsl = [ea(nc.sbuf_tensor("ATsl0", [P, DC, 4, P], BF16))]

            s_norm = emit_ln_group(
                [(tb, acc[:, tb, :], Xn[:, tb, :]) for tb in range(TBM)],
                ld_waits=[ld_acc] * TBM,
                scalar_pre=(VE, ve_consts), vector_pre=None)
            ring_free = [(None, 0), (None, 0)]
            for tb in range(TBM, TB):
                slot = tb % 2
                @block.gpsimd
                def _(g, tb=tb, slot=slot, gate=tuple(ring_free[slot])):
                    sm, val = gate
                    if sm is not None:
                        g.wait_ge(sm.h, val)
                    LDx[slot].inc(g.dma_start(out=xring[slot][:], in_=x_t[:, tb, :]), 16)
                sn = emit_ln(xring[slot][:], Xn[:, tb, :], tb,
                             ld_wait=(LDx[slot], LDx[slot].n))
                s_norm.append(sn)
                ring_free[slot] = (SE, sn)
            s_ln1_all = SE.n
            tap("ln1", Xn[:], waits=[(SE, s_ln1_all)])

            if stage == "ln1":
                @block.gpsimd
                def _(g):
                    g.wait_ge(LDo.h, LDo.n)
                return nc

            for tb in range(TBM):
                @block.sync
                def _(sy, tb=tb):
                    sy.wait_ge(SE.h, s_norm[tb])
                    TP.inc(sy.dma_start(out=XnT[:, :, tb, :], in_=Xn[:, tb, :],
                                        transpose=True), 16)
                    sy.wait_ge(TP.h, TP.n)
            tp_xnt = TP.n
            tap("xnt", XnT[:], waits=[(TP, tp_xnt)])

            if stage == "xnt":
                @block.gpsimd
                def _(g):
                    g.wait_ge(LDo.h, LDo.n)
                return nc

            accw = {}
            asb_free = [(None, 0)]
            atsl_free = [(None, 0)]
            tp_at = {}
            for g_r in range(4):
                for ibl in range(4):
                    gi = 4 * g_r + ibl
                    par = gi % 2
                    ps = pAR[par]

                    @block.tensor
                    def _(t, gi=gi, ps=ps, gate=tuple(arena_free[par])):
                        if gi == 2:
                            t.wait_ge(SE.h, s_ln1_all)
                        sm, val = gate
                        if sm is not None:
                            t.wait_ge(sm.h, val)
                        for ct in range(TB):
                            if gi < 2:
                                t.wait_ge(SE.h, s_norm[ct])
                            for jr in range(4):
                                ins = t.matmul(ps[:, jr * 512:(jr + 1) * 512],
                                               Xn[:, ct, gi * P:(gi + 1) * P],
                                               Xn[:, ct, jr * 512:(jr + 1) * 512],
                                               start=(ct == 0), stop=(ct == TB - 1),
                                               skip_group_check=True)
                        PE.inc(ins)
                    pe_gram = PE.n

                    if stage == "gram0" and gi == 0:
                        @block.vector
                        def _(v, ps=ps, pe_gram=pe_gram):
                            v.wait_ge(PE.h, pe_gram)
                            VE.inc(v.tensor_copy(sqscr[:], ps[:]))
                        tap("G0", sqscr[:], waits=[(VE, VE.n)])

                        @block.gpsimd
                        def _(g):
                            g.wait_ge(LDo.h, LDo.n)
                        return nc

                    mxn = sm_mxn[:, par:par + 1]
                    rs = sm_rs[:, par:par + 1]
                    lnr = sm_lnr[:, par:par + 1]
                    b2 = sm_b2[:, par:par + 1]

                    @block.vector
                    def _(v, ps=ps, pe_gram=pe_gram, mxn=mxn):
                        v.wait_ge(PE.h, pe_gram)
                        VE.inc(v.tensor_reduce(mxn, ps[:], AX.X, OP.max, negate=True))
                    v_mxn = VE.n

                    @block.scalar
                    def _(s, gi=gi, ps=ps, v_mxn=v_mxn, mxn=mxn, rs=rs,
                          gate=tuple(asb_free[0])):
                        s.wait_ge(VE.h, v_mxn)
                        sm, val = gate
                        if sm is not None:
                            s.wait_ge(sm.h, val)
                        SE.inc(s.activation(out=Asb[0][:], in_=ps[:], func=AF.Exp,
                                            bias=mxn, accum_out=rs))
                    s_e1 = SE.n

                    @block.scalar
                    def _(s, s_e1=s_e1, rs=rs, lnr=lnr):
                        s.wait_ge(SE.h, s_e1)
                        SE.inc(s.activation(out=lnr, in_=rs, func=AF.Ln))
                    s_ln = SE.n

                    @block.vector
                    def _(v, s_ln=s_ln, mxn=mxn, lnr=lnr, b2=b2):
                        v.wait_ge(SE.h, s_ln)
                        VE.inc(v.tensor_tensor(b2, mxn, lnr, OP.subtract))
                    v_b2 = VE.n

                    @block.scalar
                    def _(s, gi=gi, ps=ps, v_b2=v_b2, b2=b2):
                        s.wait_ge(VE.h, v_b2)
                        SE.inc(s.activation(out=Asb[0][:], in_=ps[:], func=AF.Exp,
                                            bias=b2))
                    s_e2 = (SE, SE.n)
                    arena_free[par] = s_e2

                    if stage == "gram1" and gi == 0:
                        tap("A0", Asb[0][:], waits=[(SE, s_e2)])

                        @block.gpsimd
                        def _(g):
                            g.wait_ge(LDo.h, LDo.n)
                        return nc

                    @block.sync
                    def _(sy, g_r=g_r, ibl=ibl, gi=gi, s_e2=s_e2,
                          gate=tuple(atsl_free[0])):
                        sy.wait_ge(s_e2[0].h, s_e2[1])
                        sm, val = gate
                        if sm is not None and ibl == 0:
                            sy.wait_ge(sm.h, val)
                        TP.inc(sy.dma_start(out=ATsl[0][:, :, ibl, :],
                                            in_=Asb[0][:], transpose=True), 16)
                        sy.wait_ge(TP.h, TP.n)
                    asb_free[0] = (TP, TP.n)

                    if stage == "at1" and gi == 0:
                        tap("AT0", ATsl[0][:], waits=[(TP, TP.n)])

                        @block.gpsimd
                        def _(g):
                            g.wait_ge(LDo.h, LDo.n)
                        return nc
                tp_at[g_r] = TP.n

                for tb in range(TBM):
                    par = tb % 2
                    slot = pAR[par][:, 0:512]

                    @block.tensor
                    def _(t, g_r=g_r, tb=tb, slot=slot, gate=tuple(arena_free[par])):
                        t.wait_ge(TP.h, tp_at[g_r])
                        if g_r == 0 and tb == 0:
                            t.wait_ge(TP.h, tp_xnt)
                        sm, val = gate
                        if sm is not None:
                            t.wait_ge(sm.h, val)
                        for m in range(DC):
                            ins = t.matmul(slot, XnT[:, m, tb, :],
                                           flat2(ATsl[0][:, m, :, :]),
                                           start=(m == 0), stop=(m == DC - 1))
                        PE.inc(ins)
                    pe_out = PE.n

                    @block.vector
                    def _(v, g_r=g_r, tb=tb, slot=slot, pe_out=pe_out):
                        v.wait_ge(PE.h, pe_out)
                        a_sl = acc[:, tb, g_r * 512:(g_r + 1) * 512]
                        VE.inc(v.tensor_tensor(a_sl, a_sl, slot, OP.add))
                    arena_free[par] = (VE, VE.n)
                    accw[(g_r, tb)] = VE.n
                atsl_free[0] = (PE, PE.n)
            ve_x1 = VE.n
            tap("x1", acc[:], waits=[(VE, ve_x1)])


        if stage == "attn":
            @block.gpsimd
            def _(g):
                g.wait_ge(LDo.h, LDo.n)
            return nc
        with ExitStack() as ffn_scope:
            ef = ffn_scope.enter_context
            scores = ef(nc.sbuf_tensor("scores", [P, TBM, NE], F32))
            topv = ef(nc.sbuf_tensor("topv", [P, TBM, NE], F32))
            topi = ef(nc.sbuf_tensor("topi", [P, TBM, NE], U32))
            topf = ef(nc.sbuf_tensor("topf", [P, TBM, NE], F32))
            Ssc = ef(nc.sbuf_tensor("Ssc", [P, TBM, NE], F32))
            STb = ef(nc.sbuf_tensor("STb", [NE, TBM * P], BF16))
            W1T = ef(nc.sbuf_tensor("W1T", [P, DC, 2, 256], BF16))
            W3T = ef(nc.sbuf_tensor("W3T", [P, DC, 2, 256], BF16))
            W2T = ef(nc.sbuf_tensor("W2T", [P, UB, 2, 512], BF16))
            tmp1 = ef(nc.sbuf_tensor("tmp1", [P, 512], F32))
            tmp3 = ef(nc.sbuf_tensor("tmp3", [P, 512], F32))
            silb = ef(nc.sbuf_tensor("silb", [P, 512], F32))
            posu = ef(nc.sbuf_tensor("posu", [P, TBM, NE], U32))
            scpm = ef(nc.sbuf_tensor("scpm", [P, TBM, NE], F32))
            vals = ef(nc.sbuf_tensor("vals", [P, TBM], F32))
            pairb = ef(nc.sbuf_tensor("pairb", [P, 2, TBM, 2], F32))
            pref = ef(nc.sbuf_tensor("pref", [P, NCH, 2], F32))
            ones1 = ef(nc.sbuf_tensor("ones1", [1, P], F32))
            idxf = ef(nc.sbuf_tensor("idxf", [P, 4, NCH, 2], F32))
            idxu = ef(nc.sbuf_tensor("idxu", [P, 4, NCH], U32))
            gmin = ef(nc.sbuf_tensor("gmin", [P, 4, NCH], U32))
            e3t = ef(nc.sbuf_tensor("e3t", [P, TBM, NE], F32))
            sc_done = {}
            lists_ready = {}



            from contextlib import ExitStack as _ES
            xf_scope = _ES()
            cw_mask = xf_scope.enter_context(nc.sbuf_tensor("cw_mask", [NE, TBM * P], F32))
            cw_pos = xf_scope.enter_context(nc.sbuf_tensor("cw_pos", [NE, TBM * P], F32))
            cw_trash = xf_scope.enter_context(nc.sbuf_tensor("cw_trash", [NE, TBM * P], F32))
            cw_ones = xf_scope.enter_context(nc.sbuf_tensor("cw_ones", [NE, TBM * P], F32))
            hbuf = xf_scope.enter_context(nc.sbuf_tensor("hbuf", [P, UB, H], BF16))
            xfT = xf_scope.enter_context(nc.sbuf_tensor("xfT", [P, DC, TBM, P], BF16))
            xfring = xf_scope.enter_context(nc.sbuf_tensor("xfring", [P, 2, D], BF16))
            sp = {}



            def alloc_sparse():
                xf_scope.close()
                sp["sgrow"] = ffn_scope.enter_context(nc.sbuf_tensor("sgrow", [1, 4, CAP], F32))
                sp["Sg"] = ffn_scope.enter_context(nc.sbuf_tensor("Sg", [P, 2, CAP], F32))
                sp["xg"] = ffn_scope.enter_context(nc.sbuf_tensor("xg", [P, NCH, D], BF16))
                sp["xgT"] = ffn_scope.enter_context(nc.sbuf_tensor("xgT", [P, 2, NCH, DC, P], BF16))
                sp["hb"] = ffn_scope.enter_context(nc.sbuf_tensor("hb", [P, UB, CAP], BF16))
                sp["rgout"] = ffn_scope.enter_context(nc.sbuf_tensor("rgout", [P, NCH, D], BF16))

            tp_xf = []
            gate_prev = (None, 0)
            for grp in range(4):
                tbs = range(grp * 2, grp * 2 + 2)
                if grp > 0:
                    @block.scalar
                    def _(s, g0=LDf[0].n, g1=LDf[1].n):
                        s.wait_ge(LDf[0].h, g0)
                        s.wait_ge(LDf[1].h, g1)
                sns = emit_ln_group(
                    [(tb, acc[:, tb, :], xfring[:, tb % 2, :]) for tb in tbs],
                    scalar_pre=(VE, ve_x1) if grp == 0 else gate_prev,
                    vector_pre=(VE, ve_x1) if grp == 0 else None)
                for j, tb in enumerate(tbs):
                    @block.sync
                    def _(sy, tb=tb, sn=sns[j]):
                        sy.wait_ge(SE.h, sn)
                        TP.inc(sy.dma_start(out=xfT[:, :, tb, :],
                                            in_=xfring[:, tb % 2, :],
                                            transpose=True), 16)
                        sy.wait_ge(TP.h, TP.n)
                    tp_xf.append(TP.n)

                    @block.gpsimd
                    def _(g, tb=tb, sn=sns[j]):
                        g.wait_ge(SE.h, sn)
                        LDf[tb % 2].inc(g.dma_start(
                            out=xfd.rearrange("(tb p) d -> p tb d", p=P)[:, tb, :],
                            in_=xfring[:, tb % 2, :]), 16)
                gate_prev = (TP, TP.n)
            tp_xfT = TP.n
            ldf_all = (LDf[0].n, LDf[1].n)

            gate_arena0 = tuple(arena_free[0])
            gate_arena1 = tuple(arena_free[1])
            pe_gate = []
            s_gate_evac = []
            for tb in range(TBM):
                @block.tensor
                def _(t, tb=tb):
                    t.wait_ge(VE.h, ve_consts)
                    t.wait_ge(TP.h, tp_xf[tb])
                    if tb < 2:
                        sm, val = gate_arena1
                        if sm is not None:
                            t.wait_ge(sm.h, val)
                    if tb >= 2:
                        t.wait_ge(SE.h, s_gate_evac[tb - 2])
                    for m in range(DC):
                        ins = t.matmul(pAR[1][:, (tb % 2) * 512:(tb % 2) * 512 + NE],
                                       xfT[:, m, tb, :], gT[:, m, 0:NE],
                                       start=(m == 0), stop=(m == DC - 1))
                    PE.inc(ins)
                pe_gate.append(PE.n)

                @block.scalar
                def _(s, tb=tb):
                    s.wait_ge(PE.h, pe_gate[tb])
                    SE.inc(s.activation(out=scores[:, tb, :],
                                        in_=pAR[1][:, (tb % 2) * 512:(tb % 2) * 512 + NE],
                                        func=AF.Sigmoid))
                s_gate_evac.append(SE.n)
            s_scores = SE.n
            tap("scores", scores[:], waits=[(SE, s_scores)])

            if stage == "scores":
                @block.gpsimd
                def _(g):
                    g.wait_ge(LDo.h, LDo.n)
                return nc

            @block.vector
            def _(v):
                v.wait_ge(SE.h, s_scores)
                for tb in range(TBM):
                    VE.inc(v.max(topv[:, tb, :], scores[:, tb, :]))
            v_max = VE.n

            @block.vector
            def _(v):
                v.wait_ge(VE.h, v_max)
                for tb in range(TBM):
                    VE.inc(v.max_index(topi[:, tb, :], topv[:, tb, :], scores[:, tb, :]))
            v_mi = VE.n

            @block.vector
            def _(v):
                v.wait_ge(VE.h, v_mi)
                v.wait_ge(GP.h, 1)
                for tb in range(TBM):
                    VE.inc(v.tensor_copy(topf[:, tb, :], topi[:, tb, :]))
            v_tf = VE.n

            @block.vector
            def _(v):
                v.wait_ge(VE.h, v_tf)
                for tb in range(TBM):
                    VE.inc(v.tensor_scalar(Ssc[:, tb, :], iota8[:], topf[:, tb, 0:1],
                                           None, OP.is_equal))
                    VE.inc(v.tensor_scalar(e3t[:, tb, :], iota8[:], topf[:, tb, 1:2],
                                           None, OP.is_equal))
            v_eq = VE.n

            @block.vector
            def _(v):
                v.wait_ge(VE.h, v_eq)
                for tb in range(TBM):
                    VE.inc(v.tensor_scalar(Ssc[:, tb, :], Ssc[:, tb, :], topv[:, tb, 0:1],
                                           None, OP.mult))
            v_s1 = VE.n

            @block.vector
            def _(v):
                v.wait_ge(VE.h, v_s1)
                for tb in range(TBM):
                    VE.inc(v.scalar_tensor_tensor(Ssc[:, tb, :], e3t[:, tb, :],
                                                  topv[:, tb, 1:2], Ssc[:, tb, :],
                                                  OP.mult, OP.add))
            v_S = VE.n
            tap("S", Ssc[:], waits=[(VE, v_S)])

            st_state = {}

            def emit_ST():
                ve_st = []
                for tb in range(TBM):
                    @block.tensor
                    def _(t, tb=tb):
                        t.wait_ge(VE.h, v_S)
                        t.wait_ge(VE.h, ve_consts)
                        if tb == 0:
                            sm, val = gate_arena1
                            if sm is not None:
                                t.wait_ge(sm.h, val)
                        else:
                            t.wait_ge(VE.h, ve_st[tb - 1])
                        PE.inc(t.transpose(pAR[1][0:NE, 1024:1024 + P], Ssc[:, tb, :],
                                           identt[:]))
                    pe_st = PE.n

                    @block.vector
                    def _(v, tb=tb, pe_st=pe_st):
                        v.wait_ge(PE.h, pe_st)
                        VE.inc(v.tensor_copy(STb[0:NE, tb * P:(tb + 1) * P],
                                             pAR[1][0:NE, 1024:1024 + P]))
                    ve_st.append(VE.n)
                st_state["v_stb"] = VE.n

            def emit_compact():
                v_stb = st_state["v_stb"]

                @block.vector
                def _(v):
                    v.wait_ge(VE.h, v_stb)
                    v.wait_ge(GP.h, gp_sparse)
                    VE.inc(v.memset(cw_ones[:], 1.0))
                    VE.inc(v.tensor_single_scalar(cw_mask[:], STb[:], 0.0, OP.is_gt))
                c_m = VE.n

                @block.vector
                def _(v):
                    v.wait_ge(VE.h, c_m)
                    VE.inc(v.tensor_tensor_scan(cw_pos[:], cw_ones[:], cw_mask[:],
                                                0.0, OP.mult, OP.add))
                c_p1 = VE.n

                @block.vector
                def _(v):
                    v.wait_ge(VE.h, c_p1)
                    VE.inc(v.tensor_single_scalar(cw_pos[:], cw_pos[:], -1.0, OP.add))
                    VE.inc(v.tensor_copy(cw_ones[:], STb[:]))
                c_p2 = VE.n

                @block.vector
                def _(v):
                    v.wait_ge(VE.h, c_p2)
                    VE.inc(v.tensor_tensor(cw_pos[:], cw_pos[:], cw_trash[:], OP.subtract))
                c_p3 = VE.n

                @block.vector
                def _(v):
                    v.wait_ge(VE.h, c_p3)
                    VE.inc(v.tensor_tensor(cw_pos[:], cw_mask[:], cw_pos[:], OP.mult))
                c_p4 = VE.n

                @block.vector
                def _(v):
                    v.wait_ge(VE.h, c_p4)
                    VE.inc(v.tensor_tensor(cw_pos[:], cw_pos[:], cw_trash[:], OP.add))
                c_pos = VE.n

                prev_ev = [0]
                for tb in range(TBM):
                    @block.tensor
                    def _(t, tb=tb):
                        t.wait_ge(VE.h, c_pos)
                        if tb > 0:
                            t.wait_ge(VE.h, prev_ev[0])
                        t.transpose(pAR[1][:, 1024:1024 + NE],
                                    cw_pos[0:NE, tb * P:(tb + 1) * P], identt[0:NE, 0:NE])
                        PE.inc(t.transpose(pAR[1][:, 1536:1536 + NE],
                                           cw_ones[0:NE, tb * P:(tb + 1) * P],
                                           identt[0:NE, 0:NE]))
                    pe_t = PE.n

                    @block.vector
                    def _(v, tb=tb, pe_t=pe_t):
                        v.wait_ge(PE.h, pe_t)
                        VE.inc(v.tensor_copy(posu[:, tb, :], pAR[1][:, 1024:1024 + NE]))
                        VE.inc(v.tensor_copy(scpm[:, tb, :], pAR[1][:, 1536:1536 + NE]))
                    prev_ev[0] = VE.n
                c_tp = VE.n
                st_state["c_tp"] = c_tp

                for ex in range(NE):
                    slot = ex % 2

                    @block.gpsimd
                    def _(g, ex=ex):
                        g.wait_ge(GP.h, gp_sparse)
                        LDz.inc(g.dma_start(
                            out=plist[ex, 0:CAP].rearrange("(c p) w -> p c w", p=P),
                            in_=pref[:]), 16)
                    ld_pf = LDz.n

                    @block.vector
                    def _(v, ex=ex, slot=slot, gate=GDc[slot].n):
                        v.wait_ge(VE.h, c_tp)
                        if gate:
                            v.wait_ge(GDc[slot].h, gate)
                        VE.inc(v.tensor_copy(
                            pairb[:, slot, :, 0:1],
                            vals[:].rearrange("p (c one) -> p c one", one=1)))
                        VE.inc(v.tensor_copy(pairb[:, slot, :, 1:2],
                                             scpm[:, :, ex:ex + 1]))
                    v_pair = VE.n

                    @block.gpsimd
                    def _(g, ex=ex, slot=slot, v_pair=v_pair, ld_pf=ld_pf):
                        g.wait_ge(VE.h, v_pair)
                        g.wait_ge(LDz.h, ld_pf)
                        for tb in range(TBM):
                            GDc[slot].inc(g.indirect_dma_start(
                                out=plist.rearrange("E c w -> (E c) w"),
                                out_offset=IndirectOffsetOnAxis(posu[:, tb, ex:ex + 1], 0),
                                in_=pairb[:, slot, tb, :],
                                in_offset=None,
                                element_offset=ex * (CAP + TBM * P) * 2), 16)
                    sc_done[ex] = (GDc[slot], GDc[slot].n)

            gather_done = set()
            gather_tp = {}

            def emit_gather_chain(ex):
                gather_done.add(ex)
                slot = ex % 4
                buf = ex % 2
                lr_ve, _sl = lists_ready[ex]
                for c in range(NCH):
                    @block.gpsimd
                    def _(g, ex=ex, c=c, slot=slot, lr_ve=lr_ve,
                          gate=tuple(xg_free[0]), mg=tuple(mem_gate)):
                        g.wait_ge(VE.h, lr_ve)
                        if c == 0:
                            g.wait_ge(LDf[0].h, ldf_all[0])
                            g.wait_ge(LDf[1].h, ldf_all[1])
                            sm, val = gate
                            if sm is not None:
                                g.wait_ge(sm.h, val)
                            sm2, val2 = mg
                            if sm2 is not None:
                                g.wait_ge(sm2.h, val2)
                        LDxg[c].inc(g.indirect_dma_start(
                            out=sp["xg"][:, c, :],
                            out_offset=None,
                            in_=xfd,
                            in_offset=IndirectOffsetOnAxis(gmin[:, slot, c:c + 1], 0)), 16)
                    ld_gs = [LDxg[i].n for i in range(NCH)]

                @block.sync
                def _(sy, buf=buf, ld_gs=tuple(ld_gs), xfree=tuple(xgt_free[buf]),
                      mg=tuple(mem_gate)):
                    for i in range(NCH):
                        sy.wait_ge(LDxg[i].h, ld_gs[i])
                    sm, val = xfree
                    if sm is not None:
                        sy.wait_ge(sm.h, val)
                    sm2, val2 = mg
                    if sm2 is not None:
                        sy.wait_ge(sm2.h, val2)
                    TP.inc(sy.dma_start(out=sp["xgT"][:, buf, :, :, :],
                                        in_=sp["xg"][:].rearrange("p c d -> p (c d)"),
                                        transpose=True), 16)
                    sy.wait_ge(TP.h, TP.n)
                gather_tp[ex] = TP.n
                xg_free[0] = (TP, TP.n)

            def emit_reload(ex, gate):
                slot = ex % 4

                @block.gpsimd
                def _(g, ex=ex, slot=slot, sc=sc_done[ex], gate=gate):
                    g.wait_ge(sc[0].h, sc[1])
                    if gate[0] is not None:
                        g.wait_ge(gate[0].h, gate[1])
                    LDl[slot].inc(g.dma_start(
                        out=idxf[:, slot, :, :],
                        in_=plist[ex, 0:CAP].rearrange("(c p) w -> p c w", p=P)), 16)
                    LDl[slot].inc(g.dma_start(
                        out=sp["sgrow"][0:1, slot, :],
                        in_=plist[ex, 0:CAP, 1:2].rearrange("c w -> (w c)")[None, :],
                        allow_slow_non_contiguous=True), 16)
                ld_l = LDl[slot].n

                @block.vector
                def _(v, slot=slot, ld_l=ld_l):
                    v.wait_ge(LDl[slot].h, ld_l)
                    VE.inc(v.tensor_copy(idxu[:, slot, :], idxf[:, slot, :, 0]))
                v_iu = VE.n

                @block.vector
                def _(v, slot=slot, v_iu=v_iu):
                    v.wait_ge(VE.h, v_iu)
                    VE.inc(v.tensor_single_scalar(gmin[:, slot, :], idxu[:, slot, :],
                                                  TBM * P - 1, OP.min))
                lists_ready[ex] = (VE.n, slot)

            w1t_r = w1t.rearrange("E (m p) u -> E p m u", p=P)
            w3t_r = w3t.rearrange("E (m p) u -> E p m u", p=P)
            w2t_r = w2t.rearrange("E (mu p) d -> E p mu d", p=P)
            ws1t_r = ws1t.rearrange("(m p) u -> p m u", p=P)
            ws3t_r = ws3t.rearrange("(m p) u -> p m u", p=P)
            ws2t_r = ws2t.rearrange("(mu p) d -> p mu d", p=P)

            w1_free = [(VE, ve_x1), (VE, ve_x1)]
            w3_free = [(VE, ve_x1), (VE, ve_x1)]
            w2_free = [(VE, ve_x1), (VE, ve_x1)]
            w_ready = {}
            w_emitted = set()

            def load_w(ex, kind, idx, gate_override=None):
                if ex >= NEXP or (ex, kind, idx) in w_emitted:
                    return
                w_emitted.add((ex, kind, idx))
                slot = idx % 2
                if kind == 1:
                    srcs = w1t_r[ex] if ex < NE else ws1t_r
                    srcap = srcs[:, :, idx * 256:(idx + 1) * 256]
                    dst = W1T[:, :, slot, :]
                    sem, gate = LDw1[slot], tuple(w1_free[slot])
                elif kind == 3:
                    srcs = w3t_r[ex] if ex < NE else ws3t_r
                    srcap = srcs[:, :, idx * 256:(idx + 1) * 256]
                    dst = W3T[:, :, slot, :]
                    sem, gate = LDw3[slot], tuple(w3_free[slot])
                else:
                    srcs = w2t_r[ex] if ex < NE else ws2t_r
                    srcap = srcs[:, :, idx * 512:(idx + 1) * 512]
                    dst = W2T[:, :, slot, :]
                    sem, gate = LDw2[slot], tuple(w2_free[slot])
                if gate_override is not None:
                    gate = tuple(gate_override)

                @block.gpsimd
                def _(g, srcap=srcap, dst=dst, sem=sem, gate=gate):
                    sm, val = gate
                    if sm is not None:
                        g.wait_ge(sm.h, val)
                    sem.inc(g.dma_start(out=dst, in_=srcap), 16)
                w_ready[(ex, kind, idx)] = (sem, sem.n)

            arena_free_z = [tuple(arena_free[0]), (None, 0)]
            last_sil = [0]
            last_hmul = [0]
            slotfree_r = [(None, 0), (None, 0)]
            h_readers_done = [0]

            ORDER = [NE] + list(range(NE))
            rscat = {}
            sg_free = (None, 0)
            xgt_free = [(None, 0), (None, 0)]
            xg_free = [(None, 0)]
            rg_free = (None, 0)
            mem_gate = (None, 0)
            reload_done = set()
            pending_scatter = [None]
            pend_w2 = []
            ve_acc_done = 0

            for ei, ex in enumerate(ORDER):
                nxt = ORDER[ei + 1] if ei + 1 < len(ORDER) else NEXP
                is_shared = (ex == NE)
                for q in range(2):
                    load_w(ex, 1, q)
                    load_w(ex, 3, q)
                for kg in range(2):
                    load_w(ex, 2, kg)

                if ei == 0:
                    @block.gpsimd
                    def _(g):
                        g.wait_ge(VE.h, ve_x1)
                        GP.inc(g.iota(vals[:], pattern=[[P, TBM]], base=0, channel_multiplier=1,
                                      allow_small_or_imprecise_dtypes=True))
                        GP.inc(g.iota(pref[:, :, 0:1], pattern=[[P, NCH], [0, 1]], base=TBM * P,
                                      channel_multiplier=1, allow_small_or_imprecise_dtypes=True))
                        GP.inc(g.iota(cw_trash[:], pattern=[[1, TBM * P]], base=CAP,
                                      channel_multiplier=0, allow_small_or_imprecise_dtypes=True))
                    gp_sparse = GP.n

                    @block.vector
                    def _(v):
                        v.wait_ge(GP.h, gp_sparse)
                        v.wait_ge(VE.h, ve_x1)
                        VE.inc(v.memset(pref[:, :, 1:2], 0.0))
                        VE.inc(v.memset(ones1[:], 1.0))

                    ve_zero_src = VE.n

                    ldz_zero = LDz.n

                if is_shared:
                    pair_free = [tuple(arena_free_z[0]), tuple(arena_free_z[0])]
                    for step in range(2 * UB):
                        ub, th = step // 2, step % 2
                        q = ub // 2
                        uo = (ub % 2) * P
                        o = (step % 2) * 1024
                        z1 = pAR[0][:, o:o + 512]
                        z3 = pAR[0][:, o + 512:o + 1024]
                        h_sl = hbuf[:, ub, th * 512:(th + 1) * 512]

                        @block.tensor
                        def _(t, ex=ex, q=q, uo=uo, th=th, z1=z1, z3=z3,
                              gate=tuple(pair_free[step % 2])):
                            r1 = w_ready[(ex, 1, q)]
                            r3 = w_ready[(ex, 3, q)]
                            t.wait_ge(r1[0].h, r1[1])
                            t.wait_ge(r3[0].h, r3[1])
                            sm, val = gate
                            if sm is not None:
                                t.wait_ge(sm.h, val)
                            for m in range(DC):
                                l1 = W1T[:, m, q % 2, uo:uo + P]
                                l3 = W3T[:, m, q % 2, uo:uo + P]
                                rhs = flat2(xfT[:, m, th * 4:(th + 1) * 4, :])
                                t.matmul(z1, l1, rhs, start=(m == 0),
                                         stop=(m == DC - 1), skip_group_check=True)
                                ins = t.matmul(z3, l3, rhs, start=(m == 0),
                                               stop=(m == DC - 1), skip_group_check=True)
                            PE.inc(ins)
                        pe_z = PE.n
                        if step == 7:
                            emit_ST()
                            emit_compact()
                        if ub % 2 == 1 and th == 1:
                            w1_free[q % 2] = (PE, pe_z)
                            w3_free[q % 2] = (PE, pe_z)
                            if q + 2 <= 3:
                                load_w(ex, 1, q + 2)
                                load_w(ex, 3, q + 2)
                            else:
                                load_w(nxt, 1, q - 2)
                                load_w(nxt, 3, q - 2)

                        @block.scalar
                        def _(s, pe_z=pe_z, z1=z1, lh=last_hmul[0]):
                            s.wait_ge(PE.h, pe_z)
                            if lh:
                                s.wait_ge(VE.h, lh)
                            SE.inc(s.activation(out=silb[:], in_=z1, func=AF.Silu))
                        s_sil = SE.n
                        last_sil[0] = s_sil

                        @block.vector
                        def _(v, z3=z3, h_sl=h_sl, s_sil=s_sil):
                            v.wait_ge(SE.h, s_sil)
                            VE.inc(v.tensor_tensor(h_sl, silb[:], z3, OP.mult))
                        last_hmul[0] = VE.n
                        pair_free[step % 2] = (VE, VE.n)
                    arena_free_z[0] = (VE, VE.n)
                    arena_free_z[1] = (VE, st_state["c_tp"])
                    ve_h_all = VE.n
                    pe_shared_z = PE.n

                    for kg in range(4):
                        for tb in range(TBM):
                            par = tb % 2
                            slot = pAR[par][:, 0:512]

                            @block.tensor
                            def _(t, ex=ex, kg=kg, tb=tb, slot=slot,
                                  gate=tuple(slotfree_r[par]),
                                  zgate=tuple(arena_free_z[par]), ve_h_all=ve_h_all):
                                r2 = w_ready[(ex, 2, kg)]
                                t.wait_ge(r2[0].h, r2[1])
                                t.wait_ge(VE.h, ve_h_all)
                                sm, val = gate
                                if sm is not None:
                                    t.wait_ge(sm.h, val)
                                if kg == 0 and tb < 2:
                                    sm2, val2 = zgate
                                    if sm2 is not None:
                                        t.wait_ge(sm2.h, val2)
                                for uc in range(UB):
                                    ins = t.matmul(slot,
                                                   hbuf[:, uc, tb * P:(tb + 1) * P],
                                                   W2T[:, uc, kg % 2, :],
                                                   start=(uc == 0), stop=(uc == UB - 1))
                                PE.inc(ins)
                            pe_r = PE.n
                            if tb == TBM - 1:
                                w2_free[kg % 2] = (PE, pe_r)
                                if kg + 2 <= 3:
                                    load_w(ex, 2, kg + 2)
                                else:
                                    load_w(nxt, 2, kg - 2)

                            @block.vector
                            def _(v, kg=kg, tb=tb, slot=slot, pe_r=pe_r,
                                  aw=accw[(kg, tb)]):
                                v.wait_ge(PE.h, pe_r)
                                v.wait_ge(VE.h, aw)
                                a_sl = acc[:, tb, kg * 512:(kg + 1) * 512]
                                VE.inc(v.tensor_tensor(a_sl, a_sl, slot, OP.add))
                            slotfree_r[par] = (VE, VE.n)
                            accw[(kg, tb)] = VE.n
                    h_readers_done[0] = PE.n
                    arena_free_z = [(VE, VE.n), (VE, VE.n)]
                    ve_acc_done = VE.n

                    mem_gate = (PE, PE.n)
                    alloc_sparse()
                    for _re in range(2):
                        emit_reload(_re, mem_gate)
                        reload_done.add(_re)
                    emit_gather_chain(0)

                    @block.gpsimd
                    def _(g):
                        g.wait_ge(VE.h, ve_acc_done)
                        LDz.inc(g.dma_start(out=out_t[:, 0:TBM, :], in_=acc[:]), 16)
                    ldz_acc = LDz.n
                    rg_free = (LDz, ldz_acc)
                    continue

                slot = ex % 4
                sgslot = ex % 2
                if ex not in reload_done:
                    emit_reload(ex, (GDr, rscat.get(ex - 4, 0)))
                    reload_done.add(ex)
                if ex + 2 < NE and (ex + 2) not in reload_done:
                    emit_reload(ex + 2, mem_gate if ex + 2 < 4 else
                                (GDr, rscat.get(ex - 2, 0)))
                    reload_done.add(ex + 2)
                lr_ve, _sl = lists_ready[ex]
                buf = ex % 2

                if pending_scatter[0] is not None:
                    pending_scatter[0]()
                    pending_scatter[0] = None
                if ex not in gather_done:
                    emit_gather_chain(ex)
                tp_xgt_all = gather_tp[ex]

                ve_sg_box = [0]
                for ub in range(UB):
                    par = ub % 2
                    ps = pAR[par]
                    q = ub // 2
                    uo = (ub % 2) * P
                    z1 = ps[:, 0:CAP]
                    z3 = ps[:, 512:512 + CAP]
                    h_sl = sp["hb"][:, ub, :]

                    @block.tensor
                    def _(t, ex=ex, ub=ub, q=q, uo=uo, ps=ps, z1=z1, z3=z3, buf=buf,
                          gate=tuple(arena_free_z[par]), tp_xgt_all=tp_xgt_all):
                        r1 = w_ready[(ex, 1, q)]
                        r3 = w_ready[(ex, 3, q)]
                        t.wait_ge(r1[0].h, r1[1])
                        t.wait_ge(r3[0].h, r3[1])
                        t.wait_ge(TP.h, tp_xgt_all)
                        sm, val = gate
                        if sm is not None:
                            t.wait_ge(sm.h, val)
                        for m in range(DC):
                            l1 = W1T[:, m, q % 2, uo:uo + P]
                            l3 = W3T[:, m, q % 2, uo:uo + P]
                            rhs = sp["xgT"][:, buf, :, m, :]
                            t.matmul(z1, l1, rhs, start=(m == 0),
                                     stop=(m == DC - 1), skip_group_check=True)
                            ins = t.matmul(z3, l3, rhs, start=(m == 0),
                                           stop=(m == DC - 1), skip_group_check=True)
                        PE.inc(ins)
                    pe_z = PE.n
                    if ub == 0:
                        @block.tensor
                        def _(t, ex=ex, slot=slot, gate=tuple(sg_free)):
                            t.wait_ge(LDl[slot].h, LDl[slot].n)
                            sm, val = gate
                            if sm is not None:
                                t.wait_ge(sm.h, val)
                            PE.inc(t.matmul(pAR[0][:, 1536:1536 + CAP], ones1[0:1, :],
                                            sp["sgrow"][0:1, slot, :],
                                            start=True, stop=True))
                        pe_sg = PE.n

                        @block.vector
                        def _(v, sgslot=sgslot, pe_sg=pe_sg):
                            v.wait_ge(PE.h, pe_sg)
                            VE.inc(v.tensor_copy(sp["Sg"][:, sgslot, :],
                                                 pAR[0][:, 1536:1536 + CAP]))
                        sg_free = (VE, VE.n)
                        ve_sg_box[0] = VE.n
                    if ub % 2 == 1:
                        w1_free[q % 2] = (PE, pe_z)
                        w3_free[q % 2] = (PE, pe_z)
                        if q + 2 <= 3:
                            load_w(ex, 1, q + 2)
                            load_w(ex, 3, q + 2)
                        else:
                            load_w(nxt, 1, q - 2)
                            load_w(nxt, 3, q - 2)
                    if ub == UB - 1:
                        xgt_free[buf] = (PE, pe_z)

                    @block.vector
                    def _(v, pe_z=pe_z, z1=z1, sgslot=sgslot, ls=last_sil[0],
                          ve_sg=ve_sg_box[0]):
                        v.wait_ge(PE.h, pe_z)
                        v.wait_ge(VE.h, ve_sg)
                        if ls:
                            v.wait_ge(SE.h, ls)
                        VE.inc(v.tensor_tensor(tmp1[:, 0:CAP], z1, sp["Sg"][:, sgslot, :],
                                               OP.mult))
                    v_t1 = VE.n

                    @block.scalar
                    def _(s, v_t1=v_t1):
                        s.wait_ge(VE.h, v_t1)
                        SE.inc(s.activation(out=silb[:, 0:CAP], in_=tmp1[:, 0:CAP],
                                            func=AF.Silu))
                    s_sil = SE.n
                    last_sil[0] = s_sil

                    @block.vector
                    def _(v, z3=z3, h_sl=h_sl, sgslot=sgslot, s_sil=s_sil,
                          hrd=h_readers_done[0], lh=last_hmul[0]):
                        if lh:
                            v.wait_ge(VE.h, lh)
                        v_t3 = VE.inc(v.tensor_tensor(tmp3[:, 0:CAP], z3,
                                                      sp["Sg"][:, sgslot, :], OP.mult))
                        v.wait_ge(SE.h, s_sil)
                        v.wait_ge(PE.h, hrd)
                        v.wait_ge(VE.h, v_t3)
                        VE.inc(v.tensor_tensor(h_sl, silb[:, 0:CAP], tmp3[:, 0:CAP],
                                               OP.mult))
                    last_hmul[0] = VE.n
                    arena_free_z[par] = (VE, VE.n)
                ve_h_all = VE.n



                for kg in range(4):
                    for tg in range(NCH):
                        par = (kg * NCH + tg) % 2
                        rps = pAR[par][:, 1024:1536]

                        @block.tensor
                        def _(t, ex=ex, kg=kg, tg=tg, rps=rps,
                              gate=tuple(slotfree_r[par]), ve_h_all=ve_h_all):
                            r2 = w_ready[(ex, 2, kg)]
                            t.wait_ge(r2[0].h, r2[1])
                            t.wait_ge(VE.h, ve_h_all)
                            sm, val = gate
                            if sm is not None:
                                t.wait_ge(sm.h, val)
                            for uc in range(UB):
                                ins = t.matmul(rps, sp["hb"][:, uc, tg * P:(tg + 1) * P],
                                               W2T[:, uc, kg % 2, :],
                                               start=(uc == 0), stop=(uc == UB - 1))
                            PE.inc(ins)
                        pe_r = PE.n
                        if tg == NCH - 1:
                            w2_free[kg % 2] = (PE, pe_r)
                            if kg + 2 <= 3:
                                load_w(ex, 2, kg + 2)
                            else:
                                pend_w2.append((nxt, kg - 2, (PE, pe_r)))

                        @block.vector
                        def _(v, kg=kg, tg=tg, rps=rps, pe_r=pe_r,
                              rgate=tuple(rg_free)):
                            v.wait_ge(PE.h, pe_r)
                            if kg == 0:
                                sm, val = rgate
                                if sm is not None:
                                    v.wait_ge(sm.h, val)
                            VE.inc(v.tensor_copy(sp["rgout"][:, tg, kg * 512:(kg + 1) * 512],
                                                 rps))
                        slotfree_r[par] = (VE, VE.n)
                ve_rg_all = VE.n

                def mk_scatter(ex=ex, slot=slot, ve_rg_all=ve_rg_all,
                               prev=rscat.get(ex - 1, 0), rgate=tuple(rg_free)):
                    def emit():
                        @block.gpsimd
                        def _(g):
                            g.wait_ge(VE.h, ve_rg_all)
                            sm, val = rgate
                            if sm is not None:
                                g.wait_ge(sm.h, val)
                            if prev:
                                g.wait_ge(GDr.h, prev)
                            for tg in range(NCH):
                                g.indirect_dma_start(
                                    out=out,
                                    out_offset=IndirectOffsetOnAxis(
                                        idxu[:, slot, tg:tg + 1], 0),
                                    in_=sp["rgout"][:, tg, :],
                                    in_offset=None,
                                    compute_op=OP.add).then_inc(GDr.h, 16)
                    return emit
                pending_scatter[0] = mk_scatter()
                GDr.n += NCH * 16
                rscat[ex] = GDr.n
                rg_free = (GDr, GDr.n)
                for tex, tkg, tgate in pend_w2:
                    load_w(tex, 2, tkg, gate_override=tgate)
                pend_w2 = []
                h_readers_done[0] = PE.n

            if pending_scatter[0] is not None:
                pending_scatter[0]()
                pending_scatter[0] = None
            ve_final = VE.n

            @block.gpsimd
            def _(g):
                g.wait_ge(GDr.h, GDr.n)
                g.wait_ge(LDz.h, LDz.n)
                g.wait_ge(LDo.h, LDo.n)

    return nc


def _host_inputs(L0_attn_x, L0_ffn_gate_w, L0_ffn_w1, L0_ffn_w2, L0_ffn_w3,
                 L0_ffn_ws1, L0_ffn_ws2, L0_ffn_ws3):
    xf = np.ascontiguousarray(np.asarray(L0_attn_x, dtype=np.float32))
    gw = np.ascontiguousarray(np.asarray(L0_ffn_gate_w, dtype=np.float32))
    w1 = np.ascontiguousarray(np.asarray(L0_ffn_w1, dtype=np.float32))
    w2 = np.ascontiguousarray(np.asarray(L0_ffn_w2, dtype=np.float32))
    w3 = np.ascontiguousarray(np.asarray(L0_ffn_w3, dtype=np.float32))
    ws1 = np.ascontiguousarray(np.asarray(L0_ffn_ws1, dtype=np.float32))
    ws2 = np.ascontiguousarray(np.asarray(L0_ffn_ws2, dtype=np.float32))
    ws3 = np.ascontiguousarray(np.asarray(L0_ffn_ws3, dtype=np.float32))
    gwT = np.ascontiguousarray(gw.T)
    w1t = np.ascontiguousarray(w1.transpose(0, 2, 1))
    w2t = np.ascontiguousarray(w2.transpose(0, 2, 1))
    w3t = np.ascontiguousarray(w3.transpose(0, 2, 1))
    ws1t = np.ascontiguousarray(ws1.T)
    ws2t = np.ascontiguousarray(ws2.T)
    ws3t = np.ascontiguousarray(ws3.T)
    in_maps = []
    for c in range(8):
        b, h = c // 2, c % 2
        mine = xf[b, h * 1024:(h + 1) * 1024]
        other = xf[b, (1 - h) * 1024:(2 - h) * 1024]
        xc = np.ascontiguousarray(np.concatenate([mine, other], axis=0))
        in_maps.append(dict(x=xc, gwT=gwT, w1t=w1t, w2t=w2t, w3t=w3t,
                            ws1t=ws1t, ws2t=ws2t, ws3t=ws3t))
    return in_maps


def kernel(L0_attn_x, L0_ffn_gate_w, L0_ffn_w1, L0_ffn_w2, L0_ffn_w3,
           L0_ffn_ws1, L0_ffn_ws2, L0_ffn_ws3):
    in_maps = _host_inputs(L0_attn_x, L0_ffn_gate_w, L0_ffn_w1, L0_ffn_w2,
                           L0_ffn_w3, L0_ffn_ws1, L0_ffn_ws2, L0_ffn_ws3)
    nc = build()
    res = run_bass_kernel_spmd(nc, in_maps, core_ids=list(range(8)))
    out = np.empty((4, 2048, 2048), dtype=np.float32)
    for c in range(8):
        b, h = c // 2, c % 2
        out[b, h * 1024:(h + 1) * 1024] = res.results[c]["out"][0:1024]
    return out


# version 23
# speedup vs baseline: 1.0461x; 1.0354x over previous
import numpy as np
from contextlib import ExitStack

import concourse.bass as bass
import concourse.mybir as mybir
from concourse.bass_utils import run_bass_kernel_spmd
from concourse.alu_op_type import AluOpType as OP
from concourse.bass import IndirectOffsetOnAxis

F32 = mybir.dt.float32
BF16 = mybir.dt.bfloat16
U32 = mybir.dt.uint32
AF = mybir.ActivationFunctionType
AX = mybir.AxisListType

P = 128
D = 2048
SEQ = 2048
TB = 16
TBM = 8
DC = 16
H = 1024
UB = 8
NE = 8
NEXP = 9
EPS = 1e-5
INV_D = 1.0 / D

CAP = 384
NCH = CAP // P
W1WIN = 4
W2WIN = 4


class Sem:
    def __init__(self, h):
        self.h = h
        self.n = 0

    def inc(self, instr, k=1):
        instr.then_inc(self.h, k)
        self.n += k
        return self.n


def flat2(ap):
    return ap.rearrange("p a b -> p (a b)")


def build(want_taps=(), stage="full"):
    nc = bass.Bass("TRN2", target_bir_lowering=False, debug=False)

    x = nc.dram_tensor("x", [SEQ, D], F32, kind="ExternalInput").ap()
    gwT = nc.dram_tensor("gwT", [D, NE], F32, kind="ExternalInput").ap()
    w1t = nc.dram_tensor("w1t", [NE, D, H], F32, kind="ExternalInput").ap()
    w2t = nc.dram_tensor("w2t", [NE, H, D], F32, kind="ExternalInput").ap()
    w3t = nc.dram_tensor("w3t", [NE, D, H], F32, kind="ExternalInput").ap()
    ws1t = nc.dram_tensor("ws1t", [D, H], F32, kind="ExternalInput").ap()
    ws2t = nc.dram_tensor("ws2t", [H, D], F32, kind="ExternalInput").ap()
    ws3t = nc.dram_tensor("ws3t", [D, H], F32, kind="ExternalInput").ap()
    out = nc.dram_tensor("out", [TBM * P + CAP + P, D], F32, kind="ExternalOutput").ap()
    plist = nc.dram_tensor("plist", [NE, CAP + TBM * P, 2], F32, kind="Internal").ap()
    xfd = nc.dram_tensor("xfd", [TBM * P, D], BF16, kind="Internal").ap()

    x_t = x.rearrange("(tb p) d -> p tb d", p=P)
    out_t = out.rearrange("(tb p) d -> p tb d", p=P)

    with ExitStack() as top:
        e = top.enter_context
        acc = e(nc.sbuf_tensor("acc", [P, TBM, D], F32))
        identt = e(nc.sbuf_tensor("identt", [P, P], F32))
        iota8 = e(nc.sbuf_tensor("iota8", [P, NE], F32))
        gT = e(nc.sbuf_tensor("gT", [P, DC, NE], BF16))
        gws = e(nc.sbuf_tensor("gws", [P, DC, NE], F32))
        st_sum = e(nc.sbuf_tensor("st_sum", [P, TB], F32))
        st_ssq = e(nc.sbuf_tensor("st_ssq", [P, TB], F32))
        st_var = e(nc.sbuf_tensor("st_var", [P, TB], F32))
        st_rstd = e(nc.sbuf_tensor("st_rstd", [P, TB], F32))
        st_nmr = e(nc.sbuf_tensor("st_nmr", [P, TB], F32))
        st_tmp = e(nc.sbuf_tensor("st_tmp", [P, TB], F32))
        sqscr = e(nc.sbuf_tensor("sqscr", [P, D], F32))
        sm_mxn = e(nc.sbuf_tensor("sm_mxn", [P, 2], F32))
        sm_rs = e(nc.sbuf_tensor("sm_rs", [P, 2], F32))
        sm_lnr = e(nc.sbuf_tensor("sm_lnr", [P, 2], F32))
        sm_b2 = e(nc.sbuf_tensor("sm_b2", [P, 2], F32))

        pA = e(nc.psum_tensor("pA", [P, 512 * 4], F32))
        pB = e(nc.psum_tensor("pB", [P, 512 * 4], F32))
        pAR = [pA, pB]

        LDa = Sem(e(nc.semaphore("LDa")))
        LDx = [Sem(e(nc.semaphore(f"LDx{i}"))) for i in range(2)]
        LDg = Sem(e(nc.semaphore("LDg")))
        LDw1 = [Sem(e(nc.semaphore(f"LDw1_{i}"))) for i in range(2)]
        LDw3 = [Sem(e(nc.semaphore(f"LDw3_{i}"))) for i in range(2)]
        LDw2 = [Sem(e(nc.semaphore(f"LDw2_{i}"))) for i in range(2)]
        LDo = Sem(e(nc.semaphore("LDo")))
        LDf = [Sem(e(nc.semaphore(f"LDf{i}"))) for i in range(2)]
        GDc = [Sem(e(nc.semaphore(f"GDc{i}"))) for i in range(2)]
        LDl = [Sem(e(nc.semaphore(f"LDl{i}"))) for i in range(4)]
        LDxg = [Sem(e(nc.semaphore(f"LDxg{i}"))) for i in range(NCH)]
        GDr = Sem(e(nc.semaphore("GDr")))
        LDz = Sem(e(nc.semaphore("LDz")))
        GP = Sem(e(nc.semaphore("GP")))
        TP = Sem(e(nc.semaphore("TP")))
        PE = Sem(e(nc.semaphore("PE")))
        VE = Sem(e(nc.semaphore("VE")))
        SE = Sem(e(nc.semaphore("SE")))

        block = e(nc.Block())

        def tap(name, ap, waits=()):
            if name not in want_taps:
                return
            t = nc.dram_tensor("tap_" + name, list(ap.shape), ap.dtype,
                               kind="ExternalOutput").ap()

            @block.gpsimd
            def _(g, ap=ap, t=t, waits=tuple(waits)):
                for s, v in waits:
                    g.wait_ge(s.h, v)
                LDo.inc(g.dma_start(out=t, in_=ap), 16)

        @block.gpsimd
        def _(g):
            GP.inc(g.iota(iota8[:], pattern=[[1, NE]], base=0, channel_multiplier=0,
                          allow_small_or_imprecise_dtypes=True), 1)
            GP.inc(g.iota(identt[:], pattern=[[0, P]], base=0, channel_multiplier=1,
                          allow_small_or_imprecise_dtypes=True), 1)
            GP.inc(g.iota(sqscr[:, 0:P], pattern=[[1, P]], base=0, channel_multiplier=0,
                          allow_small_or_imprecise_dtypes=True), 1)
            LDg.inc(g.dma_start(out=gws[:], in_=gwT.rearrange("(m p) e -> p m e", p=P)), 16)
            LDa.inc(g.dma_start(out=acc[:], in_=x_t[:, 0:TBM, :]), 16)
        gp_consts = GP.n
        ld_gws = (LDg, LDg.n)
        ld_acc = (LDa, LDa.n)

        @block.vector
        def _(v):
            v.wait_ge(GP.h, gp_consts)
            v.wait_ge(ld_gws[0].h, ld_gws[1])
            VE.inc(v.tensor_tensor(identt[:], identt[:], sqscr[:, 0:P], OP.is_equal))
            VE.inc(v.tensor_copy(gT[:], gws[:]))
        ve_consts = VE.n

        st_m2 = e(nc.sbuf_tensor("st_m2", [P, TB], F32))

        def emit_ln_group(items, ld_waits=None, scalar_pre=None, vector_pre=None):
            cols = [c for c, _, _ in items]
            lo, hi = min(cols), max(cols) + 1
            csl = slice(lo, hi)

            @block.vector
            def _(v):
                if vector_pre is not None:
                    v.wait_ge(vector_pre[0].h, vector_pre[1])
                for i, (c, src_ap, _) in enumerate(items):
                    if ld_waits is not None and ld_waits[i] is not None:
                        v.wait_ge(ld_waits[i][0].h, ld_waits[i][1])
                    VE.inc(v.tensor_reduce(st_sum[:, c:c + 1], src_ap, AX.X, OP.add,
                                           negate=True))
            v_red = VE.n

            @block.scalar
            def _(s):
                if scalar_pre is not None:
                    s.wait_ge(scalar_pre[0].h, scalar_pre[1])
                for i, (c, src_ap, _) in enumerate(items):
                    if ld_waits is not None and ld_waits[i] is not None:
                        s.wait_ge(ld_waits[i][0].h, ld_waits[i][1])
                    if i > 0:
                        s.wait_ge(SE.h, SE.n)
                    SE.inc(s.activation(out=sqscr[:], in_=src_ap, func=AF.Square,
                                        accum_out=st_ssq[:, c:c + 1]))
            s_sq = SE.n

            @block.vector
            def _(v):
                v.wait_ge(VE.h, v_red)
                v.wait_ge(SE.h, s_sq)
                VE.inc(v.tensor_scalar_mul(st_sum[:, csl], st_sum[:, csl], INV_D))
                VE.inc(v.tensor_scalar_mul(st_tmp[:, csl], st_ssq[:, csl], INV_D))
            v_d1 = VE.n

            @block.vector
            def _(v):
                v.wait_ge(VE.h, v_d1)
                VE.inc(v.tensor_tensor(st_m2[:, csl], st_sum[:, csl], st_sum[:, csl],
                                       OP.mult))
            v_d2 = VE.n

            @block.vector
            def _(v):
                v.wait_ge(VE.h, v_d2)
                VE.inc(v.tensor_tensor(st_var[:, csl], st_tmp[:, csl], st_m2[:, csl],
                                       OP.subtract))
            v_d3 = VE.n

            @block.vector
            def _(v):
                v.wait_ge(VE.h, v_d3)
                VE.inc(v.tensor_scalar_add(st_var[:, csl], st_var[:, csl], EPS))
            v_d4 = VE.n

            @block.scalar
            def _(s):
                s.wait_ge(VE.h, v_d4)
                SE.inc(s.activation(out=st_tmp[:, csl], in_=st_var[:, csl], func=AF.Sqrt))
            s_std = SE.n

            @block.vector
            def _(v):
                v.wait_ge(SE.h, s_std)
                VE.inc(v.reciprocal(st_rstd[:, csl], st_tmp[:, csl]))
            v_rec = VE.n

            @block.vector
            def _(v):
                v.wait_ge(VE.h, v_rec)
                VE.inc(v.tensor_tensor(st_nmr[:, csl], st_sum[:, csl], st_rstd[:, csl],
                                       OP.mult))
            v_nmr = VE.n

            outs = []
            for c, src_ap, dst_ap in items:
                @block.scalar
                def _(s, c=c, src_ap=src_ap, dst_ap=dst_ap):
                    s.wait_ge(VE.h, v_nmr)
                    SE.inc(s.activation(out=dst_ap, in_=src_ap, func=AF.Identity,
                                        bias=st_nmr[:, c:c + 1],
                                        scale=st_rstd[:, c:c + 1]))
                outs.append(SE.n)
            return outs

        def emit_ln(src_ap, dst_ap, col, ld_wait=None, scalar_pre=None, vector_pre=None):
            return emit_ln_group([(col, src_ap, dst_ap)], ld_waits=[ld_wait],
                                 scalar_pre=scalar_pre, vector_pre=vector_pre)[0]

        arena_free = [(None, 0), (None, 0)]
        with ExitStack() as attn_scope:
            ea = attn_scope.enter_context
            Xn = ea(nc.sbuf_tensor("Xn", [P, TB, D], BF16))
            XnT = ea(nc.sbuf_tensor("XnT", [P, DC, TBM, P], BF16))
            xring = [ea(nc.sbuf_tensor(f"xring{i}", [P, D], F32)) for i in range(2)]
            Asb = [ea(nc.sbuf_tensor("Asb0", [P, D], BF16))]
            ATsl = [ea(nc.sbuf_tensor("ATsl0", [P, DC, 4, P], BF16))]

            s_norm = emit_ln_group(
                [(tb, acc[:, tb, :], Xn[:, tb, :]) for tb in range(TBM)],
                ld_waits=[ld_acc] * TBM,
                scalar_pre=(VE, ve_consts), vector_pre=None)
            ring_free = [(None, 0), (None, 0)]
            for tb in range(TBM, TB):
                slot = tb % 2
                @block.gpsimd
                def _(g, tb=tb, slot=slot, gate=tuple(ring_free[slot])):
                    sm, val = gate
                    if sm is not None:
                        g.wait_ge(sm.h, val)
                    LDx[slot].inc(g.dma_start(out=xring[slot][:], in_=x_t[:, tb, :]), 16)
                sn = emit_ln(xring[slot][:], Xn[:, tb, :], tb,
                             ld_wait=(LDx[slot], LDx[slot].n))
                s_norm.append(sn)
                ring_free[slot] = (SE, sn)
            s_ln1_all = SE.n
            tap("ln1", Xn[:], waits=[(SE, s_ln1_all)])

            if stage == "ln1":
                @block.gpsimd
                def _(g):
                    g.wait_ge(LDo.h, LDo.n)
                return nc

            for tb in range(TBM):
                @block.sync
                def _(sy, tb=tb):
                    sy.wait_ge(SE.h, s_norm[tb])
                    TP.inc(sy.dma_start(out=XnT[:, :, tb, :], in_=Xn[:, tb, :],
                                        transpose=True), 16)
                    sy.wait_ge(TP.h, TP.n)
            tp_xnt = TP.n
            tap("xnt", XnT[:], waits=[(TP, tp_xnt)])

            if stage == "xnt":
                @block.gpsimd
                def _(g):
                    g.wait_ge(LDo.h, LDo.n)
                return nc

            accw = {}
            asb_free = [(None, 0)]
            atsl_free = [(None, 0)]
            tp_at = {}
            for g_r in range(4):
                for ibl in range(4):
                    gi = 4 * g_r + ibl
                    par = gi % 2
                    ps = pAR[par]

                    @block.tensor
                    def _(t, gi=gi, ps=ps, gate=tuple(arena_free[par])):
                        if gi == 2:
                            t.wait_ge(SE.h, s_ln1_all)
                        sm, val = gate
                        if sm is not None:
                            t.wait_ge(sm.h, val)
                        for ct in range(TB):
                            if gi < 2:
                                t.wait_ge(SE.h, s_norm[ct])
                            for jr in range(4):
                                ins = t.matmul(ps[:, jr * 512:(jr + 1) * 512],
                                               Xn[:, ct, gi * P:(gi + 1) * P],
                                               Xn[:, ct, jr * 512:(jr + 1) * 512],
                                               start=(ct == 0), stop=(ct == TB - 1),
                                               skip_group_check=True)
                        PE.inc(ins)
                    pe_gram = PE.n

                    if stage == "gram0" and gi == 0:
                        @block.vector
                        def _(v, ps=ps, pe_gram=pe_gram):
                            v.wait_ge(PE.h, pe_gram)
                            VE.inc(v.tensor_copy(sqscr[:], ps[:]))
                        tap("G0", sqscr[:], waits=[(VE, VE.n)])

                        @block.gpsimd
                        def _(g):
                            g.wait_ge(LDo.h, LDo.n)
                        return nc

                    mxn = sm_mxn[:, par:par + 1]
                    rs = sm_rs[:, par:par + 1]
                    lnr = sm_lnr[:, par:par + 1]
                    b2 = sm_b2[:, par:par + 1]

                    @block.vector
                    def _(v, ps=ps, pe_gram=pe_gram, mxn=mxn):
                        v.wait_ge(PE.h, pe_gram)
                        VE.inc(v.tensor_reduce(mxn, ps[:], AX.X, OP.max, negate=True))
                    v_mxn = VE.n

                    @block.scalar
                    def _(s, gi=gi, ps=ps, v_mxn=v_mxn, mxn=mxn, rs=rs,
                          gate=tuple(asb_free[0])):
                        s.wait_ge(VE.h, v_mxn)
                        sm, val = gate
                        if sm is not None:
                            s.wait_ge(sm.h, val)
                        SE.inc(s.activation(out=Asb[0][:], in_=ps[:], func=AF.Exp,
                                            bias=mxn, accum_out=rs))
                    s_e1 = SE.n

                    @block.scalar
                    def _(s, s_e1=s_e1, rs=rs, lnr=lnr):
                        s.wait_ge(SE.h, s_e1)
                        SE.inc(s.activation(out=lnr, in_=rs, func=AF.Ln))
                    s_ln = SE.n

                    @block.vector
                    def _(v, s_ln=s_ln, mxn=mxn, lnr=lnr, b2=b2):
                        v.wait_ge(SE.h, s_ln)
                        VE.inc(v.tensor_tensor(b2, mxn, lnr, OP.subtract))
                    v_b2 = VE.n

                    @block.scalar
                    def _(s, gi=gi, ps=ps, v_b2=v_b2, b2=b2):
                        s.wait_ge(VE.h, v_b2)
                        SE.inc(s.activation(out=Asb[0][:], in_=ps[:], func=AF.Exp,
                                            bias=b2))
                    s_e2 = (SE, SE.n)
                    arena_free[par] = s_e2

                    if stage == "gram1" and gi == 0:
                        tap("A0", Asb[0][:], waits=[(SE, s_e2)])

                        @block.gpsimd
                        def _(g):
                            g.wait_ge(LDo.h, LDo.n)
                        return nc

                    @block.sync
                    def _(sy, g_r=g_r, ibl=ibl, gi=gi, s_e2=s_e2,
                          gate=tuple(atsl_free[0])):
                        sy.wait_ge(s_e2[0].h, s_e2[1])
                        sm, val = gate
                        if sm is not None and ibl == 0:
                            sy.wait_ge(sm.h, val)
                        TP.inc(sy.dma_start(out=ATsl[0][:, :, ibl, :],
                                            in_=Asb[0][:], transpose=True), 16)
                        sy.wait_ge(TP.h, TP.n)
                    asb_free[0] = (TP, TP.n)

                    if stage == "at1" and gi == 0:
                        tap("AT0", ATsl[0][:], waits=[(TP, TP.n)])

                        @block.gpsimd
                        def _(g):
                            g.wait_ge(LDo.h, LDo.n)
                        return nc
                tp_at[g_r] = TP.n

                for tb in range(TBM):
                    par = tb % 2
                    slot = pAR[par][:, 0:512]

                    @block.tensor
                    def _(t, g_r=g_r, tb=tb, slot=slot, gate=tuple(arena_free[par])):
                        t.wait_ge(TP.h, tp_at[g_r])
                        if g_r == 0 and tb == 0:
                            t.wait_ge(TP.h, tp_xnt)
                        sm, val = gate
                        if sm is not None:
                            t.wait_ge(sm.h, val)
                        for m in range(DC):
                            ins = t.matmul(slot, XnT[:, m, tb, :],
                                           flat2(ATsl[0][:, m, :, :]),
                                           start=(m == 0), stop=(m == DC - 1))
                        PE.inc(ins)
                    pe_out = PE.n

                    @block.vector
                    def _(v, g_r=g_r, tb=tb, slot=slot, pe_out=pe_out):
                        v.wait_ge(PE.h, pe_out)
                        a_sl = acc[:, tb, g_r * 512:(g_r + 1) * 512]
                        VE.inc(v.tensor_tensor(a_sl, a_sl, slot, OP.add))
                    arena_free[par] = (VE, VE.n)
                    accw[(g_r, tb)] = VE.n
                atsl_free[0] = (PE, PE.n)
            ve_x1 = VE.n
            tap("x1", acc[:], waits=[(VE, ve_x1)])


        if stage == "attn":
            @block.gpsimd
            def _(g):
                g.wait_ge(LDo.h, LDo.n)
            return nc
        with ExitStack() as ffn_scope:
            ef = ffn_scope.enter_context
            scores = ef(nc.sbuf_tensor("scores", [P, TBM, NE], F32))
            topv = ef(nc.sbuf_tensor("topv", [P, TBM, NE], F32))
            topi = ef(nc.sbuf_tensor("topi", [P, TBM, NE], U32))
            topf = ef(nc.sbuf_tensor("topf", [P, TBM, NE], F32))
            Ssc = ef(nc.sbuf_tensor("Ssc", [P, TBM, NE], F32))
            STb = ef(nc.sbuf_tensor("STb", [NE, TBM * P], BF16))
            W1T = ef(nc.sbuf_tensor("W1T", [P, DC, 2, 256], BF16))
            W3T = ef(nc.sbuf_tensor("W3T", [P, DC, 2, 256], BF16))
            W2T = ef(nc.sbuf_tensor("W2T", [P, UB, 2, 512], BF16))
            tmp1 = ef(nc.sbuf_tensor("tmp1", [P, 512], F32))
            tmp3 = ef(nc.sbuf_tensor("tmp3", [P, 512], F32))
            silb = ef(nc.sbuf_tensor("silb", [P, 512], F32))
            posu = ef(nc.sbuf_tensor("posu", [P, TBM, NE], U32))
            scpm = ef(nc.sbuf_tensor("scpm", [P, TBM, NE], F32))
            vals = ef(nc.sbuf_tensor("vals", [P, TBM], F32))
            pairb = ef(nc.sbuf_tensor("pairb", [P, 2, TBM, 2], F32))
            pref = ef(nc.sbuf_tensor("pref", [P, NCH, 2], F32))
            ones1 = ef(nc.sbuf_tensor("ones1", [1, P], F32))
            idxf = ef(nc.sbuf_tensor("idxf", [P, 4, NCH, 2], F32))
            idxu = ef(nc.sbuf_tensor("idxu", [P, 4, NCH], U32))
            gmin = ef(nc.sbuf_tensor("gmin", [P, 4, NCH], U32))
            e3t = ef(nc.sbuf_tensor("e3t", [P, TBM, NE], F32))
            sc_done = {}
            lists_ready = {}



            from contextlib import ExitStack as _ES
            xf_scope = _ES()
            cw_mask = xf_scope.enter_context(nc.sbuf_tensor("cw_mask", [NE, TBM * P], F32))
            cw_pos = xf_scope.enter_context(nc.sbuf_tensor("cw_pos", [NE, TBM * P], F32))
            cw_trash = xf_scope.enter_context(nc.sbuf_tensor("cw_trash", [NE, TBM * P], F32))
            cw_ones = xf_scope.enter_context(nc.sbuf_tensor("cw_ones", [NE, TBM * P], F32))
            hbuf = xf_scope.enter_context(nc.sbuf_tensor("hbuf", [P, UB, H], BF16))
            xfT = xf_scope.enter_context(nc.sbuf_tensor("xfT", [P, DC, TBM, P], BF16))
            xfring = xf_scope.enter_context(nc.sbuf_tensor("xfring", [P, 2, D], BF16))
            sp = {}



            def alloc_sparse():
                xf_scope.close()
                sp["sgrow"] = ffn_scope.enter_context(nc.sbuf_tensor("sgrow", [1, 4, CAP], F32))
                sp["Sg"] = ffn_scope.enter_context(nc.sbuf_tensor("Sg", [P, 2, CAP], F32))
                sp["xg"] = ffn_scope.enter_context(nc.sbuf_tensor("xg", [P, NCH, D], BF16))
                sp["xgT"] = ffn_scope.enter_context(nc.sbuf_tensor("xgT", [P, 2, NCH, DC, P], BF16))
                sp["hb"] = ffn_scope.enter_context(nc.sbuf_tensor("hb", [P, UB, CAP], BF16))
                sp["rgout"] = ffn_scope.enter_context(nc.sbuf_tensor("rgout", [P, NCH, D], BF16))


            w1t_r = w1t.rearrange("E (m p) u -> E p m u", p=P)
            w3t_r = w3t.rearrange("E (m p) u -> E p m u", p=P)
            w2t_r = w2t.rearrange("E (mu p) d -> E p mu d", p=P)
            ws1t_r = ws1t.rearrange("(m p) u -> p m u", p=P)
            ws3t_r = ws3t.rearrange("(m p) u -> p m u", p=P)
            ws2t_r = ws2t.rearrange("(mu p) d -> p mu d", p=P)

            w1_free = [(VE, ve_x1), (VE, ve_x1)]
            w3_free = [(VE, ve_x1), (VE, ve_x1)]
            w2_free = [(VE, ve_x1), (VE, ve_x1)]
            w_ready = {}
            w_emitted = set()

            def load_w(ex, kind, idx, gate_override=None):
                if ex >= NEXP or (ex, kind, idx) in w_emitted:
                    return
                w_emitted.add((ex, kind, idx))
                slot = idx % 2
                if kind == 1:
                    srcs = w1t_r[ex] if ex < NE else ws1t_r
                    srcap = srcs[:, :, idx * 256:(idx + 1) * 256]
                    dst = W1T[:, :, slot, :]
                    sem, gate = LDw1[slot], tuple(w1_free[slot])
                elif kind == 3:
                    srcs = w3t_r[ex] if ex < NE else ws3t_r
                    srcap = srcs[:, :, idx * 256:(idx + 1) * 256]
                    dst = W3T[:, :, slot, :]
                    sem, gate = LDw3[slot], tuple(w3_free[slot])
                else:
                    srcs = w2t_r[ex] if ex < NE else ws2t_r
                    srcap = srcs[:, :, idx * 512:(idx + 1) * 512]
                    dst = W2T[:, :, slot, :]
                    sem, gate = LDw2[slot], tuple(w2_free[slot])
                if gate_override is not None:
                    gate = tuple(gate_override)

                @block.gpsimd
                def _(g, srcap=srcap, dst=dst, sem=sem, gate=gate):
                    sm, val = gate
                    if sm is not None:
                        g.wait_ge(sm.h, val)
                    sem.inc(g.dma_start(out=dst, in_=srcap), 16)
                w_ready[(ex, kind, idx)] = (sem, sem.n)

            for _q in range(2):
                load_w(NE, 1, _q)
                load_w(NE, 3, _q)
            for _kg in range(2):
                load_w(NE, 2, _kg)

            tp_xf = []
            gate_prev = (None, 0)
            for grp in range(4):
                tbs = range(grp * 2, grp * 2 + 2)
                if grp > 0:
                    @block.scalar
                    def _(s, g0=LDf[0].n, g1=LDf[1].n):
                        s.wait_ge(LDf[0].h, g0)
                        s.wait_ge(LDf[1].h, g1)
                sns = emit_ln_group(
                    [(tb, acc[:, tb, :], xfring[:, tb % 2, :]) for tb in tbs],
                    scalar_pre=(VE, ve_x1) if grp == 0 else gate_prev,
                    vector_pre=(VE, ve_x1) if grp == 0 else None)
                for j, tb in enumerate(tbs):
                    @block.sync
                    def _(sy, tb=tb, sn=sns[j]):
                        sy.wait_ge(SE.h, sn)
                        TP.inc(sy.dma_start(out=xfT[:, :, tb, :],
                                            in_=xfring[:, tb % 2, :],
                                            transpose=True), 16)
                        sy.wait_ge(TP.h, TP.n)
                    tp_xf.append(TP.n)

                    @block.gpsimd
                    def _(g, tb=tb, sn=sns[j]):
                        g.wait_ge(SE.h, sn)
                        LDf[tb % 2].inc(g.dma_start(
                            out=xfd.rearrange("(tb p) d -> p tb d", p=P)[:, tb, :],
                            in_=xfring[:, tb % 2, :]), 16)
                gate_prev = (TP, TP.n)
            tp_xfT = TP.n
            ldf_all = (LDf[0].n, LDf[1].n)

            gate_arena0 = tuple(arena_free[0])
            gate_arena1 = tuple(arena_free[1])
            pe_gate = []
            s_gate_evac = []
            for tb in range(TBM):
                @block.tensor
                def _(t, tb=tb):
                    t.wait_ge(VE.h, ve_consts)
                    t.wait_ge(TP.h, tp_xf[tb])
                    if tb < 2:
                        sm, val = gate_arena1
                        if sm is not None:
                            t.wait_ge(sm.h, val)
                    if tb >= 2:
                        t.wait_ge(SE.h, s_gate_evac[tb - 2])
                    for m in range(DC):
                        ins = t.matmul(pAR[1][:, (tb % 2) * 512:(tb % 2) * 512 + NE],
                                       xfT[:, m, tb, :], gT[:, m, 0:NE],
                                       start=(m == 0), stop=(m == DC - 1))
                    PE.inc(ins)
                pe_gate.append(PE.n)

                @block.scalar
                def _(s, tb=tb):
                    s.wait_ge(PE.h, pe_gate[tb])
                    SE.inc(s.activation(out=scores[:, tb, :],
                                        in_=pAR[1][:, (tb % 2) * 512:(tb % 2) * 512 + NE],
                                        func=AF.Sigmoid))
                s_gate_evac.append(SE.n)
            s_scores = SE.n
            tap("scores", scores[:], waits=[(SE, s_scores)])

            if stage == "scores":
                @block.gpsimd
                def _(g):
                    g.wait_ge(LDo.h, LDo.n)
                return nc

            @block.vector
            def _(v):
                v.wait_ge(SE.h, s_scores)
                for tb in range(TBM):
                    VE.inc(v.max(topv[:, tb, :], scores[:, tb, :]))
            v_max = VE.n

            @block.vector
            def _(v):
                v.wait_ge(VE.h, v_max)
                for tb in range(TBM):
                    VE.inc(v.max_index(topi[:, tb, :], topv[:, tb, :], scores[:, tb, :]))
            v_mi = VE.n

            @block.vector
            def _(v):
                v.wait_ge(VE.h, v_mi)
                v.wait_ge(GP.h, 1)
                for tb in range(TBM):
                    VE.inc(v.tensor_copy(topf[:, tb, :], topi[:, tb, :]))
            v_tf = VE.n

            @block.vector
            def _(v):
                v.wait_ge(VE.h, v_tf)
                for tb in range(TBM):
                    VE.inc(v.tensor_scalar(Ssc[:, tb, :], iota8[:], topf[:, tb, 0:1],
                                           None, OP.is_equal))
                    VE.inc(v.tensor_scalar(e3t[:, tb, :], iota8[:], topf[:, tb, 1:2],
                                           None, OP.is_equal))
            v_eq = VE.n

            @block.vector
            def _(v):
                v.wait_ge(VE.h, v_eq)
                for tb in range(TBM):
                    VE.inc(v.tensor_scalar(Ssc[:, tb, :], Ssc[:, tb, :], topv[:, tb, 0:1],
                                           None, OP.mult))
            v_s1 = VE.n

            @block.vector
            def _(v):
                v.wait_ge(VE.h, v_s1)
                for tb in range(TBM):
                    VE.inc(v.scalar_tensor_tensor(Ssc[:, tb, :], e3t[:, tb, :],
                                                  topv[:, tb, 1:2], Ssc[:, tb, :],
                                                  OP.mult, OP.add))
            v_S = VE.n
            tap("S", Ssc[:], waits=[(VE, v_S)])

            st_state = {}

            def emit_ST():
                ve_st = []
                for tb in range(TBM):
                    @block.tensor
                    def _(t, tb=tb):
                        t.wait_ge(VE.h, v_S)
                        t.wait_ge(VE.h, ve_consts)
                        if tb == 0:
                            sm, val = gate_arena1
                            if sm is not None:
                                t.wait_ge(sm.h, val)
                        else:
                            t.wait_ge(VE.h, ve_st[tb - 1])
                        PE.inc(t.transpose(pAR[1][0:NE, 1024:1024 + P], Ssc[:, tb, :],
                                           identt[:]))
                    pe_st = PE.n

                    @block.vector
                    def _(v, tb=tb, pe_st=pe_st):
                        v.wait_ge(PE.h, pe_st)
                        VE.inc(v.tensor_copy(STb[0:NE, tb * P:(tb + 1) * P],
                                             pAR[1][0:NE, 1024:1024 + P]))
                    ve_st.append(VE.n)
                st_state["v_stb"] = VE.n

            def emit_compact():
                v_stb = st_state["v_stb"]

                @block.vector
                def _(v):
                    v.wait_ge(VE.h, v_stb)
                    v.wait_ge(GP.h, gp_sparse)
                    VE.inc(v.memset(cw_ones[:], 1.0))
                    VE.inc(v.tensor_single_scalar(cw_mask[:], STb[:], 0.0, OP.is_gt))
                c_m = VE.n

                @block.vector
                def _(v):
                    v.wait_ge(VE.h, c_m)
                    VE.inc(v.tensor_tensor_scan(cw_pos[:], cw_ones[:], cw_mask[:],
                                                0.0, OP.mult, OP.add))
                c_p1 = VE.n

                @block.vector
                def _(v):
                    v.wait_ge(VE.h, c_p1)
                    VE.inc(v.tensor_single_scalar(cw_pos[:], cw_pos[:], -1.0, OP.add))
                    VE.inc(v.tensor_copy(cw_ones[:], STb[:]))
                c_p2 = VE.n

                @block.vector
                def _(v):
                    v.wait_ge(VE.h, c_p2)
                    VE.inc(v.tensor_tensor(cw_pos[:], cw_pos[:], cw_trash[:], OP.subtract))
                c_p3 = VE.n

                @block.vector
                def _(v):
                    v.wait_ge(VE.h, c_p3)
                    VE.inc(v.tensor_tensor(cw_pos[:], cw_mask[:], cw_pos[:], OP.mult))
                c_p4 = VE.n

                @block.vector
                def _(v):
                    v.wait_ge(VE.h, c_p4)
                    VE.inc(v.tensor_tensor(cw_pos[:], cw_pos[:], cw_trash[:], OP.add))
                c_pos = VE.n

                prev_ev = [0]
                for tb in range(TBM):
                    @block.tensor
                    def _(t, tb=tb):
                        t.wait_ge(VE.h, c_pos)
                        if tb > 0:
                            t.wait_ge(VE.h, prev_ev[0])
                        t.transpose(pAR[1][:, 1024:1024 + NE],
                                    cw_pos[0:NE, tb * P:(tb + 1) * P], identt[0:NE, 0:NE])
                        PE.inc(t.transpose(pAR[1][:, 1536:1536 + NE],
                                           cw_ones[0:NE, tb * P:(tb + 1) * P],
                                           identt[0:NE, 0:NE]))
                    pe_t = PE.n

                    @block.vector
                    def _(v, tb=tb, pe_t=pe_t):
                        v.wait_ge(PE.h, pe_t)
                        VE.inc(v.tensor_copy(posu[:, tb, :], pAR[1][:, 1024:1024 + NE]))
                        VE.inc(v.tensor_copy(scpm[:, tb, :], pAR[1][:, 1536:1536 + NE]))
                    prev_ev[0] = VE.n
                c_tp = VE.n
                st_state["c_tp"] = c_tp

                for ex in range(NE):
                    slot = ex % 2

                    @block.gpsimd
                    def _(g, ex=ex):
                        g.wait_ge(GP.h, gp_sparse)
                        LDz.inc(g.dma_start(
                            out=plist[ex, 0:CAP].rearrange("(c p) w -> p c w", p=P),
                            in_=pref[:]), 16)
                    ld_pf = LDz.n

                    @block.vector
                    def _(v, ex=ex, slot=slot, gate=GDc[slot].n):
                        v.wait_ge(VE.h, c_tp)
                        if gate:
                            v.wait_ge(GDc[slot].h, gate)
                        VE.inc(v.tensor_copy(
                            pairb[:, slot, :, 0:1],
                            vals[:].rearrange("p (c one) -> p c one", one=1)))
                        VE.inc(v.tensor_copy(pairb[:, slot, :, 1:2],
                                             scpm[:, :, ex:ex + 1]))
                    v_pair = VE.n

                    @block.gpsimd
                    def _(g, ex=ex, slot=slot, v_pair=v_pair, ld_pf=ld_pf):
                        g.wait_ge(VE.h, v_pair)
                        g.wait_ge(LDz.h, ld_pf)
                        for tb in range(TBM):
                            GDc[slot].inc(g.indirect_dma_start(
                                out=plist.rearrange("E c w -> (E c) w"),
                                out_offset=IndirectOffsetOnAxis(posu[:, tb, ex:ex + 1], 0),
                                in_=pairb[:, slot, tb, :],
                                in_offset=None,
                                element_offset=ex * (CAP + TBM * P) * 2), 16)
                    sc_done[ex] = (GDc[slot], GDc[slot].n)

            gather_done = set()
            gather_tp = {}

            def emit_gather_chain(ex):
                gather_done.add(ex)
                slot = ex % 4
                buf = ex % 2
                lr_ve, _sl = lists_ready[ex]
                for c in range(NCH):
                    @block.gpsimd
                    def _(g, ex=ex, c=c, slot=slot, lr_ve=lr_ve,
                          gate=tuple(xg_free[0]), mg=tuple(mem_gate)):
                        g.wait_ge(VE.h, lr_ve)
                        if c == 0:
                            g.wait_ge(LDf[0].h, ldf_all[0])
                            g.wait_ge(LDf[1].h, ldf_all[1])
                            sm, val = gate
                            if sm is not None:
                                g.wait_ge(sm.h, val)
                            sm2, val2 = mg
                            if sm2 is not None:
                                g.wait_ge(sm2.h, val2)
                        LDxg[c].inc(g.indirect_dma_start(
                            out=sp["xg"][:, c, :],
                            out_offset=None,
                            in_=xfd,
                            in_offset=IndirectOffsetOnAxis(gmin[:, slot, c:c + 1], 0)), 16)
                    ld_gs = [LDxg[i].n for i in range(NCH)]

                @block.sync
                def _(sy, buf=buf, ld_gs=tuple(ld_gs), xfree=tuple(xgt_free[buf]),
                      mg=tuple(mem_gate)):
                    for i in range(NCH):
                        sy.wait_ge(LDxg[i].h, ld_gs[i])
                    sm, val = xfree
                    if sm is not None:
                        sy.wait_ge(sm.h, val)
                    sm2, val2 = mg
                    if sm2 is not None:
                        sy.wait_ge(sm2.h, val2)
                    TP.inc(sy.dma_start(out=sp["xgT"][:, buf, :, :, :],
                                        in_=sp["xg"][:].rearrange("p c d -> p (c d)"),
                                        transpose=True), 16)
                    sy.wait_ge(TP.h, TP.n)
                gather_tp[ex] = TP.n
                xg_free[0] = (TP, TP.n)

            def emit_reload(ex, gate):
                slot = ex % 4

                @block.gpsimd
                def _(g, ex=ex, slot=slot, sc=sc_done[ex], gate=gate):
                    g.wait_ge(sc[0].h, sc[1])
                    if gate[0] is not None:
                        g.wait_ge(gate[0].h, gate[1])
                    LDl[slot].inc(g.dma_start(
                        out=idxf[:, slot, :, :],
                        in_=plist[ex, 0:CAP].rearrange("(c p) w -> p c w", p=P)), 16)
                    LDl[slot].inc(g.dma_start(
                        out=sp["sgrow"][0:1, slot, :],
                        in_=plist[ex, 0:CAP, 1:2].rearrange("c w -> (w c)")[None, :],
                        allow_slow_non_contiguous=True), 16)
                ld_l = LDl[slot].n

                @block.vector
                def _(v, slot=slot, ld_l=ld_l):
                    v.wait_ge(LDl[slot].h, ld_l)
                    VE.inc(v.tensor_copy(idxu[:, slot, :], idxf[:, slot, :, 0]))
                v_iu = VE.n

                @block.vector
                def _(v, slot=slot, v_iu=v_iu):
                    v.wait_ge(VE.h, v_iu)
                    VE.inc(v.tensor_single_scalar(gmin[:, slot, :], idxu[:, slot, :],
                                                  TBM * P - 1, OP.min))
                lists_ready[ex] = (VE.n, slot)

            gather_done = set()
            gather_tp = {}

            def emit_gather_chain(ex):
                gather_done.add(ex)
                slot = ex % 4
                buf = ex % 2
                lr_ve, _sl = lists_ready[ex]
                for c in range(NCH):
                    @block.gpsimd
                    def _(g, ex=ex, c=c, slot=slot, lr_ve=lr_ve,
                          gate=tuple(xg_free[0]), mg=tuple(mem_gate)):
                        g.wait_ge(VE.h, lr_ve)
                        if c == 0:
                            g.wait_ge(LDf[0].h, ldf_all[0])
                            g.wait_ge(LDf[1].h, ldf_all[1])
                            sm, val = gate
                            if sm is not None:
                                g.wait_ge(sm.h, val)
                            sm2, val2 = mg
                            if sm2 is not None:
                                g.wait_ge(sm2.h, val2)
                        LDxg[c].inc(g.indirect_dma_start(
                            out=sp["xg"][:, c, :],
                            out_offset=None,
                            in_=xfd,
                            in_offset=IndirectOffsetOnAxis(gmin[:, slot, c:c + 1], 0)), 16)
                    ld_gs = [LDxg[i].n for i in range(NCH)]

                @block.sync
                def _(sy, buf=buf, ld_gs=tuple(ld_gs), xfree=tuple(xgt_free[buf]),
                      mg=tuple(mem_gate)):
                    for i in range(NCH):
                        sy.wait_ge(LDxg[i].h, ld_gs[i])
                    sm, val = xfree
                    if sm is not None:
                        sy.wait_ge(sm.h, val)
                    sm2, val2 = mg
                    if sm2 is not None:
                        sy.wait_ge(sm2.h, val2)
                    TP.inc(sy.dma_start(out=sp["xgT"][:, buf, :, :, :],
                                        in_=sp["xg"][:].rearrange("p c d -> p (c d)"),
                                        transpose=True), 16)
                    sy.wait_ge(TP.h, TP.n)
                gather_tp[ex] = TP.n
                xg_free[0] = (TP, TP.n)

            def emit_reload(ex, gate):
                slot = ex % 4

                @block.gpsimd
                def _(g, ex=ex, slot=slot, sc=sc_done[ex], gate=gate):
                    g.wait_ge(sc[0].h, sc[1])
                    if gate[0] is not None:
                        g.wait_ge(gate[0].h, gate[1])
                    LDl[slot].inc(g.dma_start(
                        out=idxf[:, slot, :, :],
                        in_=plist[ex, 0:CAP].rearrange("(c p) w -> p c w", p=P)), 16)
                    LDl[slot].inc(g.dma_start(
                        out=sp["sgrow"][0:1, slot, :],
                        in_=plist[ex, 0:CAP, 1:2].rearrange("c w -> (w c)")[None, :],
                        allow_slow_non_contiguous=True), 16)
                ld_l = LDl[slot].n

                @block.vector
                def _(v, slot=slot, ld_l=ld_l):
                    v.wait_ge(LDl[slot].h, ld_l)
                    VE.inc(v.tensor_copy(idxu[:, slot, :], idxf[:, slot, :, 0]))
                v_iu = VE.n

                @block.vector
                def _(v, slot=slot, v_iu=v_iu):
                    v.wait_ge(VE.h, v_iu)
                    VE.inc(v.tensor_single_scalar(gmin[:, slot, :], idxu[:, slot, :],
                                                  TBM * P - 1, OP.min))
                lists_ready[ex] = (VE.n, slot)


            arena_free_z = [tuple(arena_free[0]), (None, 0)]
            last_sil = [0]
            last_hmul = [0]
            slotfree_r = [(None, 0), (None, 0)]
            h_readers_done = [0]

            ORDER = [NE] + list(range(NE))
            rscat = {}
            sg_free = (None, 0)
            xgt_free = [(None, 0), (None, 0)]
            xg_free = [(None, 0)]
            rg_free = (None, 0)
            mem_gate = (None, 0)
            reload_done = set()
            pending_scatter = [None]
            pend_w2 = []
            ve_acc_done = 0

            for ei, ex in enumerate(ORDER):
                nxt = ORDER[ei + 1] if ei + 1 < len(ORDER) else NEXP
                is_shared = (ex == NE)
                for q in range(2):
                    load_w(ex, 1, q)
                    load_w(ex, 3, q)
                for kg in range(2):
                    load_w(ex, 2, kg)

                if ei == 0:
                    @block.gpsimd
                    def _(g):
                        g.wait_ge(VE.h, ve_x1)
                        GP.inc(g.iota(vals[:], pattern=[[P, TBM]], base=0, channel_multiplier=1,
                                      allow_small_or_imprecise_dtypes=True))
                        GP.inc(g.iota(pref[:, :, 0:1], pattern=[[P, NCH], [0, 1]], base=TBM * P,
                                      channel_multiplier=1, allow_small_or_imprecise_dtypes=True))
                        GP.inc(g.iota(cw_trash[:], pattern=[[1, TBM * P]], base=CAP,
                                      channel_multiplier=0, allow_small_or_imprecise_dtypes=True))
                    gp_sparse = GP.n

                    @block.vector
                    def _(v):
                        v.wait_ge(GP.h, gp_sparse)
                        v.wait_ge(VE.h, ve_x1)
                        VE.inc(v.memset(pref[:, :, 1:2], 0.0))
                        VE.inc(v.memset(ones1[:], 1.0))

                    ve_zero_src = VE.n

                    ldz_zero = LDz.n

                if is_shared:
                    pair_free = [tuple(arena_free_z[0]), tuple(arena_free_z[0])]
                    for step in range(2 * UB):
                        ub, th = step // 2, step % 2
                        q = ub // 2
                        uo = (ub % 2) * P
                        o = (step % 2) * 1024
                        z1 = pAR[0][:, o:o + 512]
                        z3 = pAR[0][:, o + 512:o + 1024]
                        h_sl = hbuf[:, ub, th * 512:(th + 1) * 512]

                        @block.tensor
                        def _(t, ex=ex, q=q, uo=uo, th=th, z1=z1, z3=z3,
                              gate=tuple(pair_free[step % 2])):
                            r1 = w_ready[(ex, 1, q)]
                            r3 = w_ready[(ex, 3, q)]
                            t.wait_ge(r1[0].h, r1[1])
                            t.wait_ge(r3[0].h, r3[1])
                            sm, val = gate
                            if sm is not None:
                                t.wait_ge(sm.h, val)
                            for m in range(DC):
                                l1 = W1T[:, m, q % 2, uo:uo + P]
                                l3 = W3T[:, m, q % 2, uo:uo + P]
                                rhs = flat2(xfT[:, m, th * 4:(th + 1) * 4, :])
                                t.matmul(z1, l1, rhs, start=(m == 0),
                                         stop=(m == DC - 1), skip_group_check=True)
                                ins = t.matmul(z3, l3, rhs, start=(m == 0),
                                               stop=(m == DC - 1), skip_group_check=True)
                            PE.inc(ins)
                        pe_z = PE.n
                        if step == 7:
                            emit_ST()
                            emit_compact()
                        if ub % 2 == 1 and th == 1:
                            w1_free[q % 2] = (PE, pe_z)
                            w3_free[q % 2] = (PE, pe_z)
                            if q + 2 <= 3:
                                load_w(ex, 1, q + 2)
                                load_w(ex, 3, q + 2)
                            else:
                                load_w(nxt, 1, q - 2)
                                load_w(nxt, 3, q - 2)

                        @block.scalar
                        def _(s, pe_z=pe_z, z1=z1, lh=last_hmul[0]):
                            s.wait_ge(PE.h, pe_z)
                            if lh:
                                s.wait_ge(VE.h, lh)
                            SE.inc(s.activation(out=silb[:], in_=z1, func=AF.Silu))
                        s_sil = SE.n
                        last_sil[0] = s_sil

                        @block.vector
                        def _(v, z3=z3, h_sl=h_sl, s_sil=s_sil):
                            v.wait_ge(SE.h, s_sil)
                            VE.inc(v.tensor_tensor(h_sl, silb[:], z3, OP.mult))
                        last_hmul[0] = VE.n
                        pair_free[step % 2] = (VE, VE.n)
                    arena_free_z[0] = (VE, VE.n)
                    arena_free_z[1] = (VE, st_state["c_tp"])
                    ve_h_all = VE.n
                    pe_shared_z = PE.n

                    for kg in range(4):
                        for tb in range(TBM):
                            par = tb % 2
                            slot = pAR[par][:, 0:512]

                            @block.tensor
                            def _(t, ex=ex, kg=kg, tb=tb, slot=slot,
                                  gate=tuple(slotfree_r[par]),
                                  zgate=tuple(arena_free_z[par]), ve_h_all=ve_h_all):
                                r2 = w_ready[(ex, 2, kg)]
                                t.wait_ge(r2[0].h, r2[1])
                                t.wait_ge(VE.h, ve_h_all)
                                sm, val = gate
                                if sm is not None:
                                    t.wait_ge(sm.h, val)
                                if kg == 0 and tb < 2:
                                    sm2, val2 = zgate
                                    if sm2 is not None:
                                        t.wait_ge(sm2.h, val2)
                                for uc in range(UB):
                                    ins = t.matmul(slot,
                                                   hbuf[:, uc, tb * P:(tb + 1) * P],
                                                   W2T[:, uc, kg % 2, :],
                                                   start=(uc == 0), stop=(uc == UB - 1))
                                PE.inc(ins)
                            pe_r = PE.n
                            if tb == TBM - 1:
                                w2_free[kg % 2] = (PE, pe_r)
                                if kg + 2 <= 3:
                                    load_w(ex, 2, kg + 2)
                                else:
                                    load_w(nxt, 2, kg - 2)

                            @block.vector
                            def _(v, kg=kg, tb=tb, slot=slot, pe_r=pe_r,
                                  aw=accw[(kg, tb)]):
                                v.wait_ge(PE.h, pe_r)
                                v.wait_ge(VE.h, aw)
                                a_sl = acc[:, tb, kg * 512:(kg + 1) * 512]
                                VE.inc(v.tensor_tensor(a_sl, a_sl, slot, OP.add))
                            slotfree_r[par] = (VE, VE.n)
                            accw[(kg, tb)] = VE.n
                    h_readers_done[0] = PE.n
                    arena_free_z = [(VE, VE.n), (VE, VE.n)]
                    ve_acc_done = VE.n

                    mem_gate = (PE, PE.n)
                    alloc_sparse()
                    for _re in range(2):
                        emit_reload(_re, mem_gate)
                        reload_done.add(_re)
                    emit_gather_chain(0)

                    @block.gpsimd
                    def _(g):
                        g.wait_ge(VE.h, ve_acc_done)
                        LDz.inc(g.dma_start(out=out_t[:, 0:TBM, :], in_=acc[:]), 16)
                    ldz_acc = LDz.n
                    rg_free = (LDz, ldz_acc)
                    continue

                slot = ex % 4
                sgslot = ex % 2
                if ex not in reload_done:
                    emit_reload(ex, (GDr, rscat.get(ex - 4, 0)))
                    reload_done.add(ex)
                if ex + 2 < NE and (ex + 2) not in reload_done:
                    emit_reload(ex + 2, mem_gate if ex + 2 < 4 else
                                (GDr, rscat.get(ex - 2, 0)))
                    reload_done.add(ex + 2)
                lr_ve, _sl = lists_ready[ex]
                buf = ex % 2

                if pending_scatter[0] is not None:
                    pending_scatter[0]()
                    pending_scatter[0] = None
                if ex not in gather_done:
                    emit_gather_chain(ex)
                tp_xgt_all = gather_tp[ex]

                ve_sg_box = [0]
                for ub in range(UB):
                    par = ub % 2
                    ps = pAR[par]
                    q = ub // 2
                    uo = (ub % 2) * P
                    z1 = ps[:, 0:CAP]
                    z3 = ps[:, 512:512 + CAP]
                    h_sl = sp["hb"][:, ub, :]

                    @block.tensor
                    def _(t, ex=ex, ub=ub, q=q, uo=uo, ps=ps, z1=z1, z3=z3, buf=buf,
                          gate=tuple(arena_free_z[par]), tp_xgt_all=tp_xgt_all):
                        r1 = w_ready[(ex, 1, q)]
                        r3 = w_ready[(ex, 3, q)]
                        t.wait_ge(r1[0].h, r1[1])
                        t.wait_ge(r3[0].h, r3[1])
                        t.wait_ge(TP.h, tp_xgt_all)
                        sm, val = gate
                        if sm is not None:
                            t.wait_ge(sm.h, val)
                        for m in range(DC):
                            l1 = W1T[:, m, q % 2, uo:uo + P]
                            l3 = W3T[:, m, q % 2, uo:uo + P]
                            rhs = sp["xgT"][:, buf, :, m, :]
                            t.matmul(z1, l1, rhs, start=(m == 0),
                                     stop=(m == DC - 1), skip_group_check=True)
                            ins = t.matmul(z3, l3, rhs, start=(m == 0),
                                           stop=(m == DC - 1), skip_group_check=True)
                        PE.inc(ins)
                    pe_z = PE.n
                    if ub == 0:
                        @block.tensor
                        def _(t, ex=ex, slot=slot, gate=tuple(sg_free)):
                            t.wait_ge(LDl[slot].h, LDl[slot].n)
                            sm, val = gate
                            if sm is not None:
                                t.wait_ge(sm.h, val)
                            PE.inc(t.matmul(pAR[0][:, 1536:1536 + CAP], ones1[0:1, :],
                                            sp["sgrow"][0:1, slot, :],
                                            start=True, stop=True))
                        pe_sg = PE.n

                        @block.vector
                        def _(v, sgslot=sgslot, pe_sg=pe_sg):
                            v.wait_ge(PE.h, pe_sg)
                            VE.inc(v.tensor_copy(sp["Sg"][:, sgslot, :],
                                                 pAR[0][:, 1536:1536 + CAP]))
                        sg_free = (VE, VE.n)
                        ve_sg_box[0] = VE.n
                    if ub % 2 == 1:
                        w1_free[q % 2] = (PE, pe_z)
                        w3_free[q % 2] = (PE, pe_z)
                        if q + 2 <= 3:
                            load_w(ex, 1, q + 2)
                            load_w(ex, 3, q + 2)
                        else:
                            load_w(nxt, 1, q - 2)
                            load_w(nxt, 3, q - 2)
                    if ub == UB - 1:
                        xgt_free[buf] = (PE, pe_z)

                    @block.vector
                    def _(v, pe_z=pe_z, z1=z1, sgslot=sgslot, ls=last_sil[0],
                          ve_sg=ve_sg_box[0]):
                        v.wait_ge(PE.h, pe_z)
                        v.wait_ge(VE.h, ve_sg)
                        if ls:
                            v.wait_ge(SE.h, ls)
                        VE.inc(v.tensor_tensor(tmp1[:, 0:CAP], z1, sp["Sg"][:, sgslot, :],
                                               OP.mult))
                    v_t1 = VE.n

                    @block.scalar
                    def _(s, v_t1=v_t1):
                        s.wait_ge(VE.h, v_t1)
                        SE.inc(s.activation(out=silb[:, 0:CAP], in_=tmp1[:, 0:CAP],
                                            func=AF.Silu))
                    s_sil = SE.n
                    last_sil[0] = s_sil

                    @block.vector
                    def _(v, z3=z3, h_sl=h_sl, sgslot=sgslot, s_sil=s_sil,
                          hrd=h_readers_done[0], lh=last_hmul[0]):
                        if lh:
                            v.wait_ge(VE.h, lh)
                        v_t3 = VE.inc(v.tensor_tensor(tmp3[:, 0:CAP], z3,
                                                      sp["Sg"][:, sgslot, :], OP.mult))
                        v.wait_ge(SE.h, s_sil)
                        v.wait_ge(PE.h, hrd)
                        v.wait_ge(VE.h, v_t3)
                        VE.inc(v.tensor_tensor(h_sl, silb[:, 0:CAP], tmp3[:, 0:CAP],
                                               OP.mult))
                    last_hmul[0] = VE.n
                    arena_free_z[par] = (VE, VE.n)
                ve_h_all = VE.n



                for kg in range(4):
                    for tg in range(NCH):
                        par = (kg * NCH + tg) % 2
                        rps = pAR[par][:, 1024:1536]

                        @block.tensor
                        def _(t, ex=ex, kg=kg, tg=tg, rps=rps,
                              gate=tuple(slotfree_r[par]), ve_h_all=ve_h_all):
                            r2 = w_ready[(ex, 2, kg)]
                            t.wait_ge(r2[0].h, r2[1])
                            t.wait_ge(VE.h, ve_h_all)
                            sm, val = gate
                            if sm is not None:
                                t.wait_ge(sm.h, val)
                            for uc in range(UB):
                                ins = t.matmul(rps, sp["hb"][:, uc, tg * P:(tg + 1) * P],
                                               W2T[:, uc, kg % 2, :],
                                               start=(uc == 0), stop=(uc == UB - 1))
                            PE.inc(ins)
                        pe_r = PE.n
                        if tg == NCH - 1:
                            w2_free[kg % 2] = (PE, pe_r)
                            if kg + 2 <= 3:
                                load_w(ex, 2, kg + 2)
                            else:
                                pend_w2.append((nxt, kg - 2, (PE, pe_r)))

                        @block.vector
                        def _(v, kg=kg, tg=tg, rps=rps, pe_r=pe_r,
                              rgate=tuple(rg_free)):
                            v.wait_ge(PE.h, pe_r)
                            if kg == 0:
                                sm, val = rgate
                                if sm is not None:
                                    v.wait_ge(sm.h, val)
                            VE.inc(v.tensor_copy(sp["rgout"][:, tg, kg * 512:(kg + 1) * 512],
                                                 rps))
                        slotfree_r[par] = (VE, VE.n)
                ve_rg_all = VE.n

                def mk_scatter(ex=ex, slot=slot, ve_rg_all=ve_rg_all,
                               prev=rscat.get(ex - 1, 0), rgate=tuple(rg_free)):
                    def emit():
                        @block.gpsimd
                        def _(g):
                            g.wait_ge(VE.h, ve_rg_all)
                            sm, val = rgate
                            if sm is not None:
                                g.wait_ge(sm.h, val)
                            if prev:
                                g.wait_ge(GDr.h, prev)
                            for tg in range(NCH):
                                g.indirect_dma_start(
                                    out=out,
                                    out_offset=IndirectOffsetOnAxis(
                                        idxu[:, slot, tg:tg + 1], 0),
                                    in_=sp["rgout"][:, tg, :],
                                    in_offset=None,
                                    compute_op=OP.add).then_inc(GDr.h, 16)
                    return emit
                pending_scatter[0] = mk_scatter()
                GDr.n += NCH * 16
                rscat[ex] = GDr.n
                rg_free = (GDr, GDr.n)
                for tex, tkg, tgate in pend_w2:
                    load_w(tex, 2, tkg, gate_override=tgate)
                pend_w2 = []
                h_readers_done[0] = PE.n

            if pending_scatter[0] is not None:
                pending_scatter[0]()
                pending_scatter[0] = None
            ve_final = VE.n

            @block.gpsimd
            def _(g):
                g.wait_ge(GDr.h, GDr.n)
                g.wait_ge(LDz.h, LDz.n)
                g.wait_ge(LDo.h, LDo.n)

    return nc


def _host_inputs(L0_attn_x, L0_ffn_gate_w, L0_ffn_w1, L0_ffn_w2, L0_ffn_w3,
                 L0_ffn_ws1, L0_ffn_ws2, L0_ffn_ws3):
    xf = np.ascontiguousarray(np.asarray(L0_attn_x, dtype=np.float32))
    gw = np.ascontiguousarray(np.asarray(L0_ffn_gate_w, dtype=np.float32))
    w1 = np.ascontiguousarray(np.asarray(L0_ffn_w1, dtype=np.float32))
    w2 = np.ascontiguousarray(np.asarray(L0_ffn_w2, dtype=np.float32))
    w3 = np.ascontiguousarray(np.asarray(L0_ffn_w3, dtype=np.float32))
    ws1 = np.ascontiguousarray(np.asarray(L0_ffn_ws1, dtype=np.float32))
    ws2 = np.ascontiguousarray(np.asarray(L0_ffn_ws2, dtype=np.float32))
    ws3 = np.ascontiguousarray(np.asarray(L0_ffn_ws3, dtype=np.float32))
    gwT = np.ascontiguousarray(gw.T)
    w1t = np.ascontiguousarray(w1.transpose(0, 2, 1))
    w2t = np.ascontiguousarray(w2.transpose(0, 2, 1))
    w3t = np.ascontiguousarray(w3.transpose(0, 2, 1))
    ws1t = np.ascontiguousarray(ws1.T)
    ws2t = np.ascontiguousarray(ws2.T)
    ws3t = np.ascontiguousarray(ws3.T)
    in_maps = []
    for c in range(8):
        b, h = c // 2, c % 2
        mine = xf[b, h * 1024:(h + 1) * 1024]
        other = xf[b, (1 - h) * 1024:(2 - h) * 1024]
        xc = np.ascontiguousarray(np.concatenate([mine, other], axis=0))
        in_maps.append(dict(x=xc, gwT=gwT, w1t=w1t, w2t=w2t, w3t=w3t,
                            ws1t=ws1t, ws2t=ws2t, ws3t=ws3t))
    return in_maps


def kernel(L0_attn_x, L0_ffn_gate_w, L0_ffn_w1, L0_ffn_w2, L0_ffn_w3,
           L0_ffn_ws1, L0_ffn_ws2, L0_ffn_ws3):
    in_maps = _host_inputs(L0_attn_x, L0_ffn_gate_w, L0_ffn_w1, L0_ffn_w2,
                           L0_ffn_w3, L0_ffn_ws1, L0_ffn_ws2, L0_ffn_ws3)
    nc = build()
    res = run_bass_kernel_spmd(nc, in_maps, core_ids=list(range(8)))
    out = np.empty((4, 2048, 2048), dtype=np.float32)
    for c in range(8):
        b, h = c // 2, c % 2
        out[b, h * 1024:(h + 1) * 1024] = res.results[c]["out"][0:1024]
    return out


# version 24
# speedup vs baseline: 1.0605x; 1.0137x over previous
import numpy as np
from contextlib import ExitStack

import concourse.bass as bass
import concourse.mybir as mybir
from concourse.bass_utils import run_bass_kernel_spmd
from concourse.alu_op_type import AluOpType as OP
from concourse.bass import IndirectOffsetOnAxis

F32 = mybir.dt.float32
BF16 = mybir.dt.bfloat16
U32 = mybir.dt.uint32
AF = mybir.ActivationFunctionType
AX = mybir.AxisListType

P = 128
D = 2048
SEQ = 2048
TB = 16
TBM = 8
DC = 16
H = 1024
UB = 8
NE = 8
NEXP = 9
EPS = 1e-5
INV_D = 1.0 / D

CAP = 384
NCH = CAP // P
W1WIN = 4
W2WIN = 4


class Sem:
    def __init__(self, h):
        self.h = h
        self.n = 0

    def inc(self, instr, k=1):
        instr.then_inc(self.h, k)
        self.n += k
        return self.n


def flat2(ap):
    return ap.rearrange("p a b -> p (a b)")


def build(want_taps=(), stage="full"):
    nc = bass.Bass("TRN2", target_bir_lowering=False, debug=False)

    x = nc.dram_tensor("x", [SEQ, D], F32, kind="ExternalInput").ap()
    gwT = nc.dram_tensor("gwT", [D, NE], F32, kind="ExternalInput").ap()
    w1t = nc.dram_tensor("w1t", [NE, D, H], F32, kind="ExternalInput").ap()
    w2t = nc.dram_tensor("w2t", [NE, H, D], F32, kind="ExternalInput").ap()
    w3t = nc.dram_tensor("w3t", [NE, D, H], F32, kind="ExternalInput").ap()
    ws1t = nc.dram_tensor("ws1t", [D, H], F32, kind="ExternalInput").ap()
    ws2t = nc.dram_tensor("ws2t", [H, D], F32, kind="ExternalInput").ap()
    ws3t = nc.dram_tensor("ws3t", [D, H], F32, kind="ExternalInput").ap()
    out = nc.dram_tensor("out", [TBM * P + CAP + P, D], F32, kind="ExternalOutput").ap()
    plist = nc.dram_tensor("plist", [NE, CAP + TBM * P, 2], F32, kind="Internal").ap()
    xfd = nc.dram_tensor("xfd", [TBM * P, D], BF16, kind="Internal").ap()

    x_t = x.rearrange("(tb p) d -> p tb d", p=P)
    out_t = out.rearrange("(tb p) d -> p tb d", p=P)

    with ExitStack() as top:
        e = top.enter_context
        acc = e(nc.sbuf_tensor("acc", [P, TBM, D], F32))
        identt = e(nc.sbuf_tensor("identt", [P, P], F32))
        iota8 = e(nc.sbuf_tensor("iota8", [P, NE], F32))
        gT = e(nc.sbuf_tensor("gT", [P, DC, NE], BF16))
        gws = e(nc.sbuf_tensor("gws", [P, DC, NE], F32))
        st_sum = e(nc.sbuf_tensor("st_sum", [P, TB], F32))
        st_ssq = e(nc.sbuf_tensor("st_ssq", [P, TB], F32))
        st_var = e(nc.sbuf_tensor("st_var", [P, TB], F32))
        st_rstd = e(nc.sbuf_tensor("st_rstd", [P, TB], F32))
        st_nmr = e(nc.sbuf_tensor("st_nmr", [P, TB], F32))
        st_tmp = e(nc.sbuf_tensor("st_tmp", [P, TB], F32))
        sqscr = e(nc.sbuf_tensor("sqscr", [P, D], F32))
        sm_mxn = e(nc.sbuf_tensor("sm_mxn", [P, 2], F32))
        sm_rs = e(nc.sbuf_tensor("sm_rs", [P, 2], F32))
        sm_lnr = e(nc.sbuf_tensor("sm_lnr", [P, 2], F32))
        sm_b2 = e(nc.sbuf_tensor("sm_b2", [P, 2], F32))

        pA = e(nc.psum_tensor("pA", [P, 512 * 4], F32))
        pB = e(nc.psum_tensor("pB", [P, 512 * 4], F32))
        pAR = [pA, pB]

        LDa = Sem(e(nc.semaphore("LDa")))
        LDx = [Sem(e(nc.semaphore(f"LDx{i}"))) for i in range(2)]
        LDg = Sem(e(nc.semaphore("LDg")))
        LDw1 = [Sem(e(nc.semaphore(f"LDw1_{i}"))) for i in range(2)]
        LDw3 = [Sem(e(nc.semaphore(f"LDw3_{i}"))) for i in range(2)]
        LDw2 = [Sem(e(nc.semaphore(f"LDw2_{i}"))) for i in range(2)]
        LDo = Sem(e(nc.semaphore("LDo")))
        LDf = [Sem(e(nc.semaphore(f"LDf{i}"))) for i in range(2)]
        GDc = [Sem(e(nc.semaphore(f"GDc{i}"))) for i in range(2)]
        LDl = [Sem(e(nc.semaphore(f"LDl{i}"))) for i in range(4)]
        LDxg = [Sem(e(nc.semaphore(f"LDxg{i}"))) for i in range(NCH)]
        GDr = Sem(e(nc.semaphore("GDr")))
        LDz = Sem(e(nc.semaphore("LDz")))
        GP = Sem(e(nc.semaphore("GP")))
        TP = Sem(e(nc.semaphore("TP")))
        PE = Sem(e(nc.semaphore("PE")))
        VE = Sem(e(nc.semaphore("VE")))
        SE = Sem(e(nc.semaphore("SE")))

        block = e(nc.Block())

        def tap(name, ap, waits=()):
            if name not in want_taps:
                return
            t = nc.dram_tensor("tap_" + name, list(ap.shape), ap.dtype,
                               kind="ExternalOutput").ap()

            @block.gpsimd
            def _(g, ap=ap, t=t, waits=tuple(waits)):
                for s, v in waits:
                    g.wait_ge(s.h, v)
                LDo.inc(g.dma_start(out=t, in_=ap), 16)

        @block.gpsimd
        def _(g):
            GP.inc(g.iota(iota8[:], pattern=[[1, NE]], base=0, channel_multiplier=0,
                          allow_small_or_imprecise_dtypes=True), 1)
            GP.inc(g.iota(identt[:], pattern=[[0, P]], base=0, channel_multiplier=1,
                          allow_small_or_imprecise_dtypes=True), 1)
            GP.inc(g.iota(sqscr[:, 0:P], pattern=[[1, P]], base=0, channel_multiplier=0,
                          allow_small_or_imprecise_dtypes=True), 1)
            LDg.inc(g.dma_start(out=gws[:], in_=gwT.rearrange("(m p) e -> p m e", p=P)), 16)
            LDa.inc(g.dma_start(out=acc[:], in_=x_t[:, 0:TBM, :]), 16)
        gp_consts = GP.n
        ld_gws = (LDg, LDg.n)
        ld_acc = (LDa, LDa.n)

        @block.vector
        def _(v):
            v.wait_ge(GP.h, gp_consts)
            v.wait_ge(ld_gws[0].h, ld_gws[1])
            VE.inc(v.tensor_tensor(identt[:], identt[:], sqscr[:, 0:P], OP.is_equal))
            VE.inc(v.tensor_copy(gT[:], gws[:]))
        ve_consts = VE.n

        st_m2 = e(nc.sbuf_tensor("st_m2", [P, TB], F32))

        def emit_ln_group(items, ld_waits=None, scalar_pre=None, vector_pre=None):
            cols = [c for c, _, _ in items]
            lo, hi = min(cols), max(cols) + 1
            csl = slice(lo, hi)

            @block.vector
            def _(v):
                if vector_pre is not None:
                    v.wait_ge(vector_pre[0].h, vector_pre[1])
                for i, (c, src_ap, _) in enumerate(items):
                    if ld_waits is not None and ld_waits[i] is not None:
                        v.wait_ge(ld_waits[i][0].h, ld_waits[i][1])
                    VE.inc(v.tensor_reduce(st_sum[:, c:c + 1], src_ap, AX.X, OP.add,
                                           negate=True))
            v_red = VE.n

            @block.scalar
            def _(s):
                if scalar_pre is not None:
                    s.wait_ge(scalar_pre[0].h, scalar_pre[1])
                for i, (c, src_ap, _) in enumerate(items):
                    if ld_waits is not None and ld_waits[i] is not None:
                        s.wait_ge(ld_waits[i][0].h, ld_waits[i][1])
                    if i > 0:
                        s.wait_ge(SE.h, SE.n)
                    SE.inc(s.activation(out=sqscr[:], in_=src_ap, func=AF.Square,
                                        accum_out=st_ssq[:, c:c + 1]))
            s_sq = SE.n

            @block.vector
            def _(v):
                v.wait_ge(VE.h, v_red)
                v.wait_ge(SE.h, s_sq)
                VE.inc(v.tensor_scalar_mul(st_sum[:, csl], st_sum[:, csl], INV_D))
                VE.inc(v.tensor_scalar_mul(st_tmp[:, csl], st_ssq[:, csl], INV_D))
            v_d1 = VE.n

            @block.vector
            def _(v):
                v.wait_ge(VE.h, v_d1)
                VE.inc(v.tensor_tensor(st_m2[:, csl], st_sum[:, csl], st_sum[:, csl],
                                       OP.mult))
            v_d2 = VE.n

            @block.vector
            def _(v):
                v.wait_ge(VE.h, v_d2)
                VE.inc(v.tensor_tensor(st_var[:, csl], st_tmp[:, csl], st_m2[:, csl],
                                       OP.subtract))
            v_d3 = VE.n

            @block.vector
            def _(v):
                v.wait_ge(VE.h, v_d3)
                VE.inc(v.tensor_scalar_add(st_var[:, csl], st_var[:, csl], EPS))
            v_d4 = VE.n

            @block.scalar
            def _(s):
                s.wait_ge(VE.h, v_d4)
                SE.inc(s.activation(out=st_tmp[:, csl], in_=st_var[:, csl], func=AF.Sqrt))
            s_std = SE.n

            @block.vector
            def _(v):
                v.wait_ge(SE.h, s_std)
                VE.inc(v.reciprocal(st_rstd[:, csl], st_tmp[:, csl]))
            v_rec = VE.n

            @block.vector
            def _(v):
                v.wait_ge(VE.h, v_rec)
                VE.inc(v.tensor_tensor(st_nmr[:, csl], st_sum[:, csl], st_rstd[:, csl],
                                       OP.mult))
            v_nmr = VE.n

            outs = []
            for c, src_ap, dst_ap in items:
                @block.scalar
                def _(s, c=c, src_ap=src_ap, dst_ap=dst_ap):
                    s.wait_ge(VE.h, v_nmr)
                    SE.inc(s.activation(out=dst_ap, in_=src_ap, func=AF.Identity,
                                        bias=st_nmr[:, c:c + 1],
                                        scale=st_rstd[:, c:c + 1]))
                outs.append(SE.n)
            return outs

        def emit_ln(src_ap, dst_ap, col, ld_wait=None, scalar_pre=None, vector_pre=None):
            return emit_ln_group([(col, src_ap, dst_ap)], ld_waits=[ld_wait],
                                 scalar_pre=scalar_pre, vector_pre=vector_pre)[0]

        arena_free = [(None, 0), (None, 0)]
        with ExitStack() as attn_scope:
            ea = attn_scope.enter_context
            Xn = ea(nc.sbuf_tensor("Xn", [P, TB, D], BF16))
            XnT = ea(nc.sbuf_tensor("XnT", [P, DC, TBM, P], BF16))
            xring = [ea(nc.sbuf_tensor(f"xring{i}", [P, D], F32)) for i in range(2)]
            Asb = [ea(nc.sbuf_tensor("Asb0", [P, D], BF16))]
            ATsl = [ea(nc.sbuf_tensor("ATsl0", [P, DC, 4, P], BF16))]

            s_norm = emit_ln_group(
                [(tb, acc[:, tb, :], Xn[:, tb, :]) for tb in range(TBM)],
                ld_waits=[ld_acc] * TBM,
                scalar_pre=(VE, ve_consts), vector_pre=None)
            ring_free = [(None, 0), (None, 0)]
            for tb in range(TBM, TB):
                slot = tb % 2
                @block.gpsimd
                def _(g, tb=tb, slot=slot, gate=tuple(ring_free[slot])):
                    sm, val = gate
                    if sm is not None:
                        g.wait_ge(sm.h, val)
                    LDx[slot].inc(g.dma_start(out=xring[slot][:], in_=x_t[:, tb, :]), 16)
                sn = emit_ln(xring[slot][:], Xn[:, tb, :], tb,
                             ld_wait=(LDx[slot], LDx[slot].n))
                s_norm.append(sn)
                ring_free[slot] = (SE, sn)
            s_ln1_all = SE.n
            tap("ln1", Xn[:], waits=[(SE, s_ln1_all)])

            if stage == "ln1":
                @block.gpsimd
                def _(g):
                    g.wait_ge(LDo.h, LDo.n)
                return nc

            for tb in range(TBM):
                @block.sync
                def _(sy, tb=tb):
                    sy.wait_ge(SE.h, s_norm[tb])
                    TP.inc(sy.dma_start(out=XnT[:, :, tb, :], in_=Xn[:, tb, :],
                                        transpose=True), 16)
                    sy.wait_ge(TP.h, TP.n)
            tp_xnt = TP.n
            tap("xnt", XnT[:], waits=[(TP, tp_xnt)])

            if stage == "xnt":
                @block.gpsimd
                def _(g):
                    g.wait_ge(LDo.h, LDo.n)
                return nc

            accw = {}
            asb_free = [(None, 0)]
            atsl_free = [(None, 0)]
            tp_at = {}
            for g_r in range(4):
                for ibl in range(4):
                    gi = 4 * g_r + ibl
                    par = gi % 2
                    ps = pAR[par]

                    @block.tensor
                    def _(t, gi=gi, ps=ps, gate=tuple(arena_free[par])):
                        if gi == 2:
                            t.wait_ge(SE.h, s_ln1_all)
                        sm, val = gate
                        if sm is not None:
                            t.wait_ge(sm.h, val)
                        for ct in range(TB):
                            if gi < 2:
                                t.wait_ge(SE.h, s_norm[ct])
                            for jr in range(4):
                                ins = t.matmul(ps[:, jr * 512:(jr + 1) * 512],
                                               Xn[:, ct, gi * P:(gi + 1) * P],
                                               Xn[:, ct, jr * 512:(jr + 1) * 512],
                                               start=(ct == 0), stop=(ct == TB - 1),
                                               skip_group_check=True)
                        PE.inc(ins)
                    pe_gram = PE.n

                    if stage == "gram0" and gi == 0:
                        @block.vector
                        def _(v, ps=ps, pe_gram=pe_gram):
                            v.wait_ge(PE.h, pe_gram)
                            VE.inc(v.tensor_copy(sqscr[:], ps[:]))
                        tap("G0", sqscr[:], waits=[(VE, VE.n)])

                        @block.gpsimd
                        def _(g):
                            g.wait_ge(LDo.h, LDo.n)
                        return nc

                    mxn = sm_mxn[:, par:par + 1]
                    rs = sm_rs[:, par:par + 1]
                    lnr = sm_lnr[:, par:par + 1]
                    b2 = sm_b2[:, par:par + 1]

                    @block.vector
                    def _(v, ps=ps, pe_gram=pe_gram, mxn=mxn):
                        v.wait_ge(PE.h, pe_gram)
                        VE.inc(v.tensor_reduce(mxn, ps[:], AX.X, OP.max, negate=True))
                    v_mxn = VE.n

                    @block.scalar
                    def _(s, gi=gi, ps=ps, v_mxn=v_mxn, mxn=mxn, rs=rs,
                          gate=tuple(asb_free[0])):
                        s.wait_ge(VE.h, v_mxn)
                        sm, val = gate
                        if sm is not None:
                            s.wait_ge(sm.h, val)
                        SE.inc(s.activation(out=Asb[0][:], in_=ps[:], func=AF.Exp,
                                            bias=mxn, accum_out=rs))
                    s_e1 = SE.n

                    @block.scalar
                    def _(s, s_e1=s_e1, rs=rs, lnr=lnr):
                        s.wait_ge(SE.h, s_e1)
                        SE.inc(s.activation(out=lnr, in_=rs, func=AF.Ln))
                    s_ln = SE.n

                    @block.vector
                    def _(v, s_ln=s_ln, mxn=mxn, lnr=lnr, b2=b2):
                        v.wait_ge(SE.h, s_ln)
                        VE.inc(v.tensor_tensor(b2, mxn, lnr, OP.subtract))
                    v_b2 = VE.n

                    @block.scalar
                    def _(s, gi=gi, ps=ps, v_b2=v_b2, b2=b2):
                        s.wait_ge(VE.h, v_b2)
                        SE.inc(s.activation(out=Asb[0][:], in_=ps[:], func=AF.Exp,
                                            bias=b2))
                    s_e2 = (SE, SE.n)
                    arena_free[par] = s_e2

                    if stage == "gram1" and gi == 0:
                        tap("A0", Asb[0][:], waits=[(SE, s_e2)])

                        @block.gpsimd
                        def _(g):
                            g.wait_ge(LDo.h, LDo.n)
                        return nc

                    @block.sync
                    def _(sy, g_r=g_r, ibl=ibl, gi=gi, s_e2=s_e2,
                          gate=tuple(atsl_free[0])):
                        sy.wait_ge(s_e2[0].h, s_e2[1])
                        sm, val = gate
                        if sm is not None and ibl == 0:
                            sy.wait_ge(sm.h, val)
                        TP.inc(sy.dma_start(out=ATsl[0][:, :, ibl, :],
                                            in_=Asb[0][:], transpose=True), 16)
                        sy.wait_ge(TP.h, TP.n)
                    asb_free[0] = (TP, TP.n)

                    if stage == "at1" and gi == 0:
                        tap("AT0", ATsl[0][:], waits=[(TP, TP.n)])

                        @block.gpsimd
                        def _(g):
                            g.wait_ge(LDo.h, LDo.n)
                        return nc
                tp_at[g_r] = TP.n

                for tb in range(TBM):
                    par = tb % 2
                    slot = pAR[par][:, 0:512]

                    @block.tensor
                    def _(t, g_r=g_r, tb=tb, slot=slot, gate=tuple(arena_free[par])):
                        t.wait_ge(TP.h, tp_at[g_r])
                        if g_r == 0 and tb == 0:
                            t.wait_ge(TP.h, tp_xnt)
                        sm, val = gate
                        if sm is not None:
                            t.wait_ge(sm.h, val)
                        for m in range(DC):
                            ins = t.matmul(slot, XnT[:, m, tb, :],
                                           flat2(ATsl[0][:, m, :, :]),
                                           start=(m == 0), stop=(m == DC - 1))
                        PE.inc(ins)
                    pe_out = PE.n

                    @block.vector
                    def _(v, g_r=g_r, tb=tb, slot=slot, pe_out=pe_out):
                        v.wait_ge(PE.h, pe_out)
                        a_sl = acc[:, tb, g_r * 512:(g_r + 1) * 512]
                        VE.inc(v.tensor_tensor(a_sl, a_sl, slot, OP.add))
                    arena_free[par] = (VE, VE.n)
                    accw[(g_r, tb)] = VE.n
                atsl_free[0] = (PE, PE.n)
            ve_x1 = VE.n
            tap("x1", acc[:], waits=[(VE, ve_x1)])


        if stage == "attn":
            @block.gpsimd
            def _(g):
                g.wait_ge(LDo.h, LDo.n)
            return nc
        with ExitStack() as ffn_scope:
            ef = ffn_scope.enter_context
            scores = ef(nc.sbuf_tensor("scores", [P, TBM, NE], F32))
            topv = ef(nc.sbuf_tensor("topv", [P, TBM, NE], F32))
            topi = ef(nc.sbuf_tensor("topi", [P, TBM, NE], U32))
            topf = ef(nc.sbuf_tensor("topf", [P, TBM, NE], F32))
            Ssc = ef(nc.sbuf_tensor("Ssc", [P, TBM, NE], F32))
            STb = ef(nc.sbuf_tensor("STb", [NE, TBM * P], BF16))
            W1T = ef(nc.sbuf_tensor("W1T", [P, DC, 2, 256], BF16))
            W3T = ef(nc.sbuf_tensor("W3T", [P, DC, 2, 256], BF16))
            W2T = ef(nc.sbuf_tensor("W2T", [P, UB, 2, 512], BF16))
            tmp1 = ef(nc.sbuf_tensor("tmp1", [P, 512], F32))
            tmp3 = ef(nc.sbuf_tensor("tmp3", [P, 512], F32))
            silb = ef(nc.sbuf_tensor("silb", [P, 512], F32))
            posu = ef(nc.sbuf_tensor("posu", [P, TBM, NE], U32))
            scpm = ef(nc.sbuf_tensor("scpm", [P, TBM, NE], F32))
            vals = ef(nc.sbuf_tensor("vals", [P, TBM], F32))
            pairb = ef(nc.sbuf_tensor("pairb", [P, 2, TBM, 2], F32))
            pref = ef(nc.sbuf_tensor("pref", [P, NCH, 2], F32))
            ones1 = ef(nc.sbuf_tensor("ones1", [1, P], F32))
            idxf = ef(nc.sbuf_tensor("idxf", [P, 4, NCH, 2], F32))
            idxu = ef(nc.sbuf_tensor("idxu", [P, 4, NCH], U32))
            gmin = ef(nc.sbuf_tensor("gmin", [P, 4, NCH], U32))
            e3t = ef(nc.sbuf_tensor("e3t", [P, TBM, NE], F32))
            sc_done = {}
            lists_ready = {}



            from contextlib import ExitStack as _ES
            xf_scope = _ES()
            cw_mask = xf_scope.enter_context(nc.sbuf_tensor("cw_mask", [NE, TBM * P], F32))
            cw_pos = xf_scope.enter_context(nc.sbuf_tensor("cw_pos", [NE, TBM * P], F32))
            cw_trash = xf_scope.enter_context(nc.sbuf_tensor("cw_trash", [NE, TBM * P], F32))
            cw_ones = xf_scope.enter_context(nc.sbuf_tensor("cw_ones", [NE, TBM * P], F32))
            hbuf = xf_scope.enter_context(nc.sbuf_tensor("hbuf", [P, UB, H], BF16))
            xfT = xf_scope.enter_context(nc.sbuf_tensor("xfT", [P, DC, TBM, P], BF16))
            xfring = xf_scope.enter_context(nc.sbuf_tensor("xfring", [P, 2, D], BF16))
            sp = {}



            def alloc_sparse():
                xf_scope.close()
                sp["sgrow"] = ffn_scope.enter_context(nc.sbuf_tensor("sgrow", [1, 4, CAP], F32))
                sp["Sg"] = ffn_scope.enter_context(nc.sbuf_tensor("Sg", [P, 2, CAP], F32))
                sp["xg"] = ffn_scope.enter_context(nc.sbuf_tensor("xg", [P, NCH, D], BF16))
                sp["xgT"] = ffn_scope.enter_context(nc.sbuf_tensor("xgT", [P, 2, NCH, DC, P], BF16))
                sp["hb"] = ffn_scope.enter_context(nc.sbuf_tensor("hb", [P, UB, CAP], BF16))
                sp["rgout"] = ffn_scope.enter_context(nc.sbuf_tensor("rgout", [P, NCH, D], BF16))

            tp_xf = []
            gate_prev = (None, 0)
            for grp in range(4):
                tbs = range(grp * 2, grp * 2 + 2)
                if grp > 0:
                    @block.scalar
                    def _(s, g0=LDf[0].n, g1=LDf[1].n):
                        s.wait_ge(LDf[0].h, g0)
                        s.wait_ge(LDf[1].h, g1)
                sns = emit_ln_group(
                    [(tb, acc[:, tb, :], xfring[:, tb % 2, :]) for tb in tbs],
                    scalar_pre=(VE, ve_x1) if grp == 0 else gate_prev,
                    vector_pre=(VE, ve_x1) if grp == 0 else None)
                for j, tb in enumerate(tbs):
                    @block.sync
                    def _(sy, tb=tb, sn=sns[j]):
                        sy.wait_ge(SE.h, sn)
                        TP.inc(sy.dma_start(out=xfT[:, :, tb, :],
                                            in_=xfring[:, tb % 2, :],
                                            transpose=True), 16)
                        sy.wait_ge(TP.h, TP.n)
                    tp_xf.append(TP.n)

                    @block.gpsimd
                    def _(g, tb=tb, sn=sns[j]):
                        g.wait_ge(SE.h, sn)
                        LDf[tb % 2].inc(g.dma_start(
                            out=xfd.rearrange("(tb p) d -> p tb d", p=P)[:, tb, :],
                            in_=xfring[:, tb % 2, :]), 16)
                gate_prev = (TP, TP.n)
            tp_xfT = TP.n
            ldf_all = (LDf[0].n, LDf[1].n)

            gate_arena0 = tuple(arena_free[0])
            gate_arena1 = tuple(arena_free[1])
            pe_gate = []
            s_gate_evac = []
            for tb in range(TBM):
                @block.tensor
                def _(t, tb=tb):
                    t.wait_ge(VE.h, ve_consts)
                    t.wait_ge(TP.h, tp_xf[tb])
                    if tb < 2:
                        sm, val = gate_arena1
                        if sm is not None:
                            t.wait_ge(sm.h, val)
                    if tb >= 2:
                        t.wait_ge(SE.h, s_gate_evac[tb - 2])
                    for m in range(DC):
                        ins = t.matmul(pAR[1][:, (tb % 2) * 512:(tb % 2) * 512 + NE],
                                       xfT[:, m, tb, :], gT[:, m, 0:NE],
                                       start=(m == 0), stop=(m == DC - 1))
                    PE.inc(ins)
                pe_gate.append(PE.n)

                @block.scalar
                def _(s, tb=tb):
                    s.wait_ge(PE.h, pe_gate[tb])
                    SE.inc(s.activation(out=scores[:, tb, :],
                                        in_=pAR[1][:, (tb % 2) * 512:(tb % 2) * 512 + NE],
                                        func=AF.Sigmoid))
                s_gate_evac.append(SE.n)
            s_scores = SE.n
            tap("scores", scores[:], waits=[(SE, s_scores)])

            if stage == "scores":
                @block.gpsimd
                def _(g):
                    g.wait_ge(LDo.h, LDo.n)
                return nc

            @block.vector
            def _(v):
                v.wait_ge(SE.h, s_scores)
                for tb in range(TBM):
                    VE.inc(v.max(topv[:, tb, :], scores[:, tb, :]))
            v_max = VE.n

            @block.vector
            def _(v):
                v.wait_ge(VE.h, v_max)
                for tb in range(TBM):
                    VE.inc(v.max_index(topi[:, tb, :], topv[:, tb, :], scores[:, tb, :]))
            v_mi = VE.n

            @block.vector
            def _(v):
                v.wait_ge(VE.h, v_mi)
                v.wait_ge(GP.h, 1)
                for tb in range(TBM):
                    VE.inc(v.tensor_copy(topf[:, tb, :], topi[:, tb, :]))
            v_tf = VE.n

            @block.vector
            def _(v):
                v.wait_ge(VE.h, v_tf)
                for tb in range(TBM):
                    VE.inc(v.tensor_scalar(Ssc[:, tb, :], iota8[:], topf[:, tb, 0:1],
                                           None, OP.is_equal))
                    VE.inc(v.tensor_scalar(e3t[:, tb, :], iota8[:], topf[:, tb, 1:2],
                                           None, OP.is_equal))
            v_eq = VE.n

            @block.vector
            def _(v):
                v.wait_ge(VE.h, v_eq)
                for tb in range(TBM):
                    VE.inc(v.tensor_scalar(Ssc[:, tb, :], Ssc[:, tb, :], topv[:, tb, 0:1],
                                           None, OP.mult))
            v_s1 = VE.n

            @block.vector
            def _(v):
                v.wait_ge(VE.h, v_s1)
                for tb in range(TBM):
                    VE.inc(v.scalar_tensor_tensor(Ssc[:, tb, :], e3t[:, tb, :],
                                                  topv[:, tb, 1:2], Ssc[:, tb, :],
                                                  OP.mult, OP.add))
            v_S = VE.n
            tap("S", Ssc[:], waits=[(VE, v_S)])

            st_state = {}

            def emit_ST():
                ve_st = []
                for tb in range(TBM):
                    @block.tensor
                    def _(t, tb=tb):
                        t.wait_ge(VE.h, v_S)
                        t.wait_ge(VE.h, ve_consts)
                        if tb == 0:
                            sm, val = gate_arena1
                            if sm is not None:
                                t.wait_ge(sm.h, val)
                        else:
                            t.wait_ge(VE.h, ve_st[tb - 1])
                        PE.inc(t.transpose(pAR[1][0:NE, 1024:1024 + P], Ssc[:, tb, :],
                                           identt[:]))
                    pe_st = PE.n

                    @block.vector
                    def _(v, tb=tb, pe_st=pe_st):
                        v.wait_ge(PE.h, pe_st)
                        VE.inc(v.tensor_copy(STb[0:NE, tb * P:(tb + 1) * P],
                                             pAR[1][0:NE, 1024:1024 + P]))
                    ve_st.append(VE.n)
                st_state["v_stb"] = VE.n

            def emit_compact():
                v_stb = st_state["v_stb"]

                @block.vector
                def _(v):
                    v.wait_ge(VE.h, v_stb)
                    v.wait_ge(GP.h, gp_sparse)
                    VE.inc(v.memset(cw_ones[:], 1.0))
                    VE.inc(v.tensor_single_scalar(cw_mask[:], STb[:], 0.0, OP.is_gt))
                c_m = VE.n

                @block.vector
                def _(v):
                    v.wait_ge(VE.h, c_m)
                    VE.inc(v.tensor_tensor_scan(cw_pos[:], cw_ones[:], cw_mask[:],
                                                0.0, OP.mult, OP.add))
                c_p1 = VE.n

                @block.vector
                def _(v):
                    v.wait_ge(VE.h, c_p1)
                    VE.inc(v.tensor_single_scalar(cw_pos[:], cw_pos[:], -1.0, OP.add))
                    VE.inc(v.tensor_copy(cw_ones[:], STb[:]))
                c_p2 = VE.n

                @block.vector
                def _(v):
                    v.wait_ge(VE.h, c_p2)
                    VE.inc(v.tensor_tensor(cw_pos[:], cw_pos[:], cw_trash[:], OP.subtract))
                c_p3 = VE.n

                @block.vector
                def _(v):
                    v.wait_ge(VE.h, c_p3)
                    VE.inc(v.tensor_tensor(cw_pos[:], cw_mask[:], cw_pos[:], OP.mult))
                c_p4 = VE.n

                @block.vector
                def _(v):
                    v.wait_ge(VE.h, c_p4)
                    VE.inc(v.tensor_tensor(cw_pos[:], cw_pos[:], cw_trash[:], OP.add))
                c_pos = VE.n

                prev_ev = [0]
                for tb in range(TBM):
                    @block.tensor
                    def _(t, tb=tb):
                        t.wait_ge(VE.h, c_pos)
                        if tb > 0:
                            t.wait_ge(VE.h, prev_ev[0])
                        t.transpose(pAR[1][:, 1024:1024 + NE],
                                    cw_pos[0:NE, tb * P:(tb + 1) * P], identt[0:NE, 0:NE])
                        PE.inc(t.transpose(pAR[1][:, 1536:1536 + NE],
                                           cw_ones[0:NE, tb * P:(tb + 1) * P],
                                           identt[0:NE, 0:NE]))
                    pe_t = PE.n

                    @block.vector
                    def _(v, tb=tb, pe_t=pe_t):
                        v.wait_ge(PE.h, pe_t)
                        VE.inc(v.tensor_copy(posu[:, tb, :], pAR[1][:, 1024:1024 + NE]))
                        VE.inc(v.tensor_copy(scpm[:, tb, :], pAR[1][:, 1536:1536 + NE]))
                    prev_ev[0] = VE.n
                c_tp = VE.n
                st_state["c_tp"] = c_tp

                for ex in range(NE):
                    slot = ex % 2

                    @block.gpsimd
                    def _(g, ex=ex):
                        g.wait_ge(GP.h, gp_sparse)
                        LDz.inc(g.dma_start(
                            out=plist[ex, 0:CAP].rearrange("(c p) w -> p c w", p=P),
                            in_=pref[:]), 16)
                    ld_pf = LDz.n

                    @block.vector
                    def _(v, ex=ex, slot=slot, gate=GDc[slot].n):
                        v.wait_ge(VE.h, c_tp)
                        if gate:
                            v.wait_ge(GDc[slot].h, gate)
                        VE.inc(v.tensor_copy(
                            pairb[:, slot, :, 0:1],
                            vals[:].rearrange("p (c one) -> p c one", one=1)))
                        VE.inc(v.tensor_copy(pairb[:, slot, :, 1:2],
                                             scpm[:, :, ex:ex + 1]))
                    v_pair = VE.n

                    @block.gpsimd
                    def _(g, ex=ex, slot=slot, v_pair=v_pair, ld_pf=ld_pf):
                        g.wait_ge(VE.h, v_pair)
                        g.wait_ge(LDz.h, ld_pf)
                        for tb in range(TBM):
                            GDc[slot].inc(g.indirect_dma_start(
                                out=plist.rearrange("E c w -> (E c) w"),
                                out_offset=IndirectOffsetOnAxis(posu[:, tb, ex:ex + 1], 0),
                                in_=pairb[:, slot, tb, :],
                                in_offset=None,
                                element_offset=ex * (CAP + TBM * P) * 2), 16)
                    sc_done[ex] = (GDc[slot], GDc[slot].n)

            gather_done = set()
            gather_tp = {}

            def emit_gather_chain(ex):
                gather_done.add(ex)
                slot = ex % 4
                buf = ex % 2
                lr_ve, _sl = lists_ready[ex]
                for c in range(NCH):
                    @block.gpsimd
                    def _(g, ex=ex, c=c, slot=slot, lr_ve=lr_ve,
                          gate=tuple(xg_free[0]), mg=tuple(mem_gate)):
                        g.wait_ge(VE.h, lr_ve)
                        if c == 0:
                            g.wait_ge(LDf[0].h, ldf_all[0])
                            g.wait_ge(LDf[1].h, ldf_all[1])
                            sm, val = gate
                            if sm is not None:
                                g.wait_ge(sm.h, val)
                            sm2, val2 = mg
                            if sm2 is not None:
                                g.wait_ge(sm2.h, val2)
                        LDxg[c].inc(g.indirect_dma_start(
                            out=sp["xg"][:, c, :],
                            out_offset=None,
                            in_=xfd,
                            in_offset=IndirectOffsetOnAxis(gmin[:, slot, c:c + 1], 0)), 16)
                    ld_gs = [LDxg[i].n for i in range(NCH)]

                @block.sync
                def _(sy, buf=buf, ld_gs=tuple(ld_gs), xfree=tuple(xgt_free[buf]),
                      mg=tuple(mem_gate)):
                    for i in range(NCH):
                        sy.wait_ge(LDxg[i].h, ld_gs[i])
                    sm, val = xfree
                    if sm is not None:
                        sy.wait_ge(sm.h, val)
                    sm2, val2 = mg
                    if sm2 is not None:
                        sy.wait_ge(sm2.h, val2)
                    TP.inc(sy.dma_start(out=sp["xgT"][:, buf, :, :, :],
                                        in_=sp["xg"][:].rearrange("p c d -> p (c d)"),
                                        transpose=True), 16)
                    sy.wait_ge(TP.h, TP.n)
                gather_tp[ex] = TP.n
                xg_free[0] = (TP, TP.n)

            def emit_reload(ex, gate):
                slot = ex % 4

                @block.gpsimd
                def _(g, ex=ex, slot=slot, sc=sc_done[ex], gate=gate):
                    g.wait_ge(sc[0].h, sc[1])
                    if gate[0] is not None:
                        g.wait_ge(gate[0].h, gate[1])
                    LDl[slot].inc(g.dma_start(
                        out=idxf[:, slot, :, :],
                        in_=plist[ex, 0:CAP].rearrange("(c p) w -> p c w", p=P)), 16)
                    LDl[slot].inc(g.dma_start(
                        out=sp["sgrow"][0:1, slot, :],
                        in_=plist[ex, 0:CAP, 1:2].rearrange("c w -> (w c)")[None, :],
                        allow_slow_non_contiguous=True), 16)
                ld_l = LDl[slot].n

                @block.vector
                def _(v, slot=slot, ld_l=ld_l):
                    v.wait_ge(LDl[slot].h, ld_l)
                    VE.inc(v.tensor_copy(idxu[:, slot, :], idxf[:, slot, :, 0]))
                v_iu = VE.n

                @block.vector
                def _(v, slot=slot, v_iu=v_iu):
                    v.wait_ge(VE.h, v_iu)
                    VE.inc(v.tensor_single_scalar(gmin[:, slot, :], idxu[:, slot, :],
                                                  TBM * P - 1, OP.min))
                lists_ready[ex] = (VE.n, slot)

            w1t_r = w1t.rearrange("E (m p) u -> E p m u", p=P)
            w3t_r = w3t.rearrange("E (m p) u -> E p m u", p=P)
            w2t_r = w2t.rearrange("E (mu p) d -> E p mu d", p=P)
            ws1t_r = ws1t.rearrange("(m p) u -> p m u", p=P)
            ws3t_r = ws3t.rearrange("(m p) u -> p m u", p=P)
            ws2t_r = ws2t.rearrange("(mu p) d -> p mu d", p=P)

            w1_free = [(VE, ve_x1), (VE, ve_x1)]
            w3_free = [(VE, ve_x1), (VE, ve_x1)]
            w2_free = [(VE, ve_x1), (VE, ve_x1)]
            w_ready = {}
            w_emitted = set()

            def load_w(ex, kind, idx, gate_override=None):
                if ex >= NEXP or (ex, kind, idx) in w_emitted:
                    return
                w_emitted.add((ex, kind, idx))
                if kind == 1:
                    slot = (idx // 2) % 2
                    srcs = w1t_r[ex] if ex < NE else ws1t_r
                    srcap = srcs[:, :, idx * P:(idx + 1) * P]
                    dst = W1T[:, :, slot, (idx % 2) * P:(idx % 2) * P + P]
                    sem, gate = LDw1[slot], tuple(w1_free[slot])
                elif kind == 3:
                    slot = (idx // 2) % 2
                    srcs = w3t_r[ex] if ex < NE else ws3t_r
                    srcap = srcs[:, :, idx * P:(idx + 1) * P]
                    dst = W3T[:, :, slot, (idx % 2) * P:(idx % 2) * P + P]
                    sem, gate = LDw3[slot], tuple(w3_free[slot])
                else:
                    slot = idx % 2
                    srcs = w2t_r[ex] if ex < NE else ws2t_r
                    srcap = srcs[:, :, idx * 512:(idx + 1) * 512]
                    dst = W2T[:, :, slot, :]
                    sem, gate = LDw2[slot], tuple(w2_free[slot])
                if gate_override is not None:
                    gate = tuple(gate_override)

                @block.gpsimd
                def _(g, srcap=srcap, dst=dst, sem=sem, gate=gate):
                    sm, val = gate
                    if sm is not None:
                        g.wait_ge(sm.h, val)
                    sem.inc(g.dma_start(out=dst, in_=srcap), 16)
                w_ready[(ex, kind, idx)] = (sem, sem.n)

            arena_free_z = [tuple(arena_free[0]), (None, 0)]
            last_sil = [0]
            last_hmul = [0]
            slotfree_r = [(None, 0), (None, 0)]
            h_readers_done = [0]

            ORDER = [NE] + list(range(NE))
            rscat = {}
            sg_free = (None, 0)
            xgt_free = [(None, 0), (None, 0)]
            xg_free = [(None, 0)]
            rg_free = (None, 0)
            mem_gate = (None, 0)
            reload_done = set()
            pending_scatter = [None]
            pend_w2 = []
            ve_acc_done = 0

            for ei, ex in enumerate(ORDER):
                nxt = ORDER[ei + 1] if ei + 1 < len(ORDER) else NEXP
                is_shared = (ex == NE)
                for hh in range(4):
                    load_w(ex, 1, hh)
                    load_w(ex, 3, hh)
                for kg in range(2):
                    load_w(ex, 2, kg)

                if ei == 0:
                    @block.gpsimd
                    def _(g):
                        g.wait_ge(VE.h, ve_x1)
                        GP.inc(g.iota(vals[:], pattern=[[P, TBM]], base=0, channel_multiplier=1,
                                      allow_small_or_imprecise_dtypes=True))
                        GP.inc(g.iota(pref[:, :, 0:1], pattern=[[P, NCH], [0, 1]], base=TBM * P,
                                      channel_multiplier=1, allow_small_or_imprecise_dtypes=True))
                        GP.inc(g.iota(cw_trash[:], pattern=[[1, TBM * P]], base=CAP,
                                      channel_multiplier=0, allow_small_or_imprecise_dtypes=True))
                    gp_sparse = GP.n

                    @block.vector
                    def _(v):
                        v.wait_ge(GP.h, gp_sparse)
                        v.wait_ge(VE.h, ve_x1)
                        VE.inc(v.memset(pref[:, :, 1:2], 0.0))
                        VE.inc(v.memset(ones1[:], 1.0))

                    ve_zero_src = VE.n

                    ldz_zero = LDz.n

                if is_shared:
                    pair_free = [tuple(arena_free_z[0]), tuple(arena_free_z[0])]
                    for step in range(2 * UB):
                        ub, th = step // 2, step % 2
                        q = ub // 2
                        uo = (ub % 2) * P
                        o = (step % 2) * 1024
                        z1 = pAR[0][:, o:o + 512]
                        z3 = pAR[0][:, o + 512:o + 1024]
                        h_sl = hbuf[:, ub, th * 512:(th + 1) * 512]

                        @block.tensor
                        def _(t, ex=ex, q=q, uo=uo, th=th, z1=z1, z3=z3,
                              gate=tuple(pair_free[step % 2])):
                            r1 = w_ready[(ex, 1, 2 * q + uo // P)]
                            r3 = w_ready[(ex, 3, 2 * q + uo // P)]
                            t.wait_ge(r1[0].h, r1[1])
                            t.wait_ge(r3[0].h, r3[1])
                            sm, val = gate
                            if sm is not None:
                                t.wait_ge(sm.h, val)
                            for m in range(DC):
                                l1 = W1T[:, m, q % 2, uo:uo + P]
                                l3 = W3T[:, m, q % 2, uo:uo + P]
                                rhs = flat2(xfT[:, m, th * 4:(th + 1) * 4, :])
                                t.matmul(z1, l1, rhs, start=(m == 0),
                                         stop=(m == DC - 1), skip_group_check=True)
                                ins = t.matmul(z3, l3, rhs, start=(m == 0),
                                               stop=(m == DC - 1), skip_group_check=True)
                            PE.inc(ins)
                        pe_z = PE.n
                        if step == 7:
                            emit_ST()
                            emit_compact()
                        if ub % 2 == 1 and th == 1:
                            w1_free[q % 2] = (PE, pe_z)
                            w3_free[q % 2] = (PE, pe_z)
                            if q + 2 <= 3:
                                for hh in (2 * (q + 2), 2 * (q + 2) + 1):
                                    load_w(ex, 1, hh)
                                    load_w(ex, 3, hh)
                            else:
                                for hh in (2 * (q - 2), 2 * (q - 2) + 1):
                                    load_w(nxt, 1, hh)
                                    load_w(nxt, 3, hh)

                        @block.scalar
                        def _(s, pe_z=pe_z, z1=z1, lh=last_hmul[0]):
                            s.wait_ge(PE.h, pe_z)
                            if lh:
                                s.wait_ge(VE.h, lh)
                            SE.inc(s.activation(out=silb[:], in_=z1, func=AF.Silu))
                        s_sil = SE.n
                        last_sil[0] = s_sil

                        @block.vector
                        def _(v, z3=z3, h_sl=h_sl, s_sil=s_sil):
                            v.wait_ge(SE.h, s_sil)
                            VE.inc(v.tensor_tensor(h_sl, silb[:], z3, OP.mult))
                        last_hmul[0] = VE.n
                        pair_free[step % 2] = (VE, VE.n)
                    arena_free_z[0] = (VE, VE.n)
                    arena_free_z[1] = (VE, st_state["c_tp"])
                    ve_h_all = VE.n
                    pe_shared_z = PE.n

                    for kg in range(4):
                        for tb in range(TBM):
                            par = tb % 2
                            slot = pAR[par][:, 0:512]

                            @block.tensor
                            def _(t, ex=ex, kg=kg, tb=tb, slot=slot,
                                  gate=tuple(slotfree_r[par]),
                                  zgate=tuple(arena_free_z[par]), ve_h_all=ve_h_all):
                                r2 = w_ready[(ex, 2, kg)]
                                t.wait_ge(r2[0].h, r2[1])
                                t.wait_ge(VE.h, ve_h_all)
                                sm, val = gate
                                if sm is not None:
                                    t.wait_ge(sm.h, val)
                                if kg == 0 and tb < 2:
                                    sm2, val2 = zgate
                                    if sm2 is not None:
                                        t.wait_ge(sm2.h, val2)
                                for uc in range(UB):
                                    ins = t.matmul(slot,
                                                   hbuf[:, uc, tb * P:(tb + 1) * P],
                                                   W2T[:, uc, kg % 2, :],
                                                   start=(uc == 0), stop=(uc == UB - 1))
                                PE.inc(ins)
                            pe_r = PE.n
                            if tb == TBM - 1:
                                w2_free[kg % 2] = (PE, pe_r)
                                if kg + 2 <= 3:
                                    load_w(ex, 2, kg + 2)
                                else:
                                    load_w(nxt, 2, kg - 2)

                            @block.vector
                            def _(v, kg=kg, tb=tb, slot=slot, pe_r=pe_r,
                                  aw=accw[(kg, tb)]):
                                v.wait_ge(PE.h, pe_r)
                                v.wait_ge(VE.h, aw)
                                a_sl = acc[:, tb, kg * 512:(kg + 1) * 512]
                                VE.inc(v.tensor_tensor(a_sl, a_sl, slot, OP.add))
                            slotfree_r[par] = (VE, VE.n)
                            accw[(kg, tb)] = VE.n
                    h_readers_done[0] = PE.n
                    arena_free_z = [(VE, VE.n), (VE, VE.n)]
                    ve_acc_done = VE.n

                    mem_gate = (PE, PE.n)
                    alloc_sparse()
                    for _re in range(2):
                        emit_reload(_re, mem_gate)
                        reload_done.add(_re)
                    emit_gather_chain(0)

                    @block.gpsimd
                    def _(g):
                        g.wait_ge(VE.h, ve_acc_done)
                        LDz.inc(g.dma_start(out=out_t[:, 0:TBM, :], in_=acc[:]), 16)
                    ldz_acc = LDz.n
                    rg_free = (LDz, ldz_acc)
                    continue

                slot = ex % 4
                sgslot = ex % 2
                if ex not in reload_done:
                    emit_reload(ex, (GDr, rscat.get(ex - 4, 0)))
                    reload_done.add(ex)
                if ex + 2 < NE and (ex + 2) not in reload_done:
                    emit_reload(ex + 2, mem_gate if ex + 2 < 4 else
                                (GDr, rscat.get(ex - 2, 0)))
                    reload_done.add(ex + 2)
                lr_ve, _sl = lists_ready[ex]
                buf = ex % 2

                if pending_scatter[0] is not None:
                    pending_scatter[0]()
                    pending_scatter[0] = None
                if ex not in gather_done:
                    emit_gather_chain(ex)
                tp_xgt_all = gather_tp[ex]

                ve_sg_box = [0]
                for ub in range(UB):
                    par = ub % 2
                    ps = pAR[par]
                    q = ub // 2
                    uo = (ub % 2) * P
                    z1 = ps[:, 0:CAP]
                    z3 = ps[:, 512:512 + CAP]
                    h_sl = sp["hb"][:, ub, :]

                    @block.tensor
                    def _(t, ex=ex, ub=ub, q=q, uo=uo, ps=ps, z1=z1, z3=z3, buf=buf,
                          gate=tuple(arena_free_z[par]), tp_xgt_all=tp_xgt_all):
                        r1 = w_ready[(ex, 1, ub)]
                        r3 = w_ready[(ex, 3, ub)]
                        t.wait_ge(r1[0].h, r1[1])
                        t.wait_ge(r3[0].h, r3[1])
                        t.wait_ge(TP.h, tp_xgt_all)
                        sm, val = gate
                        if sm is not None:
                            t.wait_ge(sm.h, val)
                        for m in range(DC):
                            l1 = W1T[:, m, q % 2, uo:uo + P]
                            l3 = W3T[:, m, q % 2, uo:uo + P]
                            rhs = sp["xgT"][:, buf, :, m, :]
                            t.matmul(z1, l1, rhs, start=(m == 0),
                                     stop=(m == DC - 1), skip_group_check=True)
                            ins = t.matmul(z3, l3, rhs, start=(m == 0),
                                           stop=(m == DC - 1), skip_group_check=True)
                        PE.inc(ins)
                    pe_z = PE.n
                    if ub == 0:
                        @block.tensor
                        def _(t, ex=ex, slot=slot, gate=tuple(sg_free)):
                            t.wait_ge(LDl[slot].h, LDl[slot].n)
                            sm, val = gate
                            if sm is not None:
                                t.wait_ge(sm.h, val)
                            PE.inc(t.matmul(pAR[0][:, 1536:1536 + CAP], ones1[0:1, :],
                                            sp["sgrow"][0:1, slot, :],
                                            start=True, stop=True))
                        pe_sg = PE.n

                        @block.vector
                        def _(v, sgslot=sgslot, pe_sg=pe_sg):
                            v.wait_ge(PE.h, pe_sg)
                            VE.inc(v.tensor_copy(sp["Sg"][:, sgslot, :],
                                                 pAR[0][:, 1536:1536 + CAP]))
                        sg_free = (VE, VE.n)
                        ve_sg_box[0] = VE.n
                    if ub % 2 == 1:
                        w1_free[q % 2] = (PE, pe_z)
                        w3_free[q % 2] = (PE, pe_z)
                        if q + 2 <= 3:
                            for hh in (2 * (q + 2), 2 * (q + 2) + 1):
                                load_w(ex, 1, hh)
                                load_w(ex, 3, hh)
                        else:
                            for hh in (2 * (q - 2), 2 * (q - 2) + 1):
                                load_w(nxt, 1, hh)
                                load_w(nxt, 3, hh)
                    if ub == UB - 1:
                        xgt_free[buf] = (PE, pe_z)

                    @block.vector
                    def _(v, pe_z=pe_z, z1=z1, sgslot=sgslot, ls=last_sil[0],
                          ve_sg=ve_sg_box[0]):
                        v.wait_ge(PE.h, pe_z)
                        v.wait_ge(VE.h, ve_sg)
                        if ls:
                            v.wait_ge(SE.h, ls)
                        VE.inc(v.tensor_tensor(tmp1[:, 0:CAP], z1, sp["Sg"][:, sgslot, :],
                                               OP.mult))
                    v_t1 = VE.n

                    @block.scalar
                    def _(s, v_t1=v_t1):
                        s.wait_ge(VE.h, v_t1)
                        SE.inc(s.activation(out=silb[:, 0:CAP], in_=tmp1[:, 0:CAP],
                                            func=AF.Silu))
                    s_sil = SE.n
                    last_sil[0] = s_sil

                    @block.vector
                    def _(v, z3=z3, h_sl=h_sl, sgslot=sgslot, s_sil=s_sil,
                          hrd=h_readers_done[0], lh=last_hmul[0]):
                        if lh:
                            v.wait_ge(VE.h, lh)
                        v_t3 = VE.inc(v.tensor_tensor(tmp3[:, 0:CAP], z3,
                                                      sp["Sg"][:, sgslot, :], OP.mult))
                        v.wait_ge(SE.h, s_sil)
                        v.wait_ge(PE.h, hrd)
                        v.wait_ge(VE.h, v_t3)
                        VE.inc(v.tensor_tensor(h_sl, silb[:, 0:CAP], tmp3[:, 0:CAP],
                                               OP.mult))
                    last_hmul[0] = VE.n
                    arena_free_z[par] = (VE, VE.n)
                ve_h_all = VE.n



                for kg in range(4):
                    for tg in range(NCH):
                        par = (kg * NCH + tg) % 2
                        rps = pAR[par][:, 1024:1536]

                        @block.tensor
                        def _(t, ex=ex, kg=kg, tg=tg, rps=rps,
                              gate=tuple(slotfree_r[par]), ve_h_all=ve_h_all):
                            r2 = w_ready[(ex, 2, kg)]
                            t.wait_ge(r2[0].h, r2[1])
                            t.wait_ge(VE.h, ve_h_all)
                            sm, val = gate
                            if sm is not None:
                                t.wait_ge(sm.h, val)
                            for uc in range(UB):
                                ins = t.matmul(rps, sp["hb"][:, uc, tg * P:(tg + 1) * P],
                                               W2T[:, uc, kg % 2, :],
                                               start=(uc == 0), stop=(uc == UB - 1))
                            PE.inc(ins)
                        pe_r = PE.n
                        if tg == NCH - 1:
                            w2_free[kg % 2] = (PE, pe_r)
                            if kg + 2 <= 3:
                                load_w(ex, 2, kg + 2)
                            else:
                                pend_w2.append((nxt, kg - 2, (PE, pe_r)))

                        @block.vector
                        def _(v, kg=kg, tg=tg, rps=rps, pe_r=pe_r,
                              rgate=tuple(rg_free)):
                            v.wait_ge(PE.h, pe_r)
                            if kg == 0:
                                sm, val = rgate
                                if sm is not None:
                                    v.wait_ge(sm.h, val)
                            VE.inc(v.tensor_copy(sp["rgout"][:, tg, kg * 512:(kg + 1) * 512],
                                                 rps))
                        slotfree_r[par] = (VE, VE.n)
                ve_rg_all = VE.n

                def mk_scatter(ex=ex, slot=slot, ve_rg_all=ve_rg_all,
                               prev=rscat.get(ex - 1, 0), rgate=tuple(rg_free)):
                    def emit():
                        @block.gpsimd
                        def _(g):
                            g.wait_ge(VE.h, ve_rg_all)
                            sm, val = rgate
                            if sm is not None:
                                g.wait_ge(sm.h, val)
                            if prev:
                                g.wait_ge(GDr.h, prev)
                            for tg in range(NCH):
                                g.indirect_dma_start(
                                    out=out,
                                    out_offset=IndirectOffsetOnAxis(
                                        idxu[:, slot, tg:tg + 1], 0),
                                    in_=sp["rgout"][:, tg, :],
                                    in_offset=None,
                                    compute_op=OP.add).then_inc(GDr.h, 16)
                    return emit
                pending_scatter[0] = mk_scatter()
                GDr.n += NCH * 16
                rscat[ex] = GDr.n
                rg_free = (GDr, GDr.n)
                for tex, tkg, tgate in pend_w2:
                    load_w(tex, 2, tkg, gate_override=tgate)
                pend_w2 = []
                h_readers_done[0] = PE.n

            if pending_scatter[0] is not None:
                pending_scatter[0]()
                pending_scatter[0] = None
            ve_final = VE.n

            @block.gpsimd
            def _(g):
                g.wait_ge(GDr.h, GDr.n)
                g.wait_ge(LDz.h, LDz.n)
                g.wait_ge(LDo.h, LDo.n)

    return nc


def _host_inputs(L0_attn_x, L0_ffn_gate_w, L0_ffn_w1, L0_ffn_w2, L0_ffn_w3,
                 L0_ffn_ws1, L0_ffn_ws2, L0_ffn_ws3):
    xf = np.ascontiguousarray(np.asarray(L0_attn_x, dtype=np.float32))
    gw = np.ascontiguousarray(np.asarray(L0_ffn_gate_w, dtype=np.float32))
    w1 = np.ascontiguousarray(np.asarray(L0_ffn_w1, dtype=np.float32))
    w2 = np.ascontiguousarray(np.asarray(L0_ffn_w2, dtype=np.float32))
    w3 = np.ascontiguousarray(np.asarray(L0_ffn_w3, dtype=np.float32))
    ws1 = np.ascontiguousarray(np.asarray(L0_ffn_ws1, dtype=np.float32))
    ws2 = np.ascontiguousarray(np.asarray(L0_ffn_ws2, dtype=np.float32))
    ws3 = np.ascontiguousarray(np.asarray(L0_ffn_ws3, dtype=np.float32))
    gwT = np.ascontiguousarray(gw.T)
    w1t = np.ascontiguousarray(w1.transpose(0, 2, 1))
    w2t = np.ascontiguousarray(w2.transpose(0, 2, 1))
    w3t = np.ascontiguousarray(w3.transpose(0, 2, 1))
    ws1t = np.ascontiguousarray(ws1.T)
    ws2t = np.ascontiguousarray(ws2.T)
    ws3t = np.ascontiguousarray(ws3.T)
    in_maps = []
    for c in range(8):
        b, h = c // 2, c % 2
        mine = xf[b, h * 1024:(h + 1) * 1024]
        other = xf[b, (1 - h) * 1024:(2 - h) * 1024]
        xc = np.ascontiguousarray(np.concatenate([mine, other], axis=0))
        in_maps.append(dict(x=xc, gwT=gwT, w1t=w1t, w2t=w2t, w3t=w3t,
                            ws1t=ws1t, ws2t=ws2t, ws3t=ws3t))
    return in_maps


def kernel(L0_attn_x, L0_ffn_gate_w, L0_ffn_w1, L0_ffn_w2, L0_ffn_w3,
           L0_ffn_ws1, L0_ffn_ws2, L0_ffn_ws3):
    in_maps = _host_inputs(L0_attn_x, L0_ffn_gate_w, L0_ffn_w1, L0_ffn_w2,
                           L0_ffn_w3, L0_ffn_ws1, L0_ffn_ws2, L0_ffn_ws3)
    nc = build()
    res = run_bass_kernel_spmd(nc, in_maps, core_ids=list(range(8)))
    out = np.empty((4, 2048, 2048), dtype=np.float32)
    for c in range(8):
        b, h = c // 2, c % 2
        out[b, h * 1024:(h + 1) * 1024] = res.results[c]["out"][0:1024]
    return out


# version 25
# speedup vs baseline: 1.1409x; 1.0759x over previous
import numpy as np
from contextlib import ExitStack

import concourse.bass as bass
import concourse.mybir as mybir
from concourse.bass_utils import run_bass_kernel_spmd
from concourse.alu_op_type import AluOpType as OP
from concourse.bass import IndirectOffsetOnAxis

F32 = mybir.dt.float32
BF16 = mybir.dt.bfloat16
U32 = mybir.dt.uint32
AF = mybir.ActivationFunctionType
AX = mybir.AxisListType

P = 128
D = 2048
SEQ = 2048
TB = 16
TBM = 8
DC = 16
H = 1024
UB = 8
NE = 8
NEXP = 9
EPS = 1e-5
INV_D = 1.0 / D

CAP = 384
NCH = CAP // P
W1WIN = 4
W2WIN = 4


class Sem:
    def __init__(self, h):
        self.h = h
        self.n = 0

    def inc(self, instr, k=1):
        instr.then_inc(self.h, k)
        self.n += k
        return self.n


def flat2(ap):
    return ap.rearrange("p a b -> p (a b)")


def build(want_taps=(), stage="full"):
    nc = bass.Bass("TRN2", target_bir_lowering=False, debug=False)

    x = nc.dram_tensor("x", [SEQ, D], F32, kind="ExternalInput").ap()
    gwT = nc.dram_tensor("gwT", [D, NE], F32, kind="ExternalInput").ap()
    w1t = nc.dram_tensor("w1t", [NE, D, H], F32, kind="ExternalInput").ap()
    w2t = nc.dram_tensor("w2t", [NE, H, D], F32, kind="ExternalInput").ap()
    w3t = nc.dram_tensor("w3t", [NE, D, H], F32, kind="ExternalInput").ap()
    ws1t = nc.dram_tensor("ws1t", [D, H], F32, kind="ExternalInput").ap()
    ws2t = nc.dram_tensor("ws2t", [H, D], F32, kind="ExternalInput").ap()
    ws3t = nc.dram_tensor("ws3t", [D, H], F32, kind="ExternalInput").ap()
    out = nc.dram_tensor("out", [TBM * P + CAP + P, D], F32, kind="ExternalOutput").ap()
    plist = nc.dram_tensor("plist", [NE, CAP + TBM * P, 2], F32, kind="Internal").ap()
    xfd = nc.dram_tensor("xfd", [TBM * P, D], BF16, kind="Internal").ap()

    x_t = x.rearrange("(tb p) d -> p tb d", p=P)
    out_t = out.rearrange("(tb p) d -> p tb d", p=P)

    with ExitStack() as top:
        e = top.enter_context
        acc = e(nc.sbuf_tensor("acc", [P, TBM, D], F32))
        identt = e(nc.sbuf_tensor("identt", [P, P], F32))
        iota8 = e(nc.sbuf_tensor("iota8", [P, NE], F32))
        gT = e(nc.sbuf_tensor("gT", [P, DC, NE], BF16))
        gws = e(nc.sbuf_tensor("gws", [P, DC, NE], F32))
        st_sum = e(nc.sbuf_tensor("st_sum", [P, TB], F32))
        st_ssq = e(nc.sbuf_tensor("st_ssq", [P, TB], F32))
        st_var = e(nc.sbuf_tensor("st_var", [P, TB], F32))
        st_rstd = e(nc.sbuf_tensor("st_rstd", [P, TB], F32))
        st_nmr = e(nc.sbuf_tensor("st_nmr", [P, TB], F32))
        st_tmp = e(nc.sbuf_tensor("st_tmp", [P, TB], F32))
        sqscr = e(nc.sbuf_tensor("sqscr", [P, D], F32))
        sm_mxn = e(nc.sbuf_tensor("sm_mxn", [P, 2], F32))
        sm_rs = e(nc.sbuf_tensor("sm_rs", [P, 2], F32))
        sm_lnr = e(nc.sbuf_tensor("sm_lnr", [P, 2], F32))
        sm_b2 = e(nc.sbuf_tensor("sm_b2", [P, 2], F32))

        pA = e(nc.psum_tensor("pA", [P, 512 * 4], F32))
        pB = e(nc.psum_tensor("pB", [P, 512 * 4], F32))
        pAR = [pA, pB]

        LDa = Sem(e(nc.semaphore("LDa")))
        LDx = [Sem(e(nc.semaphore(f"LDx{i}"))) for i in range(2)]
        LDg = Sem(e(nc.semaphore("LDg")))
        LDw1 = [Sem(e(nc.semaphore(f"LDw1_{i}"))) for i in range(2)]
        LDw3 = [Sem(e(nc.semaphore(f"LDw3_{i}"))) for i in range(2)]
        LDw2 = [Sem(e(nc.semaphore(f"LDw2_{i}"))) for i in range(2)]
        LDo = Sem(e(nc.semaphore("LDo")))
        LDf = [Sem(e(nc.semaphore(f"LDf{i}"))) for i in range(2)]
        GDc = [Sem(e(nc.semaphore(f"GDc{i}"))) for i in range(2)]
        LDl = [Sem(e(nc.semaphore(f"LDl{i}"))) for i in range(4)]
        LDxg = [Sem(e(nc.semaphore(f"LDxg{i}"))) for i in range(NCH)]
        GDr = Sem(e(nc.semaphore("GDr")))
        LDz = Sem(e(nc.semaphore("LDz")))
        GP = Sem(e(nc.semaphore("GP")))
        TP = Sem(e(nc.semaphore("TP")))
        PE = Sem(e(nc.semaphore("PE")))
        VE = Sem(e(nc.semaphore("VE")))
        SE = Sem(e(nc.semaphore("SE")))

        block = e(nc.Block())

        def tap(name, ap, waits=()):
            if name not in want_taps:
                return
            t = nc.dram_tensor("tap_" + name, list(ap.shape), ap.dtype,
                               kind="ExternalOutput").ap()

            @block.gpsimd
            def _(g, ap=ap, t=t, waits=tuple(waits)):
                for s, v in waits:
                    g.wait_ge(s.h, v)
                LDo.inc(g.dma_start(out=t, in_=ap), 16)

        @block.gpsimd
        def _(g):
            GP.inc(g.iota(iota8[:], pattern=[[1, NE]], base=0, channel_multiplier=0,
                          allow_small_or_imprecise_dtypes=True), 1)
            GP.inc(g.iota(identt[:], pattern=[[0, P]], base=0, channel_multiplier=1,
                          allow_small_or_imprecise_dtypes=True), 1)
            GP.inc(g.iota(sqscr[:, 0:P], pattern=[[1, P]], base=0, channel_multiplier=0,
                          allow_small_or_imprecise_dtypes=True), 1)
            LDg.inc(g.dma_start(out=gws[:], in_=gwT.rearrange("(m p) e -> p m e", p=P)), 16)
            LDa.inc(g.dma_start(out=acc[:], in_=x_t[:, 0:TBM, :]), 16)
        gp_consts = GP.n
        ld_gws = (LDg, LDg.n)
        ld_acc = (LDa, LDa.n)

        @block.vector
        def _(v):
            v.wait_ge(GP.h, gp_consts)
            v.wait_ge(ld_gws[0].h, ld_gws[1])
            VE.inc(v.tensor_tensor(identt[:], identt[:], sqscr[:, 0:P], OP.is_equal))
            VE.inc(v.tensor_copy(gT[:], gws[:]))
        ve_consts = VE.n

        st_m2 = e(nc.sbuf_tensor("st_m2", [P, TB], F32))

        def emit_ln_group(items, ld_waits=None, scalar_pre=None, vector_pre=None):
            cols = [c for c, _, _ in items]
            lo, hi = min(cols), max(cols) + 1
            csl = slice(lo, hi)

            @block.vector
            def _(v):
                if vector_pre is not None:
                    v.wait_ge(vector_pre[0].h, vector_pre[1])
                for i, (c, src_ap, _) in enumerate(items):
                    if ld_waits is not None and ld_waits[i] is not None:
                        v.wait_ge(ld_waits[i][0].h, ld_waits[i][1])
                    VE.inc(v.tensor_reduce(st_sum[:, c:c + 1], src_ap, AX.X, OP.add,
                                           negate=True))
            v_red = VE.n

            @block.scalar
            def _(s):
                if scalar_pre is not None:
                    s.wait_ge(scalar_pre[0].h, scalar_pre[1])
                for i, (c, src_ap, _) in enumerate(items):
                    if ld_waits is not None and ld_waits[i] is not None:
                        s.wait_ge(ld_waits[i][0].h, ld_waits[i][1])
                    if i > 0:
                        s.wait_ge(SE.h, SE.n)
                    SE.inc(s.activation(out=sqscr[:], in_=src_ap, func=AF.Square,
                                        accum_out=st_ssq[:, c:c + 1]))
            s_sq = SE.n

            @block.vector
            def _(v):
                v.wait_ge(VE.h, v_red)
                v.wait_ge(SE.h, s_sq)
                VE.inc(v.tensor_scalar_mul(st_sum[:, csl], st_sum[:, csl], INV_D))
                VE.inc(v.tensor_scalar_mul(st_tmp[:, csl], st_ssq[:, csl], INV_D))
            v_d1 = VE.n

            @block.vector
            def _(v):
                v.wait_ge(VE.h, v_d1)
                VE.inc(v.tensor_tensor(st_m2[:, csl], st_sum[:, csl], st_sum[:, csl],
                                       OP.mult))
            v_d2 = VE.n

            @block.vector
            def _(v):
                v.wait_ge(VE.h, v_d2)
                VE.inc(v.tensor_tensor(st_var[:, csl], st_tmp[:, csl], st_m2[:, csl],
                                       OP.subtract))
            v_d3 = VE.n

            @block.vector
            def _(v):
                v.wait_ge(VE.h, v_d3)
                VE.inc(v.tensor_scalar_add(st_var[:, csl], st_var[:, csl], EPS))
            v_d4 = VE.n

            @block.scalar
            def _(s):
                s.wait_ge(VE.h, v_d4)
                SE.inc(s.activation(out=st_tmp[:, csl], in_=st_var[:, csl], func=AF.Sqrt))
            s_std = SE.n

            @block.vector
            def _(v):
                v.wait_ge(SE.h, s_std)
                VE.inc(v.reciprocal(st_rstd[:, csl], st_tmp[:, csl]))
            v_rec = VE.n

            @block.vector
            def _(v):
                v.wait_ge(VE.h, v_rec)
                VE.inc(v.tensor_tensor(st_nmr[:, csl], st_sum[:, csl], st_rstd[:, csl],
                                       OP.mult))
            v_nmr = VE.n

            outs = []
            for c, src_ap, dst_ap in items:
                @block.scalar
                def _(s, c=c, src_ap=src_ap, dst_ap=dst_ap):
                    s.wait_ge(VE.h, v_nmr)
                    SE.inc(s.activation(out=dst_ap, in_=src_ap, func=AF.Identity,
                                        bias=st_nmr[:, c:c + 1],
                                        scale=st_rstd[:, c:c + 1]))
                outs.append(SE.n)
            return outs

        def emit_ln(src_ap, dst_ap, col, ld_wait=None, scalar_pre=None, vector_pre=None):
            return emit_ln_group([(col, src_ap, dst_ap)], ld_waits=[ld_wait],
                                 scalar_pre=scalar_pre, vector_pre=vector_pre)[0]

        arena_free = [(None, 0), (None, 0)]
        with ExitStack() as attn_scope:
            ea = attn_scope.enter_context
            Xn = ea(nc.sbuf_tensor("Xn", [P, TB, D], BF16))
            XnT = ea(nc.sbuf_tensor("XnT", [P, DC, TBM, P], BF16))
            xring = [ea(nc.sbuf_tensor(f"xring{i}", [P, D], F32)) for i in range(2)]
            Asb = [ea(nc.sbuf_tensor("Asb0", [P, D], BF16))]
            ATsl = [ea(nc.sbuf_tensor("ATsl0", [P, DC, 4, P], BF16))]

            s_norm = emit_ln_group(
                [(tb, acc[:, tb, :], Xn[:, tb, :]) for tb in range(TBM)],
                ld_waits=[ld_acc] * TBM,
                scalar_pre=(VE, ve_consts), vector_pre=None)
            ring_free = [(None, 0), (None, 0)]
            for tb in range(TBM, TB):
                slot = tb % 2
                @block.gpsimd
                def _(g, tb=tb, slot=slot, gate=tuple(ring_free[slot])):
                    sm, val = gate
                    if sm is not None:
                        g.wait_ge(sm.h, val)
                    LDx[slot].inc(g.dma_start(out=xring[slot][:], in_=x_t[:, tb, :]), 16)
                sn = emit_ln(xring[slot][:], Xn[:, tb, :], tb,
                             ld_wait=(LDx[slot], LDx[slot].n))
                s_norm.append(sn)
                ring_free[slot] = (SE, sn)
            s_ln1_all = SE.n
            tap("ln1", Xn[:], waits=[(SE, s_ln1_all)])

            if stage == "ln1":
                @block.gpsimd
                def _(g):
                    g.wait_ge(LDo.h, LDo.n)
                return nc

            for tb in range(TBM):
                @block.sync
                def _(sy, tb=tb):
                    sy.wait_ge(SE.h, s_norm[tb])
                    TP.inc(sy.dma_start(out=XnT[:, :, tb, :], in_=Xn[:, tb, :],
                                        transpose=True), 16)
                    sy.wait_ge(TP.h, TP.n)
            tp_xnt = TP.n
            tap("xnt", XnT[:], waits=[(TP, tp_xnt)])

            if stage == "xnt":
                @block.gpsimd
                def _(g):
                    g.wait_ge(LDo.h, LDo.n)
                return nc

            accw = {}
            asb_free = [(None, 0)]
            atsl_free = [(None, 0)]
            tp_at = {}
            for g_r in range(4):
                for ibl in range(4):
                    gi = 4 * g_r + ibl
                    par = gi % 2
                    ps = pAR[par]

                    @block.tensor
                    def _(t, gi=gi, ps=ps, gate=tuple(arena_free[par])):
                        if gi == 2:
                            t.wait_ge(SE.h, s_ln1_all)
                        sm, val = gate
                        if sm is not None:
                            t.wait_ge(sm.h, val)
                        for ct in range(TB):
                            if gi < 2:
                                t.wait_ge(SE.h, s_norm[ct])
                            for jr in range(4):
                                ins = t.matmul(ps[:, jr * 512:(jr + 1) * 512],
                                               Xn[:, ct, gi * P:(gi + 1) * P],
                                               Xn[:, ct, jr * 512:(jr + 1) * 512],
                                               start=(ct == 0), stop=(ct == TB - 1),
                                               skip_group_check=True)
                        PE.inc(ins)
                    pe_gram = PE.n

                    if stage == "gram0" and gi == 0:
                        @block.vector
                        def _(v, ps=ps, pe_gram=pe_gram):
                            v.wait_ge(PE.h, pe_gram)
                            VE.inc(v.tensor_copy(sqscr[:], ps[:]))
                        tap("G0", sqscr[:], waits=[(VE, VE.n)])

                        @block.gpsimd
                        def _(g):
                            g.wait_ge(LDo.h, LDo.n)
                        return nc

                    mxn = sm_mxn[:, par:par + 1]
                    rs = sm_rs[:, par:par + 1]
                    lnr = sm_lnr[:, par:par + 1]
                    b2 = sm_b2[:, par:par + 1]

                    @block.vector
                    def _(v, ps=ps, pe_gram=pe_gram, mxn=mxn):
                        v.wait_ge(PE.h, pe_gram)
                        VE.inc(v.tensor_reduce(mxn, ps[:], AX.X, OP.max, negate=True))
                    v_mxn = VE.n

                    @block.scalar
                    def _(s, gi=gi, ps=ps, v_mxn=v_mxn, mxn=mxn, rs=rs,
                          gate=tuple(asb_free[0])):
                        s.wait_ge(VE.h, v_mxn)
                        sm, val = gate
                        if sm is not None:
                            s.wait_ge(sm.h, val)
                        SE.inc(s.activation(out=Asb[0][:], in_=ps[:], func=AF.Exp,
                                            bias=mxn, accum_out=rs))
                    s_e1 = SE.n

                    @block.scalar
                    def _(s, s_e1=s_e1, rs=rs, lnr=lnr):
                        s.wait_ge(SE.h, s_e1)
                        SE.inc(s.activation(out=lnr, in_=rs, func=AF.Ln))
                    s_ln = SE.n

                    @block.vector
                    def _(v, s_ln=s_ln, mxn=mxn, lnr=lnr, b2=b2):
                        v.wait_ge(SE.h, s_ln)
                        VE.inc(v.tensor_tensor(b2, mxn, lnr, OP.subtract))
                    v_b2 = VE.n

                    @block.scalar
                    def _(s, gi=gi, ps=ps, v_b2=v_b2, b2=b2):
                        s.wait_ge(VE.h, v_b2)
                        SE.inc(s.activation(out=Asb[0][:], in_=ps[:], func=AF.Exp,
                                            bias=b2))
                    s_e2 = (SE, SE.n)
                    arena_free[par] = s_e2

                    if stage == "gram1" and gi == 0:
                        tap("A0", Asb[0][:], waits=[(SE, s_e2)])

                        @block.gpsimd
                        def _(g):
                            g.wait_ge(LDo.h, LDo.n)
                        return nc

                    @block.sync
                    def _(sy, g_r=g_r, ibl=ibl, gi=gi, s_e2=s_e2,
                          gate=tuple(atsl_free[0])):
                        sy.wait_ge(s_e2[0].h, s_e2[1])
                        sm, val = gate
                        if sm is not None and ibl == 0:
                            sy.wait_ge(sm.h, val)
                        TP.inc(sy.dma_start(out=ATsl[0][:, :, ibl, :],
                                            in_=Asb[0][:], transpose=True), 16)
                        sy.wait_ge(TP.h, TP.n)
                    asb_free[0] = (TP, TP.n)

                    if stage == "at1" and gi == 0:
                        tap("AT0", ATsl[0][:], waits=[(TP, TP.n)])

                        @block.gpsimd
                        def _(g):
                            g.wait_ge(LDo.h, LDo.n)
                        return nc
                tp_at[g_r] = TP.n

                for tb in range(TBM):
                    par = tb % 2
                    slot = pAR[par][:, 0:512]

                    @block.tensor
                    def _(t, g_r=g_r, tb=tb, slot=slot, gate=tuple(arena_free[par])):
                        t.wait_ge(TP.h, tp_at[g_r])
                        if g_r == 0 and tb == 0:
                            t.wait_ge(TP.h, tp_xnt)
                        sm, val = gate
                        if sm is not None:
                            t.wait_ge(sm.h, val)
                        for m in range(DC):
                            ins = t.matmul(slot, XnT[:, m, tb, :],
                                           flat2(ATsl[0][:, m, :, :]),
                                           start=(m == 0), stop=(m == DC - 1))
                        PE.inc(ins)
                    pe_out = PE.n

                    @block.vector
                    def _(v, g_r=g_r, tb=tb, slot=slot, pe_out=pe_out):
                        v.wait_ge(PE.h, pe_out)
                        a_sl = acc[:, tb, g_r * 512:(g_r + 1) * 512]
                        VE.inc(v.tensor_tensor(a_sl, a_sl, slot, OP.add))
                    arena_free[par] = (VE, VE.n)
                    accw[(g_r, tb)] = VE.n
                atsl_free[0] = (PE, PE.n)
            ve_x1 = VE.n
            tap("x1", acc[:], waits=[(VE, ve_x1)])


        if stage == "attn":
            @block.gpsimd
            def _(g):
                g.wait_ge(LDo.h, LDo.n)
            return nc
        with ExitStack() as ffn_scope:
            ef = ffn_scope.enter_context
            scores = ef(nc.sbuf_tensor("scores", [P, TBM, NE], F32))
            topv = ef(nc.sbuf_tensor("topv", [P, TBM, NE], F32))
            topi = ef(nc.sbuf_tensor("topi", [P, TBM, NE], U32))
            topf = ef(nc.sbuf_tensor("topf", [P, TBM, NE], F32))
            Ssc = ef(nc.sbuf_tensor("Ssc", [P, TBM, NE], F32))
            STb = ef(nc.sbuf_tensor("STb", [NE, TBM * P], BF16))
            W1T = ef(nc.sbuf_tensor("W1T", [P, DC, 2, 256], BF16))
            W3T = ef(nc.sbuf_tensor("W3T", [P, DC, 2, 256], BF16))
            W2T = ef(nc.sbuf_tensor("W2T", [P, UB, 2, 512], BF16))
            tmp1 = ef(nc.sbuf_tensor("tmp1", [P, 512], F32))
            tmp3 = ef(nc.sbuf_tensor("tmp3", [P, 512], F32))
            silb = ef(nc.sbuf_tensor("silb", [P, 512], F32))
            posu = ef(nc.sbuf_tensor("posu", [P, TBM, NE], U32))
            scpm = ef(nc.sbuf_tensor("scpm", [P, TBM, NE], F32))
            vals = ef(nc.sbuf_tensor("vals", [P, TBM], F32))
            pairb = ef(nc.sbuf_tensor("pairb", [P, 2, TBM, 2], F32))
            pref = ef(nc.sbuf_tensor("pref", [P, NCH, 2], F32))
            ones1 = ef(nc.sbuf_tensor("ones1", [1, P], F32))
            idxf = ef(nc.sbuf_tensor("idxf", [P, 4, NCH, 2], F32))
            idxu = ef(nc.sbuf_tensor("idxu", [P, 4, NCH], U32))
            gmin = ef(nc.sbuf_tensor("gmin", [P, 4, NCH], U32))
            e3t = ef(nc.sbuf_tensor("e3t", [P, TBM, NE], F32))
            sc_done = {}
            lists_ready = {}



            from contextlib import ExitStack as _ES
            xf_scope = _ES()
            cw_mask = xf_scope.enter_context(nc.sbuf_tensor("cw_mask", [NE, TBM * P], F32))
            cw_pos = xf_scope.enter_context(nc.sbuf_tensor("cw_pos", [NE, TBM * P], F32))
            cw_trash = xf_scope.enter_context(nc.sbuf_tensor("cw_trash", [NE, TBM * P], F32))
            cw_ones = xf_scope.enter_context(nc.sbuf_tensor("cw_ones", [NE, TBM * P], F32))
            hbuf = xf_scope.enter_context(nc.sbuf_tensor("hbuf", [P, UB, H], BF16))
            xfT = xf_scope.enter_context(nc.sbuf_tensor("xfT", [P, DC, TBM, P], BF16))
            xfring = xf_scope.enter_context(nc.sbuf_tensor("xfring", [P, 2, D], BF16))
            sp = {}



            def alloc_sparse():
                xf_scope.close()
                sp["sgrow"] = ffn_scope.enter_context(nc.sbuf_tensor("sgrow", [1, 4, CAP], F32))
                sp["Sg"] = ffn_scope.enter_context(nc.sbuf_tensor("Sg", [P, 2, CAP], F32))
                sp["xg"] = ffn_scope.enter_context(nc.sbuf_tensor("xg", [P, NCH, D], BF16))
                sp["xgT"] = ffn_scope.enter_context(nc.sbuf_tensor("xgT", [P, 2, NCH, DC, P], BF16))
                sp["hb"] = ffn_scope.enter_context(nc.sbuf_tensor("hb", [P, UB, CAP], BF16))
                sp["rgout"] = ffn_scope.enter_context(nc.sbuf_tensor("rgout", [P, NCH, D], BF16))

            tp_xf = []
            gate_prev = (None, 0)
            for grp in range(4):
                tbs = range(grp * 2, grp * 2 + 2)
                if grp > 0:
                    @block.scalar
                    def _(s, g0=LDf[0].n, g1=LDf[1].n):
                        s.wait_ge(LDf[0].h, g0)
                        s.wait_ge(LDf[1].h, g1)
                sns = emit_ln_group(
                    [(tb, acc[:, tb, :], xfring[:, tb % 2, :]) for tb in tbs],
                    scalar_pre=(VE, ve_x1) if grp == 0 else gate_prev,
                    vector_pre=(VE, ve_x1) if grp == 0 else None)
                for j, tb in enumerate(tbs):
                    @block.sync
                    def _(sy, tb=tb, sn=sns[j]):
                        sy.wait_ge(SE.h, sn)
                        TP.inc(sy.dma_start(out=xfT[:, :, tb, :],
                                            in_=xfring[:, tb % 2, :],
                                            transpose=True), 16)
                        sy.wait_ge(TP.h, TP.n)
                    tp_xf.append(TP.n)

                    @block.gpsimd
                    def _(g, tb=tb, sn=sns[j]):
                        g.wait_ge(SE.h, sn)
                        LDf[tb % 2].inc(g.dma_start(
                            out=xfd.rearrange("(tb p) d -> p tb d", p=P)[:, tb, :],
                            in_=xfring[:, tb % 2, :]), 16)
                gate_prev = (TP, TP.n)
            tp_xfT = TP.n
            ldf_all = (LDf[0].n, LDf[1].n)

            gate_arena0 = tuple(arena_free[0])
            gate_arena1 = tuple(arena_free[1])
            pe_gate = []
            s_gate_evac = []
            for tb in range(TBM):
                @block.tensor
                def _(t, tb=tb):
                    t.wait_ge(VE.h, ve_consts)
                    t.wait_ge(TP.h, tp_xf[tb])
                    if tb < 2:
                        sm, val = gate_arena1
                        if sm is not None:
                            t.wait_ge(sm.h, val)
                    if tb >= 2:
                        t.wait_ge(SE.h, s_gate_evac[tb - 2])
                    for m in range(DC):
                        ins = t.matmul(pAR[1][:, (tb % 2) * 512:(tb % 2) * 512 + NE],
                                       xfT[:, m, tb, :], gT[:, m, 0:NE],
                                       start=(m == 0), stop=(m == DC - 1))
                    PE.inc(ins)
                pe_gate.append(PE.n)

                @block.scalar
                def _(s, tb=tb):
                    s.wait_ge(PE.h, pe_gate[tb])
                    SE.inc(s.activation(out=scores[:, tb, :],
                                        in_=pAR[1][:, (tb % 2) * 512:(tb % 2) * 512 + NE],
                                        func=AF.Sigmoid))
                s_gate_evac.append(SE.n)
            s_scores = SE.n
            tap("scores", scores[:], waits=[(SE, s_scores)])

            if stage == "scores":
                @block.gpsimd
                def _(g):
                    g.wait_ge(LDo.h, LDo.n)
                return nc

            @block.vector
            def _(v):
                v.wait_ge(SE.h, s_scores)
                for tb in range(TBM):
                    VE.inc(v.max(topv[:, tb, :], scores[:, tb, :]))
            v_max = VE.n

            @block.vector
            def _(v):
                v.wait_ge(VE.h, v_max)
                for tb in range(TBM):
                    VE.inc(v.max_index(topi[:, tb, :], topv[:, tb, :], scores[:, tb, :]))
            v_mi = VE.n

            @block.vector
            def _(v):
                v.wait_ge(VE.h, v_mi)
                v.wait_ge(GP.h, 1)
                for tb in range(TBM):
                    VE.inc(v.tensor_copy(topf[:, tb, :], topi[:, tb, :]))
            v_tf = VE.n

            @block.vector
            def _(v):
                v.wait_ge(VE.h, v_tf)
                for tb in range(TBM):
                    VE.inc(v.tensor_scalar(Ssc[:, tb, :], iota8[:], topf[:, tb, 0:1],
                                           None, OP.is_equal))
                    VE.inc(v.tensor_scalar(e3t[:, tb, :], iota8[:], topf[:, tb, 1:2],
                                           None, OP.is_equal))
            v_eq = VE.n

            @block.vector
            def _(v):
                v.wait_ge(VE.h, v_eq)
                for tb in range(TBM):
                    VE.inc(v.tensor_scalar(Ssc[:, tb, :], Ssc[:, tb, :], topv[:, tb, 0:1],
                                           None, OP.mult))
            v_s1 = VE.n

            @block.vector
            def _(v):
                v.wait_ge(VE.h, v_s1)
                for tb in range(TBM):
                    VE.inc(v.scalar_tensor_tensor(Ssc[:, tb, :], e3t[:, tb, :],
                                                  topv[:, tb, 1:2], Ssc[:, tb, :],
                                                  OP.mult, OP.add))
            v_S = VE.n
            tap("S", Ssc[:], waits=[(VE, v_S)])

            st_state = {}

            def emit_ST():
                ve_st = []
                for tb in range(TBM):
                    @block.tensor
                    def _(t, tb=tb):
                        t.wait_ge(VE.h, v_S)
                        t.wait_ge(VE.h, ve_consts)
                        if tb == 0:
                            sm, val = gate_arena1
                            if sm is not None:
                                t.wait_ge(sm.h, val)
                        else:
                            t.wait_ge(VE.h, ve_st[tb - 1])
                        PE.inc(t.transpose(pAR[1][0:NE, 1024:1024 + P], Ssc[:, tb, :],
                                           identt[:]))
                    pe_st = PE.n

                    @block.vector
                    def _(v, tb=tb, pe_st=pe_st):
                        v.wait_ge(PE.h, pe_st)
                        VE.inc(v.tensor_copy(STb[0:NE, tb * P:(tb + 1) * P],
                                             pAR[1][0:NE, 1024:1024 + P]))
                    ve_st.append(VE.n)
                st_state["v_stb"] = VE.n

            def emit_compact():
                v_stb = st_state["v_stb"]

                @block.vector
                def _(v):
                    v.wait_ge(VE.h, v_stb)
                    v.wait_ge(GP.h, gp_sparse)
                    VE.inc(v.memset(cw_ones[:], 1.0))
                    VE.inc(v.tensor_single_scalar(cw_mask[:], STb[:], 0.0, OP.is_gt))
                c_m = VE.n

                @block.vector
                def _(v):
                    v.wait_ge(VE.h, c_m)
                    VE.inc(v.tensor_tensor_scan(cw_pos[:], cw_ones[:], cw_mask[:],
                                                0.0, OP.mult, OP.add))
                c_p1 = VE.n

                @block.vector
                def _(v):
                    v.wait_ge(VE.h, c_p1)
                    VE.inc(v.tensor_single_scalar(cw_pos[:], cw_pos[:], -1.0, OP.add))
                    VE.inc(v.tensor_copy(cw_ones[:], STb[:]))
                c_p2 = VE.n

                @block.vector
                def _(v):
                    v.wait_ge(VE.h, c_p2)
                    VE.inc(v.tensor_tensor(cw_pos[:], cw_pos[:], cw_trash[:], OP.subtract))
                c_p3 = VE.n

                @block.vector
                def _(v):
                    v.wait_ge(VE.h, c_p3)
                    VE.inc(v.tensor_tensor(cw_pos[:], cw_mask[:], cw_pos[:], OP.mult))
                c_p4 = VE.n

                @block.vector
                def _(v):
                    v.wait_ge(VE.h, c_p4)
                    VE.inc(v.tensor_tensor(cw_pos[:], cw_pos[:], cw_trash[:], OP.add))
                c_pos = VE.n

                prev_ev = [0]
                for tb in range(TBM):
                    @block.tensor
                    def _(t, tb=tb):
                        t.wait_ge(VE.h, c_pos)
                        if tb > 0:
                            t.wait_ge(VE.h, prev_ev[0])
                        t.transpose(pAR[1][:, 1024:1024 + NE],
                                    cw_pos[0:NE, tb * P:(tb + 1) * P], identt[0:NE, 0:NE])
                        PE.inc(t.transpose(pAR[1][:, 1536:1536 + NE],
                                           cw_ones[0:NE, tb * P:(tb + 1) * P],
                                           identt[0:NE, 0:NE]))
                    pe_t = PE.n

                    @block.vector
                    def _(v, tb=tb, pe_t=pe_t):
                        v.wait_ge(PE.h, pe_t)
                        VE.inc(v.tensor_copy(posu[:, tb, :], pAR[1][:, 1024:1024 + NE]))
                        VE.inc(v.tensor_copy(scpm[:, tb, :], pAR[1][:, 1536:1536 + NE]))
                    prev_ev[0] = VE.n
                c_tp = VE.n
                st_state["c_tp"] = c_tp

                for ex in range(NE):
                    slot = ex % 2

                    @block.gpsimd
                    def _(g, ex=ex):
                        g.wait_ge(GP.h, gp_sparse)
                        LDz.inc(g.dma_start(
                            out=plist[ex, 0:CAP].rearrange("(c p) w -> p c w", p=P),
                            in_=pref[:]), 16)
                    ld_pf = LDz.n

                    @block.vector
                    def _(v, ex=ex, slot=slot, gate=GDc[slot].n):
                        v.wait_ge(VE.h, c_tp)
                        if gate:
                            v.wait_ge(GDc[slot].h, gate)
                        VE.inc(v.tensor_copy(
                            pairb[:, slot, :, 0:1],
                            vals[:].rearrange("p (c one) -> p c one", one=1)))
                        VE.inc(v.tensor_copy(pairb[:, slot, :, 1:2],
                                             scpm[:, :, ex:ex + 1]))
                    v_pair = VE.n

                    @block.gpsimd
                    def _(g, ex=ex, slot=slot, v_pair=v_pair, ld_pf=ld_pf):
                        g.wait_ge(VE.h, v_pair)
                        g.wait_ge(LDz.h, ld_pf)
                        for tb in range(TBM):
                            GDc[slot].inc(g.indirect_dma_start(
                                out=plist.rearrange("E c w -> (E c) w"),
                                out_offset=IndirectOffsetOnAxis(posu[:, tb, ex:ex + 1], 0),
                                in_=pairb[:, slot, tb, :],
                                in_offset=None,
                                element_offset=ex * (CAP + TBM * P) * 2), 16)
                    sc_done[ex] = (GDc[slot], GDc[slot].n)

            gather_done = set()
            gather_tp = {}

            def emit_gather_chain(ex):
                gather_done.add(ex)
                slot = ex % 4
                buf = ex % 2
                lr_ve, _sl = lists_ready[ex]
                for c in range(NCH):
                    @block.gpsimd
                    def _(g, ex=ex, c=c, slot=slot, lr_ve=lr_ve,
                          gate=tuple(xg_free[0]), mg=tuple(mem_gate)):
                        g.wait_ge(VE.h, lr_ve)
                        if c == 0:
                            g.wait_ge(LDf[0].h, ldf_all[0])
                            g.wait_ge(LDf[1].h, ldf_all[1])
                            sm, val = gate
                            if sm is not None:
                                g.wait_ge(sm.h, val)
                            sm2, val2 = mg
                            if sm2 is not None:
                                g.wait_ge(sm2.h, val2)
                        LDxg[c].inc(g.indirect_dma_start(
                            out=sp["xg"][:, c, :],
                            out_offset=None,
                            in_=xfd,
                            in_offset=IndirectOffsetOnAxis(gmin[:, slot, c:c + 1], 0)), 16)
                    ld_gs = [LDxg[i].n for i in range(NCH)]

                @block.sync
                def _(sy, buf=buf, ld_gs=tuple(ld_gs), xfree=tuple(xgt_free[buf]),
                      mg=tuple(mem_gate)):
                    for i in range(NCH):
                        sy.wait_ge(LDxg[i].h, ld_gs[i])
                    sm, val = xfree
                    if sm is not None:
                        sy.wait_ge(sm.h, val)
                    sm2, val2 = mg
                    if sm2 is not None:
                        sy.wait_ge(sm2.h, val2)
                    TP.inc(sy.dma_start(out=sp["xgT"][:, buf, :, :, :],
                                        in_=sp["xg"][:].rearrange("p c d -> p (c d)"),
                                        transpose=True), 16)
                    sy.wait_ge(TP.h, TP.n)
                gather_tp[ex] = TP.n
                xg_free[0] = (TP, TP.n)

            def emit_reload(ex, gate):
                slot = ex % 4

                @block.gpsimd
                def _(g, ex=ex, slot=slot, sc=sc_done[ex], gate=gate):
                    g.wait_ge(sc[0].h, sc[1])
                    if gate[0] is not None:
                        g.wait_ge(gate[0].h, gate[1])
                    LDl[slot].inc(g.dma_start(
                        out=idxf[:, slot, :, :],
                        in_=plist[ex, 0:CAP].rearrange("(c p) w -> p c w", p=P)), 16)
                    LDl[slot].inc(g.dma_start(
                        out=sp["sgrow"][0:1, slot, :],
                        in_=plist[ex, 0:CAP, 1:2].rearrange("c w -> (w c)")[None, :],
                        allow_slow_non_contiguous=True), 16)
                ld_l = LDl[slot].n

                @block.vector
                def _(v, slot=slot, ld_l=ld_l):
                    v.wait_ge(LDl[slot].h, ld_l)
                    VE.inc(v.tensor_copy(idxu[:, slot, :], idxf[:, slot, :, 0]))
                v_iu = VE.n

                @block.vector
                def _(v, slot=slot, v_iu=v_iu):
                    v.wait_ge(VE.h, v_iu)
                    VE.inc(v.tensor_single_scalar(gmin[:, slot, :], idxu[:, slot, :],
                                                  TBM * P - 1, OP.min))
                lists_ready[ex] = (VE.n, slot)

            w1t_r = w1t.rearrange("E (m p) u -> E p m u", p=P)
            w3t_r = w3t.rearrange("E (m p) u -> E p m u", p=P)
            w2t_r = w2t.rearrange("E (mu p) d -> E p mu d", p=P)
            ws1t_r = ws1t.rearrange("(m p) u -> p m u", p=P)
            ws3t_r = ws3t.rearrange("(m p) u -> p m u", p=P)
            ws2t_r = ws2t.rearrange("(mu p) d -> p mu d", p=P)

            w1_free = [(VE, ve_x1), (VE, ve_x1)]
            w3_free = [(VE, ve_x1), (VE, ve_x1)]
            w2_free = [(VE, ve_x1), (VE, ve_x1)]
            w_ready = {}
            w_emitted = set()

            def load_w(ex, kind, idx, gate_override=None):
                if ex >= NEXP or (ex, kind, idx) in w_emitted:
                    return
                w_emitted.add((ex, kind, idx))
                if kind == 1:
                    slot = (idx // 2) % 2
                    srcs = w1t_r[ex] if ex < NE else ws1t_r
                    srcap = srcs[:, :, idx * P:(idx + 1) * P]
                    dst = W1T[:, :, slot, (idx % 2) * P:(idx % 2) * P + P]
                    sem, gate = LDw1[slot], tuple(w1_free[slot])
                elif kind == 3:
                    slot = (idx // 2) % 2
                    srcs = w3t_r[ex] if ex < NE else ws3t_r
                    srcap = srcs[:, :, idx * P:(idx + 1) * P]
                    dst = W3T[:, :, slot, (idx % 2) * P:(idx % 2) * P + P]
                    sem, gate = LDw3[slot], tuple(w3_free[slot])
                else:
                    slot = idx % 2
                    srcs = w2t_r[ex] if ex < NE else ws2t_r
                    srcap = srcs[:, :, idx * 512:(idx + 1) * 512]
                    dst = W2T[:, :, slot, :]
                    sem, gate = LDw2[slot], tuple(w2_free[slot])
                if gate_override is not None:
                    gate = tuple(gate_override)

                @block.gpsimd
                def _(g, srcap=srcap, dst=dst, sem=sem, gate=gate):
                    sm, val = gate
                    if sm is not None:
                        g.wait_ge(sm.h, val)
                    sem.inc(g.dma_start(out=dst, in_=srcap), 16)
                w_ready[(ex, kind, idx)] = (sem, sem.n)

            arena_free_z = [tuple(arena_free[0]), (None, 0)]
            last_sil = [0]
            last_hmul = [0]
            slotfree_r = [(None, 0), (None, 0)]
            h_readers_done = [0]

            ORDER = [NE] + list(range(NE))
            rscat = {}
            sg_free = (None, 0)
            xgt_free = [(None, 0), (None, 0)]
            xg_free = [(None, 0)]
            rg_free = (None, 0)
            mem_gate = (None, 0)
            reload_done = set()
            pending_scatter = [None]
            pend_w2 = []
            ve_acc_done = 0

            for ei, ex in enumerate(ORDER):
                nxt = ORDER[ei + 1] if ei + 1 < len(ORDER) else NEXP
                is_shared = (ex == NE)
                for hh in range(4):
                    load_w(ex, 1, hh)
                    load_w(ex, 3, hh)
                for kg in range(2):
                    load_w(ex, 2, kg)

                if ei == 0:
                    @block.gpsimd
                    def _(g):
                        g.wait_ge(VE.h, ve_x1)
                        GP.inc(g.iota(vals[:], pattern=[[P, TBM]], base=0, channel_multiplier=1,
                                      allow_small_or_imprecise_dtypes=True))
                        GP.inc(g.iota(pref[:, :, 0:1], pattern=[[P, NCH], [0, 1]], base=TBM * P,
                                      channel_multiplier=1, allow_small_or_imprecise_dtypes=True))
                        GP.inc(g.iota(cw_trash[:], pattern=[[1, TBM * P]], base=CAP,
                                      channel_multiplier=0, allow_small_or_imprecise_dtypes=True))
                    gp_sparse = GP.n

                    @block.vector
                    def _(v):
                        v.wait_ge(GP.h, gp_sparse)
                        v.wait_ge(VE.h, ve_x1)
                        VE.inc(v.memset(pref[:, :, 1:2], 0.0))
                        VE.inc(v.memset(ones1[:], 1.0))

                    ve_zero_src = VE.n

                    ldz_zero = LDz.n

                if is_shared:
                    pair_free = [tuple(arena_free_z[0]), tuple(arena_free_z[0])]
                    for step in range(2 * UB):
                        ub, th = step // 2, step % 2
                        q = ub // 2
                        uo = (ub % 2) * P
                        o = (step % 2) * 1024
                        z1 = pAR[0][:, o:o + 512]
                        z3 = pAR[0][:, o + 512:o + 1024]
                        h_sl = hbuf[:, ub, th * 512:(th + 1) * 512]

                        @block.tensor
                        def _(t, ex=ex, q=q, uo=uo, th=th, z1=z1, z3=z3,
                              gate=tuple(pair_free[step % 2])):
                            r1 = w_ready[(ex, 1, 2 * q + uo // P)]
                            r3 = w_ready[(ex, 3, 2 * q + uo // P)]
                            t.wait_ge(r1[0].h, r1[1])
                            t.wait_ge(r3[0].h, r3[1])
                            sm, val = gate
                            if sm is not None:
                                t.wait_ge(sm.h, val)
                            for m in range(DC):
                                l1 = W1T[:, m, q % 2, uo:uo + P]
                                l3 = W3T[:, m, q % 2, uo:uo + P]
                                rhs = flat2(xfT[:, m, th * 4:(th + 1) * 4, :])
                                t.matmul(z1, l1, rhs, start=(m == 0),
                                         stop=(m == DC - 1), skip_group_check=True)
                                ins = t.matmul(z3, l3, rhs, start=(m == 0),
                                               stop=(m == DC - 1), skip_group_check=True)
                            PE.inc(ins)
                        pe_z = PE.n
                        if step == 7:
                            emit_ST()
                            emit_compact()
                        if ub % 2 == 1 and th == 1:
                            w1_free[q % 2] = (PE, pe_z)
                            w3_free[q % 2] = (PE, pe_z)
                            if q + 2 <= 3:
                                for hh in (2 * (q + 2), 2 * (q + 2) + 1):
                                    load_w(ex, 1, hh)
                                    load_w(ex, 3, hh)
                            else:
                                for hh in (2 * (q - 2), 2 * (q - 2) + 1):
                                    load_w(nxt, 1, hh)
                                    load_w(nxt, 3, hh)

                        @block.scalar
                        def _(s, pe_z=pe_z, z1=z1, lh=last_hmul[0]):
                            s.wait_ge(PE.h, pe_z)
                            if lh:
                                s.wait_ge(VE.h, lh)
                            SE.inc(s.activation(out=silb[:], in_=z1, func=AF.Silu))
                        s_sil = SE.n
                        last_sil[0] = s_sil

                        @block.vector
                        def _(v, z3=z3, h_sl=h_sl, s_sil=s_sil):
                            v.wait_ge(SE.h, s_sil)
                            VE.inc(v.tensor_tensor(h_sl, silb[:], z3, OP.mult))
                        last_hmul[0] = VE.n
                        pair_free[step % 2] = (VE, VE.n)
                    arena_free_z[0] = (VE, VE.n)
                    arena_free_z[1] = (VE, st_state["c_tp"])
                    ve_h_all = VE.n
                    pe_shared_z = PE.n

                    for kg in range(4):
                        for tb in range(TBM):
                            par = tb % 2
                            slot = pAR[par][:, 0:512]

                            @block.tensor
                            def _(t, ex=ex, kg=kg, tb=tb, slot=slot,
                                  gate=tuple(slotfree_r[par]),
                                  zgate=tuple(arena_free_z[par]), ve_h_all=ve_h_all):
                                r2 = w_ready[(ex, 2, kg)]
                                t.wait_ge(r2[0].h, r2[1])
                                t.wait_ge(VE.h, ve_h_all)
                                sm, val = gate
                                if sm is not None:
                                    t.wait_ge(sm.h, val)
                                if kg == 0 and tb < 2:
                                    sm2, val2 = zgate
                                    if sm2 is not None:
                                        t.wait_ge(sm2.h, val2)
                                for uc in range(UB):
                                    ins = t.matmul(slot,
                                                   hbuf[:, uc, tb * P:(tb + 1) * P],
                                                   W2T[:, uc, kg % 2, :],
                                                   start=(uc == 0), stop=(uc == UB - 1))
                                PE.inc(ins)
                            pe_r = PE.n
                            if tb == TBM - 1:
                                w2_free[kg % 2] = (PE, pe_r)
                                if kg + 2 <= 3:
                                    load_w(ex, 2, kg + 2)
                                else:
                                    load_w(nxt, 2, kg - 2)

                            @block.vector
                            def _(v, kg=kg, tb=tb, slot=slot, pe_r=pe_r,
                                  aw=accw[(kg, tb)]):
                                v.wait_ge(PE.h, pe_r)
                                v.wait_ge(VE.h, aw)
                                a_sl = acc[:, tb, kg * 512:(kg + 1) * 512]
                                VE.inc(v.tensor_tensor(a_sl, a_sl, slot, OP.add))
                            slotfree_r[par] = (VE, VE.n)
                            accw[(kg, tb)] = VE.n
                    h_readers_done[0] = PE.n
                    arena_free_z = [(VE, VE.n), (VE, VE.n)]
                    ve_acc_done = VE.n

                    mem_gate = (PE, PE.n)
                    alloc_sparse()
                    for _re in range(2):
                        emit_reload(_re, mem_gate)
                        reload_done.add(_re)
                    emit_gather_chain(0)

                    @block.gpsimd
                    def _(g):
                        g.wait_ge(VE.h, ve_acc_done)
                        LDz.inc(g.dma_start(out=out_t[:, 0:TBM, :], in_=acc[:]), 16)
                    ldz_acc = LDz.n
                    rg_free = (LDz, ldz_acc)
                    continue

                slot = ex % 4
                sgslot = ex % 2
                if ex not in reload_done:
                    emit_reload(ex, (GDr, rscat.get(ex - 4, 0)))
                    reload_done.add(ex)
                if ex + 2 < NE and (ex + 2) not in reload_done:
                    emit_reload(ex + 2, mem_gate if ex + 2 < 4 else
                                (GDr, rscat.get(ex - 2, 0)))
                    reload_done.add(ex + 2)
                lr_ve, _sl = lists_ready[ex]
                buf = ex % 2

                if pending_scatter[0] is not None:
                    pending_scatter[0]()
                    pending_scatter[0] = None
                if ex not in gather_done:
                    emit_gather_chain(ex)
                tp_xgt_all = gather_tp[ex]

                ve_sg_box = [0]
                for ub in range(UB):
                    par = ub % 2
                    ps = pAR[par]
                    q = ub // 2
                    uo = (ub % 2) * P
                    z1 = ps[:, 0:CAP]
                    z3 = ps[:, 512:512 + CAP]
                    h_sl = sp["hb"][:, ub, :]

                    @block.tensor
                    def _(t, ex=ex, ub=ub, q=q, uo=uo, ps=ps, z1=z1, z3=z3, buf=buf,
                          gate=tuple(arena_free_z[par]), tp_xgt_all=tp_xgt_all):
                        r1 = w_ready[(ex, 1, ub)]
                        r3 = w_ready[(ex, 3, ub)]
                        t.wait_ge(r1[0].h, r1[1])
                        t.wait_ge(r3[0].h, r3[1])
                        t.wait_ge(TP.h, tp_xgt_all)
                        sm, val = gate
                        if sm is not None:
                            t.wait_ge(sm.h, val)
                        for m in range(DC):
                            l1 = W1T[:, m, q % 2, uo:uo + P]
                            l3 = W3T[:, m, q % 2, uo:uo + P]
                            rhs = sp["xgT"][:, buf, :, m, :]
                            t.matmul(z1, l1, rhs, start=(m == 0),
                                     stop=(m == DC - 1), skip_group_check=True)
                            ins = t.matmul(z3, l3, rhs, start=(m == 0),
                                           stop=(m == DC - 1), skip_group_check=True)
                        PE.inc(ins)
                    pe_z = PE.n
                    if ub == 0:
                        @block.tensor
                        def _(t, ex=ex, slot=slot, gate=tuple(sg_free)):
                            t.wait_ge(LDl[slot].h, LDl[slot].n)
                            sm, val = gate
                            if sm is not None:
                                t.wait_ge(sm.h, val)
                            PE.inc(t.matmul(pAR[0][:, 1536:1536 + CAP], ones1[0:1, :],
                                            sp["sgrow"][0:1, slot, :],
                                            start=True, stop=True))
                        pe_sg = PE.n

                        @block.vector
                        def _(v, sgslot=sgslot, pe_sg=pe_sg):
                            v.wait_ge(PE.h, pe_sg)
                            VE.inc(v.tensor_copy(sp["Sg"][:, sgslot, :],
                                                 pAR[0][:, 1536:1536 + CAP]))
                        sg_free = (VE, VE.n)
                        ve_sg_box[0] = VE.n
                    if ub % 2 == 1:
                        w1_free[q % 2] = (PE, pe_z)
                        w3_free[q % 2] = (PE, pe_z)
                        if q + 2 <= 3:
                            for hh in (2 * (q + 2), 2 * (q + 2) + 1):
                                load_w(ex, 1, hh)
                                load_w(ex, 3, hh)
                        else:
                            for hh in (2 * (q - 2), 2 * (q - 2) + 1):
                                load_w(nxt, 1, hh)
                                load_w(nxt, 3, hh)
                    if ub == UB - 1:
                        xgt_free[buf] = (PE, pe_z)

                    @block.vector
                    def _(v, pe_z=pe_z, z1=z1, sgslot=sgslot, ls=last_sil[0],
                          ve_sg=ve_sg_box[0]):
                        v.wait_ge(PE.h, pe_z)
                        v.wait_ge(VE.h, ve_sg)
                        if ls:
                            v.wait_ge(SE.h, ls)
                        VE.inc(v.tensor_tensor(tmp1[:, 0:CAP], z1, sp["Sg"][:, sgslot, :],
                                               OP.mult))
                    v_t1 = VE.n

                    @block.scalar
                    def _(s, v_t1=v_t1):
                        s.wait_ge(VE.h, v_t1)
                        SE.inc(s.activation(out=silb[:, 0:CAP], in_=tmp1[:, 0:CAP],
                                            func=AF.Silu))
                    s_sil = SE.n
                    last_sil[0] = s_sil

                    @block.vector
                    def _(v, z3=z3, h_sl=h_sl, sgslot=sgslot, s_sil=s_sil,
                          hrd=h_readers_done[0], lh=last_hmul[0]):
                        if lh:
                            v.wait_ge(VE.h, lh)
                        v_t3 = VE.inc(v.tensor_tensor(tmp3[:, 0:CAP], z3,
                                                      sp["Sg"][:, sgslot, :], OP.mult))
                        v.wait_ge(SE.h, s_sil)
                        v.wait_ge(PE.h, hrd)
                        v.wait_ge(VE.h, v_t3)
                        VE.inc(v.tensor_tensor(h_sl, silb[:, 0:CAP], tmp3[:, 0:CAP],
                                               OP.mult))
                    last_hmul[0] = VE.n
                    arena_free_z[par] = (VE, VE.n)
                ve_h_all = VE.n

                if nxt < NE:
                    if nxt not in reload_done:
                        emit_reload(nxt, (GDr, rscat.get(nxt - 4, 0)))
                        reload_done.add(nxt)
                    if nxt not in gather_done:
                        emit_gather_chain(nxt)



                for kg in range(4):
                    for tg in range(NCH):
                        par = (kg * NCH + tg) % 2
                        rps = pAR[par][:, 1024:1536]

                        @block.tensor
                        def _(t, ex=ex, kg=kg, tg=tg, rps=rps,
                              gate=tuple(slotfree_r[par]), ve_h_all=ve_h_all):
                            r2 = w_ready[(ex, 2, kg)]
                            t.wait_ge(r2[0].h, r2[1])
                            t.wait_ge(VE.h, ve_h_all)
                            sm, val = gate
                            if sm is not None:
                                t.wait_ge(sm.h, val)
                            for uc in range(UB):
                                ins = t.matmul(rps, sp["hb"][:, uc, tg * P:(tg + 1) * P],
                                               W2T[:, uc, kg % 2, :],
                                               start=(uc == 0), stop=(uc == UB - 1))
                            PE.inc(ins)
                        pe_r = PE.n
                        if tg == NCH - 1:
                            w2_free[kg % 2] = (PE, pe_r)
                            if kg + 2 <= 3:
                                load_w(ex, 2, kg + 2)
                            else:
                                pend_w2.append((nxt, kg - 2, (PE, pe_r)))

                        @block.vector
                        def _(v, kg=kg, tg=tg, rps=rps, pe_r=pe_r,
                              rgate=tuple(rg_free)):
                            v.wait_ge(PE.h, pe_r)
                            if kg == 0:
                                sm, val = rgate
                                if sm is not None:
                                    v.wait_ge(sm.h, val)
                            VE.inc(v.tensor_copy(sp["rgout"][:, tg, kg * 512:(kg + 1) * 512],
                                                 rps))
                        slotfree_r[par] = (VE, VE.n)
                ve_rg_all = VE.n

                def mk_scatter(ex=ex, slot=slot, ve_rg_all=ve_rg_all,
                               prev=rscat.get(ex - 1, 0), rgate=tuple(rg_free)):
                    def emit():
                        @block.gpsimd
                        def _(g):
                            g.wait_ge(VE.h, ve_rg_all)
                            sm, val = rgate
                            if sm is not None:
                                g.wait_ge(sm.h, val)
                            if prev:
                                g.wait_ge(GDr.h, prev)
                            for tg in range(NCH):
                                g.indirect_dma_start(
                                    out=out,
                                    out_offset=IndirectOffsetOnAxis(
                                        idxu[:, slot, tg:tg + 1], 0),
                                    in_=sp["rgout"][:, tg, :],
                                    in_offset=None,
                                    compute_op=OP.add).then_inc(GDr.h, 16)
                    return emit
                pending_scatter[0] = mk_scatter()
                GDr.n += NCH * 16
                rscat[ex] = GDr.n
                rg_free = (GDr, GDr.n)
                for tex, tkg, tgate in pend_w2:
                    load_w(tex, 2, tkg, gate_override=tgate)
                pend_w2 = []
                h_readers_done[0] = PE.n

            if pending_scatter[0] is not None:
                pending_scatter[0]()
                pending_scatter[0] = None
            ve_final = VE.n

            @block.gpsimd
            def _(g):
                g.wait_ge(GDr.h, GDr.n)
                g.wait_ge(LDz.h, LDz.n)
                g.wait_ge(LDo.h, LDo.n)

    return nc


def _host_inputs(L0_attn_x, L0_ffn_gate_w, L0_ffn_w1, L0_ffn_w2, L0_ffn_w3,
                 L0_ffn_ws1, L0_ffn_ws2, L0_ffn_ws3):
    xf = np.ascontiguousarray(np.asarray(L0_attn_x, dtype=np.float32))
    gw = np.ascontiguousarray(np.asarray(L0_ffn_gate_w, dtype=np.float32))
    w1 = np.ascontiguousarray(np.asarray(L0_ffn_w1, dtype=np.float32))
    w2 = np.ascontiguousarray(np.asarray(L0_ffn_w2, dtype=np.float32))
    w3 = np.ascontiguousarray(np.asarray(L0_ffn_w3, dtype=np.float32))
    ws1 = np.ascontiguousarray(np.asarray(L0_ffn_ws1, dtype=np.float32))
    ws2 = np.ascontiguousarray(np.asarray(L0_ffn_ws2, dtype=np.float32))
    ws3 = np.ascontiguousarray(np.asarray(L0_ffn_ws3, dtype=np.float32))
    gwT = np.ascontiguousarray(gw.T)
    w1t = np.ascontiguousarray(w1.transpose(0, 2, 1))
    w2t = np.ascontiguousarray(w2.transpose(0, 2, 1))
    w3t = np.ascontiguousarray(w3.transpose(0, 2, 1))
    ws1t = np.ascontiguousarray(ws1.T)
    ws2t = np.ascontiguousarray(ws2.T)
    ws3t = np.ascontiguousarray(ws3.T)
    in_maps = []
    for c in range(8):
        b, h = c // 2, c % 2
        mine = xf[b, h * 1024:(h + 1) * 1024]
        other = xf[b, (1 - h) * 1024:(2 - h) * 1024]
        xc = np.ascontiguousarray(np.concatenate([mine, other], axis=0))
        in_maps.append(dict(x=xc, gwT=gwT, w1t=w1t, w2t=w2t, w3t=w3t,
                            ws1t=ws1t, ws2t=ws2t, ws3t=ws3t))
    return in_maps


def kernel(L0_attn_x, L0_ffn_gate_w, L0_ffn_w1, L0_ffn_w2, L0_ffn_w3,
           L0_ffn_ws1, L0_ffn_ws2, L0_ffn_ws3):
    in_maps = _host_inputs(L0_attn_x, L0_ffn_gate_w, L0_ffn_w1, L0_ffn_w2,
                           L0_ffn_w3, L0_ffn_ws1, L0_ffn_ws2, L0_ffn_ws3)
    nc = build()
    res = run_bass_kernel_spmd(nc, in_maps, core_ids=list(range(8)))
    out = np.empty((4, 2048, 2048), dtype=np.float32)
    for c in range(8):
        b, h = c // 2, c % 2
        out[b, h * 1024:(h + 1) * 1024] = res.results[c]["out"][0:1024]
    return out
